# version 23
# baseline (speedup 1.0000x reference)
import numpy as np
import ml_dtypes
from contextlib import ExitStack

import concourse.bass as bass
import concourse.mybir as mybir
from concourse.bass_utils import run_bass_kernel_spmd

F32 = mybir.dt.float32
F16 = mybir.dt.float16
BF16 = mybir.dt.bfloat16
AF = mybir.ActivationFunctionType
ALU = mybir.AluOpType
AX = mybir.AxisListType

N_CORES = 8
NTOK = 2048
NC = 8
PHASE_MARKS = []


class Bld:

    def __init__(self, nc, es):
        self.nc = nc
        self.es = es
        self.blk = None
        self.sems = {}
        self.cnt = {}
        self.dma_names = set()
        for name in ("mm", "dv", "ac", "gp"):
            self._mk(name)

    def _mk(self, name):
        self.sems[name] = self.es.enter_context(self.nc.semaphore(name))
        self.cnt[name] = 0

    def block(self):
        self.end()
        self.blk = self.nc.Block().__enter__()

    def end(self):
        if self.blk is not None:
            self.blk.__exit__(None, None, None)
            self.blk = None

    def t(self, fn): self.blk.tensor(fn)
    def v(self, fn): self.blk.vector(fn)
    def s(self, fn): self.blk.scalar(fn)
    def g(self, fn): self.blk.gpsimd(fn)
    def y(self, fn): self.blk.sync(fn)

    def inc(self, inst, sem, n=1):
        if sem not in self.sems:
            self._mk(sem)
        if n >= 16:
            self.dma_names.add(sem)
        inst.then_inc(self.sems[sem], n)
        self.cnt[sem] += n
        return (sem, self.cnt[sem])

    def tok(self, sem):
        if sem not in self.sems:
            self._mk(sem)
        return (sem, self.cnt[sem])

    def wait(self, eng, tok):
        if tok is not None:
            eng.wait_ge(self.sems[tok[0]], tok[1])

    def wait_all_dma(self, eng):
        for s in sorted(self.dma_names):
            if self.cnt[s]:
                eng.wait_ge(self.sems[s], self.cnt[s])


_NAME_CTR = [0]


def _sb(es, nc, name, shape, dt=BF16):
    _NAME_CTR[0] += 1
    return es.enter_context(nc.sbuf_tensor(f"{name}__{_NAME_CTR[0]}", shape, dt))


def _ps(es, nc, name, shape, dt=F32):
    _NAME_CTR[0] += 1
    return es.enter_context(nc.psum_tensor(f"{name}__{_NAME_CTR[0]}", shape, dt))



def blk_T(W, no):
    o, i = W.shape
    assert o == no * 128
    return np.ascontiguousarray(W.reshape(no, 128, i // 128, 128).transpose(0, 3, 2, 1))


def blk_N(W, no):
    i, o = W.shape
    assert o == no * 128
    return np.ascontiguousarray(W.reshape(i // 128, 128, no, 128).transpose(2, 1, 0, 3))


def col_bias(b):
    return np.ascontiguousarray(b.reshape(-1, 128).T)



class _C:
    pass


def alloc_common(B, es, nc, dram, triu=False, scr_b=False):
    C = _C()
    C.ident = _sb(es, nc, "ident_c", [128, 128], BF16)
    C.ones_col = _sb(es, nc, "ones_col", [128, 1], BF16)
    C.ones_row = _sb(es, nc, "ones_row", [1, 128], F32)
    C.ones128 = _sb(es, nc, "ones128", [128, 128], BF16)
    C.wb = _sb(es, nc, "wb", [128, 1024], BF16)
    C.sq = _sb(es, nc, "sq", [128, 1024], BF16)
    C.scr_a = _sb(es, nc, "scr_a", [128, 1024], F32)
    C.row_s = _sb(es, nc, "row_s", [1, 512], F32)

    B.block()
    def initc(g):
        B.inc(g.dma_start(C.ident[:], dram["ident"][:]), "ini_g", 16)
        g.memset(C.ones_col[:], 1.0)
        B.inc(g.memset(C.ones128[:], 1.0), "gp")
        B.inc(g.memset(C.ones_row[:], 1.0), "gp")
    B.g(initc)
    C.tok_ones128 = B.tok("gp")
    return C


def emit_ln(B, C, nc, W, y_dst_fn, ntt, tag, w_wr=None):
    with ExitStack() as S:
        ps_s1 = [_ps(S, nc, f"lns1{tag}{i}", [128, 512]) for i in range(2)]
        ps_s2 = [_ps(S, nc, f"lns2{tag}{i}", [128, 512]) for i in range(2)]
        mt = [_sb(S, nc, f"lnm{tag}{i}", [128, 512], F32) for i in range(2)]
        rt_s = _sb(S, nc, f"lnr{tag}", [128, 512], F32)
        rt = [rt_s, rt_s]
        sv = [_sb(S, nc, f"lnv{tag}{i}", [128, 512], F32) for i in range(2)]

        mm_entry = B.tok("mm")
        cast_tok = {}
        stat_tok = {}
        mm_seen = {}
        var_tok = {}
        sd_tok = {}
        norm_tok = {}

        def emit_cast(tt):
            tsl = slice(tt * 512, (tt + 1) * 512)
            mm_now = B.cnt["mm"]
            def gcast(g, tt=tt, tsl=tsl, mm_now=mm_now):
                for e in range(NC):
                    esl = slice((e % 2) * 512, (e % 2) * 512 + 512)
                    prev = (tt * NC + e) - 2
                    if prev < 0:
                        B.wait(g, mm_entry)
                    elif prev < tt * NC:
                        B.wait(g, mm_seen[divmod(prev, NC)])
                    else:
                        B.wait(g, ("mm", mm_now + (prev - tt * NC) + 1))
                    if w_wr is not None:
                        B.wait(g, w_wr.get((e, tt)))
                    g.tensor_copy(C.wb[:, esl], W[:, e, tsl])
                    cast_tok[(tt, e)] = B.inc(
                        g.tensor_tensor(C.sq[:, esl], C.wb[:, esl], C.wb[:, esl],
                                        ALU.mult), "gp")
            B.g(gcast)

        def emit_stats(tt):
            par = tt % 2
            def tstat(t, tt=tt, par=par):
                if tt == 0:
                    B.wait(t, C.tok_ones128)
                if tt >= 2:
                    B.wait(t, var_tok[tt - 2])
                for e in range(NC):
                    esl = slice((e % 2) * 512, (e % 2) * 512 + 512)
                    B.wait(t, cast_tok[(tt, e)])
                    t.matmul(ps_s1[par][:], C.ones128[:], C.wb[:, esl],
                             start=(e == 0), stop=(e == NC - 1))
                    mm_seen[(tt, e)] = B.inc(
                        t.matmul(ps_s2[par][:], C.ones128[:], C.sq[:, esl],
                                 start=(e == 0), stop=(e == NC - 1)), "mm")
            B.t(tstat)
            stat_tok[tt] = B.tok("mm")

        def emit_var(tt):
            par = tt % 2
            def vvar(v, tt=tt, par=par):
                B.wait(v, stat_tok[tt])
                t1 = B.inc(v.tensor_scalar(mt[par][:], ps_s1[par][:],
                                           1.0 / 1024.0, None, ALU.mult), "dv")
                B.wait(v, t1)
                t2 = B.inc(v.tensor_tensor(rt[par][:], mt[par][:], mt[par][:],
                                           ALU.mult), "dv")
                B.wait(v, t2)
                t3 = B.inc(v.tensor_scalar(sv[par][:], ps_s2[par][:],
                                           1.0 / 1024.0, 1e-5,
                                           ALU.mult, ALU.add), "dv")
                B.wait(v, t3)
                B.inc(v.tensor_tensor(sv[par][:], sv[par][:], rt[par][:],
                                      ALU.subtract), "dv")
            B.v(vvar)
            var_tok[tt] = B.tok("dv")
            def ssd(sc, tt=tt, par=par):
                B.wait(sc, var_tok[tt])
                B.inc(sc.activation(sv[par][:], sv[par][:], AF.Sqrt,
                                    bias=0.0, scale=1.0), "ac")
            B.s(ssd)
            sd_tok[tt] = B.tok("ac")

        def emit_norm(tt):
            par = tt % 2
            tsl = slice(tt * 512, (tt + 1) * 512)
            def vnorm(v, tt=tt, par=par, tsl=tsl):
                B.wait(v, sd_tok[tt])
                B.inc(v.reciprocal(rt[par][:], sv[par][:]), "dv")
                for e in range(NC):
                    v.tensor_tensor(sv[par][:], W[:, e, tsl], mt[par][:],
                                    ALU.subtract)
                    B.inc(v.tensor_tensor(W[:, e, tsl], sv[par][:], rt[par][:],
                                          ALU.mult), "dv")
            B.v(vnorm)
            norm_tok[tt] = B.tok("dv")
            def store(y, tt=tt, tsl=tsl):
                B.wait(y, norm_tok[tt])
                for e in range(NC):
                    B.inc(y.dma_start(y_dst_fn(e)[:, tsl], W[:, e, tsl]), "st", 16)
            B.y(store)

        for tt in range(ntt):
            emit_cast(tt)
            emit_stats(tt)
            emit_var(tt)
            if tt >= 1:
                emit_norm(tt - 1)
        emit_norm(ntt - 1)
        return B.tok("st")



DBG_L0A = False


def build_l0_attn(shared=None, d_over=None):
    if shared is not None:
        nc, es, B, C = shared
        d = d_over
        return _body_l0_attn(nc, es, B, C, d)
    nc = bass.Bass("TRN2", target_bir_lowering=False)
    d = {}
    d["x"] = nc.dram_tensor("x", [2, 1024, NTOK], F32, kind="ExternalInput").ap()
    for w in ("wq", "wk", "wv", "wu", "wo"):
        d[w] = nc.dram_tensor(w, [8, 128, 8, 128], F32, kind="ExternalInput").ap()
    for b in ("bq", "bk", "bv", "bu", "bo", "scale"):
        d[b] = nc.dram_tensor(b, [128, 8], F32, kind="ExternalInput").ap()
    d["ident"] = nc.dram_tensor("ident", [128, 128], F32, kind="ExternalInput").ap()
    d["triu"] = nc.dram_tensor("triu", [128, 128], F32, kind="ExternalInput").ap()
    d["y"] = nc.dram_tensor("y", [2, 1024, NTOK], F32, kind="ExternalOutput").ap()
    if DBG_L0A:
        d["dbg_q"] = nc.dram_tensor("dbg_q", [128, NTOK], BF16, kind="ExternalOutput").ap()
        d["dbg_k"] = nc.dram_tensor("dbg_k", [128, NTOK], BF16, kind="ExternalOutput").ap()
        d["dbg_v"] = nc.dram_tensor("dbg_v", [128, NTOK], BF16, kind="ExternalOutput").ap()
        d["dbg_O"] = nc.dram_tensor("dbg_O", [128, NC, NTOK], BF16, kind="ExternalOutput").ap()
        d["dbg_W"] = nc.dram_tensor("dbg_W", [128, NC, NTOK], F32, kind="ExternalOutput").ap()
        d["dbg_rr"] = nc.dram_tensor("dbg_rr", [1, NTOK], F32, kind="ExternalOutput").ap()

    es = ExitStack()
    B = Bld(nc, es)
    C = alloc_common(B, es, nc, d, triu=True, scr_b=True)
    _body_l0_attn(nc, es, B, C, d)
    B.block()
    def fin(y):
        B.wait_all_dma(y)
    B.y(fin)
    B.end()
    es.close()
    return nc


def _body_l0_attn(nc, es, B, C, d):
    C.scr_b = _sb(es, nc, "scr_b", [128, 2048], F32)
    C.triu = _sb(es, nc, "triu_s1", [128, 128], F32)
    B.block()
    def ldtriu(y):
        B.inc(y.dma_start(C.triu[:], d["triu"][:]), "ini_y", 16)
    B.y(ldtriu)
    biases = {}
    for b in ("bq", "bk", "bv", "bu", "bo", "scale"):
        biases[b] = _sb(es, nc, b + "_s", [128, 8], F32)
    B.block()
    def ldb(y):
        for b in ("bq", "bk", "bv", "bu", "bo", "scale"):
            B.inc(y.dma_start(biases[b][:], d[b][:]), "ini_y", 16)
    B.y(ldb)

    st_stage = B.tok("st")
    for b in (0, 1):
        PHASE_MARKS.append((f"a{b}:start", B.cnt["mm"]))
        st_b = B.tok("st")
        with ExitStack() as Sb:
            Xb = _sb(Sb, nc, f"Xb{b}", [128, NC, NTOK], BF16)
            O_sb = _sb(Sb, nc, f"O{b}", [128, NC, NTOK], BF16)
            B.block()
            mm_now = B.tok("mm")
            def _fence_v(v, tok=st_b):
                B.wait(v, tok)
            B.v(_fence_v)
            def _fence_s(sc, tok=st_b):
                B.wait(sc, tok)
            B.s(_fence_s)

            with ExitStack() as Sa:
                qh = [_sb(Sa, nc, f"qh{b}{i}", [128, NTOK], BF16) for i in range(2)]
                kh = [_sb(Sa, nc, f"kh{b}{i}", [128, NTOK], BF16) for i in range(2)]
                vh = [_sb(Sa, nc, f"vh{b}{i}", [128, NTOK], BF16) for i in range(2)]
                wqb = [_sb(Sa, nc, f"wqb{b}{i}", [128, 8, 128], BF16) for i in range(2)]
                wkb = [_sb(Sa, nc, f"wkb{b}{i}", [128, 8, 128], BF16) for i in range(2)]
                wvb = [_sb(Sa, nc, f"wvb{b}{i}", [128, 8, 128], BF16) for i in range(2)]
                S_sb = _sb(Sa, nc, f"S{b}", [128, 128], BF16)
                kT_all = [_sb(Sa, nc, f"kTa{b}{i}", [128, 16, 128], BF16) for i in range(2)]
                vT_all = [_sb(Sa, nc, f"vTa{b}{i}", [128, 16, 128], BF16) for i in range(2)]
                AT_all = [_sb(Sa, nc, f"ATa{b}{i}", [128, 16, 128], BF16) for i in range(2)]
                ps_p = [_ps(Sa, nc, f"psp{b}{i}", [128, 512]) for i in range(4)]
                ps_S = _ps(Sa, nc, f"psS{b}", [128, 128])
                ps_O = _ps(Sa, nc, f"psO{b}", [128, 128])
                ps_T = _ps(Sa, nc, f"psT{b}", [128, 128], BF16)
                ps_A = _ps(Sa, nc, f"psA{b}", [128, 128])

                B.block()
                def ldw0(g, b=b, mm_now=mm_now, st_now=st_stage):
                    B.wait(g, mm_now)
                    B.wait(g, st_now)
                    B.wait(g, B.tok("ini_y"))
                    B.wait(g, B.tok("ini_g"))
                    for wbuf, wd in ((wqb, "wq"), (wkb, "wk"), (wvb, "wv")):
                        B.inc(g.dma_start(wbuf[0][:], d[wd][0]), "w0", 16)
                    for tt in range(4):
                        tsl = slice(tt * 512, (tt + 1) * 512)
                        B.inc(g.dma_start(
                            Xb[:, :, tsl],
                            d["x"][b, :, tsl].rearrange("(c p) t -> p c t", p=128)),
                            f"dx{tt}", 16)
                B.g(ldw0)
                wtoks = {0: B.tok("w0")}
                tok_xt = [B.tok(f"dx{tt}") for tt in range(4)]

                prev_tail = {}
                pipe = {}
                for h in range(9):
                    hp = h % 2
                    if h < 8:
                        def ldw(g, h=h, hp=hp):
                            if h + 1 < 8:
                                if h >= 1:
                                    B.wait(g, prev_tail["mmw"])
                                for wbuf, wd in ((wqb, "wq"), (wkb, "wk"), (wvb, "wv")):
                                    B.inc(g.dma_start(wbuf[1 - hp][:], d[wd][h + 1]),
                                          f"w{(h + 1) % 2}", 16)
                                wtoks[h + 1] = B.tok(f"w{(h + 1) % 2}")
                        B.g(ldw)

                        dv0 = B.cnt["dv"]
                        ac0 = B.cnt["ac"]
                        evac_tok = {}
                        for g in range(12):
                            if g < 4:
                                evac_tok[g] = ("dv", dv0 + 6 + 2 * g)
                            elif g < 8:
                                evac_tok[g] = ("dv", dv0 + 18 + 2 * (g - 4))
                            else:
                                evac_tok[g] = ("ac", ac0 + 9 + (g - 8))
                        m = {}
                        def tproj(t, h=h, hp=hp, dv0=dv0, ac0=ac0):
                            B.wait(t, wtoks[h])
                            for g in range(12):
                                pi, tt = g // 4, g % 4
                                tsl = slice(tt * 512, (tt + 1) * 512)
                                if h == 0:
                                    B.wait(t, tok_xt[tt])
                                if g < 4:
                                    if h == 0:
                                        B.wait(t, ("dv", dv0))
                                        B.wait(t, ("ac", ac0))
                                    else:
                                        B.wait(t, pipe[("v_done", h - 1)])
                                else:
                                    B.wait(t, evac_tok[g - 4])
                                wbuf = (wqb, wkb, wvb)[pi]
                                for ci in range(NC):
                                    mi = t.matmul(ps_p[g % 4][:], wbuf[hp][:, ci, :],
                                                  Xb[:, ci, tsl],
                                                  start=(ci == 0), stop=(ci == NC - 1))
                                m[g] = B.inc(mi, "mm")
                        B.t(tproj)
                        prev_tail["mmw"] = B.tok("mm")

                        for pi, (dstl, bcol, gbase) in enumerate(
                                ((qh, "bq", 0), (kh, "bk", 4))):
                            dst = dstl[hp]
                            def vb1(v, dst=dst, bcol=bcol, gbase=gbase, pi=pi,
                                    dv0=dv0, ac0=ac0, h=h):
                                B.wait(v, ("dv", dv0 + 12 * pi))
                                for tt in range(4):
                                    B.wait(v, m[gbase + tt])
                                    if tt >= 2:
                                        B.wait(v, ("ac", ac0 + 4 * pi + tt - 1))
                                    B.inc(v.tensor_scalar(
                                        C.scr_a[:, (tt % 2) * 512:(tt % 2) * 512 + 512],
                                        ps_p[(gbase + tt) % 4][:],
                                        biases[bcol][:, h:h + 1], 0.0, ALU.add, ALU.min), "dv")
                            B.v(vb1)
                            def sb1(sc, gbase=gbase, pi=pi, dv0=dv0, ac0=ac0):
                                mb = dv0 + 12 * pi
                                B.wait(sc, ("dv", dv0 + 12 * pi))
                                for tt in range(4):
                                    B.wait(sc, ("dv", mb + 1 + tt))
                                    B.inc(sc.activation(
                                        C.scr_b[:, tt * 512:tt * 512 + 512],
                                        C.scr_a[:, (tt % 2) * 512:(tt % 2) * 512 + 512],
                                        AF.Exp, bias=0.0, scale=1.0), "ac")
                            B.s(sb1)
                            def vb2(v, dst=dst, bcol=bcol, gbase=gbase, pi=pi,
                                    dv0=dv0, ac0=ac0, h=h):
                                B.wait(v, ("dv", dv0 + 12 * pi + 4))
                                B.wait(v, ("ac", ac0 + 4 * pi + 4))
                                for tt in range(4):
                                    tsl = slice(tt * 512, (tt + 1) * 512)
                                    ssl = slice((tt % 2) * 512, (tt % 2) * 512 + 512)
                                    t1 = B.inc(v.tensor_scalar(C.scr_a[:, ssl],
                                                               ps_p[(gbase + tt) % 4][:],
                                                               biases[bcol][:, h:h + 1],
                                                               0.0, ALU.add, ALU.max), "dv")
                                    B.wait(v, t1)
                                    B.inc(v.scalar_tensor_tensor(
                                        dst[:, tsl],
                                        C.scr_b[:, tt * 512:tt * 512 + 512],
                                        -1.0, C.scr_a[:, ssl], ALU.add, ALU.add), "dv")
                            B.v(vb2)
                        def sbv(sc, h=h, hp=hp, ac0=ac0):
                            for tt in range(4):
                                tsl = slice(tt * 512, (tt + 1) * 512)
                                B.wait(sc, m[8 + tt])
                                B.inc(sc.activation(vh[hp][:, tsl], ps_p[(8 + tt) % 4][:],
                                                    AF.Silu,
                                                    bias=biases["bv"][:, h:h + 1],
                                                    scale=1.0), "ac")
                        B.s(sbv)
                        pipe[("qk_done", h)] = ("dv", dv0 + 24)
                        pipe[("v_done", h)] = ("ac", ac0 + 12)

                    mA, mK, mV, mB = {}, {}, {}, {}
                    hb = h - 1
                    hbp = hb % 2
                    sA = (2 if hb >= 0 else 0) + (2 if h < 8 else 0)
                    O_off = 2
                    kT_off = (2 if hb >= 0 else 0) + 1
                    vT_off = kT_off + 1
                    prev_dv = prev_tail.get("dv_all")
                    prev_ac = prev_tail.get("ac_all")
                    acB0 = B.cnt["ac"]
                    dvA0 = B.cnt["dv"]
                    def tcomb(t, h=h, hp=hp, hb=hb, hbp=hbp, acB0=acB0, dvA0=dvA0,
                              sA=sA, O_off=O_off, kT_off=kT_off, vT_off=vT_off,
                              prev_dv=prev_dv, prev_ac=prev_ac):
                        if h < 8:
                            B.wait(t, pipe[("qk_done", h)])
                            B.wait(t, pipe[("v_done", h)])
                        if hb >= 0:
                            B.wait(t, prev_dv)
                            B.wait(t, prev_ac)
                        for c in range(16):
                            cs = slice(c * 128, (c + 1) * 128)
                            if hb >= 0:
                                if c > 0:
                                    B.wait(t, ("ac", acB0 + sA * (c - 1) + O_off))
                                    t.matmul(ps_O[:], S_sb[:], qh[hbp][:, cs],
                                             start=True, stop=False)
                                t.matmul(ps_O[:], vT_all[hbp][:, c, :], AT_all[hbp][:, c, :],
                                         start=(c == 0), stop=True)
                                mB[c] = B.inc(t.matmul(ps_S[:], kT_all[hbp][:, c, :],
                                                       vT_all[hbp][:, c, :],
                                                       start=(c == 0), stop=(c == 15),
                                                       skip_group_check=True), "mm")
                            if h < 8:
                                if c >= 1:
                                    B.wait(t, ("dv", dvA0 + c))
                                mA[c] = B.inc(t.matmul(ps_A[:], kh[hp][:, cs], qh[hp][:, cs],
                                                       start=True, stop=True), "mm")
                                if c >= 1:
                                    B.wait(t, ("ac", acB0 + sA * (c - 1) + vT_off))
                                mK[c] = B.inc(t.transpose(ps_T[:], kh[hp][:, cs],
                                                          C.ident[:]), "mm")
                                B.wait(t, ("ac", acB0 + sA * c + kT_off))
                                mV[c] = B.inc(t.transpose(ps_T[:], vh[hp][:, cs],
                                                          C.ident[:]), "mm")
                    B.t(tcomb)
                    def vcomb(v, h=h, hb=hb, hp=hp):
                        if h < 8:
                            for c in range(16):
                                B.wait(v, mA[c])
                                B.inc(v.tensor_tensor(AT_all[hp][:, c, :], ps_A[:],
                                                      C.triu[:], ALU.mult), "dv")
                    B.v(vcomb)
                    def scomb(sc, h=h, hb=hb, hp=hp, hbp=hbp):
                        for c in range(16):
                            cs = slice(c * 128, (c + 1) * 128)
                            if hb >= 0:
                                B.wait(sc, mB[c])
                                B.inc(sc.activation(S_sb[:], ps_S[:], AF.Identity,
                                                    bias=0.0, scale=1.0), "ac")
                                B.inc(sc.activation(O_sb[:, hb, cs], ps_O[:], AF.Identity,
                                                    bias=0.0, scale=1.0), "ac")
                            if h < 8:
                                B.wait(sc, mK[c])
                                B.inc(sc.activation(kT_all[hp][:, c, :], ps_T[:],
                                                    AF.Identity, bias=0.0, scale=1.0), "ac")
                                B.wait(sc, mV[c])
                                B.inc(sc.activation(vT_all[hp][:, c, :], ps_T[:],
                                                    AF.Identity, bias=0.0, scale=1.0), "ac")
                    B.s(scomb)
                    prev_tail["dv_all"] = B.tok("dv")
                    prev_tail["ac_all"] = B.tok("ac")

            with ExitStack() as Sf:
                W = _sb(Sf, nc, f"W{b}", [128, NC, NTOK], F32)
                u_t = _sb(Sf, nc, f"ut{b}", [128, NC, 512], BF16)
                of_t = _sb(Sf, nc, f"oft{b}", [128, NC, 512], BF16)
                wub = [_sb(Sf, nc, f"wub{b}{i}", [128, 8, 128], BF16) for i in range(2)]
                wob = [_sb(Sf, nc, f"wob{b}{i}", [128, 8, 128], BF16) for i in range(2)]
                rstd_t = [_sb(Sf, nc, f"rstd{b}{i}", [128, 512], F32) for i in range(2)]
                rstd_s = _sb(Sf, nc, f"rstds{b}", [128, 512], F32)
                xres = [_sb(Sf, nc, f"xres{b}{i}", [128, 512], F32) for i in range(2)]

                PHASE_MARKS.append((f"a{b}:fin", B.cnt["mm"]))
                B.block()
                with ExitStack() as Sp:
                    ps_p2 = [_ps(Sp, nc, f"psp2{b}{i}", [128, 512]) for i in range(2)]
                    ps_bcr = _ps(Sp, nc, f"psbcr{b}", [128, 512])
                    def ldwuo(g):
                        B.inc(g.dma_start(wub[0][:], d["wu"][0]), "u0", 16)
                    B.g(ldwuo)
                    wu_tok0 = B.tok("u0")
                    prev = {"of_end": None, "wmm_end": None, "wev_end": None,
                            "silu_end": None}
                    w_wr = {}
                    sq_tok = {}
                    rc_tok = {}
                    for tt in range(4):
                        tsl = slice(tt * 512, (tt + 1) * 512)
                        mm0 = B.cnt["mm"]
                        dv0s = B.cnt["dv"]
                        def vsq(v, tt=tt, tsl=tsl, mm0=mm0):
                            for hh in range(8):
                                esl = slice((hh % 2) * 512, (hh % 2) * 512 + 512)
                                if hh >= 2:
                                    B.wait(v, ("mm", mm0 + hh - 1))
                                B.inc(v.tensor_tensor(C.sq[:, esl], O_sb[:, hh, tsl],
                                                      O_sb[:, hh, tsl], ALU.mult), "dv")
                        B.v(vsq)
                        def tsq(t, tt=tt, mm0=mm0, dv0s=dv0s):
                            if tt == 0:
                                B.wait(t, C.tok_ones128)
                            B.wait(t, sq_tok.get(tt - 1))
                            for hh in range(8):
                                esl = slice((hh % 2) * 512, (hh % 2) * 512 + 512)
                                B.wait(t, ("dv", dv0s + hh + 1))
                                B.inc(t.matmul(ps_bcr[:, 0:512], C.ones128[:], C.sq[:, esl],
                                               start=(hh == 0), stop=(hh == 7)), "mm")
                        B.t(tsq)
                        tok_ss = B.tok("mm")
                        def ssq(sc, tt=tt, tok_ss=tok_ss):
                            B.wait(sc, tok_ss)
                            B.wait(sc, rc_tok.get(tt - 1))
                            B.inc(sc.activation(rstd_s[:], ps_bcr[:, 0:512], AF.Sqrt,
                                                bias=0.0, scale=1.0 / 1024.0), "ac")
                        B.s(ssq)
                        sq_tok[tt] = B.tok("ac")
                        def vrc(v, tt=tt):
                            B.wait(v, sq_tok[tt])
                            t1 = B.inc(v.tensor_scalar(rstd_s[:], rstd_s[:], 1e-8,
                                                       None, ALU.add), "dv")
                            B.wait(v, t1)
                            B.inc(v.reciprocal(rstd_t[tt % 2][:], rstd_s[:]), "dv")
                        B.v(vrc)
                        rc_tok[tt] = B.tok("dv")
                        tok_rstd = rc_tok[tt]

                        acU0 = B.cnt["ac"]
                        u_toks = {}
                        def ldu(g, tt=tt, acU0=acU0, u_toks=u_toks):
                            for h in range(8):
                                if (tt, h) == (0, 0):
                                    u_toks[h] = wu_tok0
                                    continue
                                if h >= 2 or tt > 0:
                                    B.wait(g, ("ac", acU0 + h - 1))
                                u_toks[h] = B.inc(g.dma_start(wub[h % 2][:], d["wu"][h]),
                                                  f"u{h % 2}", 16)
                        B.g(ldu)

                        mmU = {}
                        def tu(t, tt=tt, tsl=tsl, acU0=acU0, u_toks=u_toks,
                               pv=prev["wev_end"]):
                            B.wait(t, pv)
                            for h in range(8):
                                B.wait(t, u_toks[h])
                                if h >= 2:
                                    B.wait(t, ("ac", acU0 + h - 1))
                                for ci in range(NC):
                                    mi = t.matmul(ps_p2[h % 2][:], wub[h % 2][:, ci, :],
                                                  Xb[:, ci, tsl],
                                                  start=(ci == 0), stop=(ci == NC - 1))
                                mmU[h] = B.inc(mi, "mm")
                        B.t(tu)
                        def su(sc, tt=tt, acU0=acU0, pv=prev["of_end"]):
                            B.wait(sc, pv)
                            for h in range(8):
                                B.wait(sc, mmU[h])
                                B.inc(sc.activation(u_t[:, h, :], ps_p2[h % 2][:], AF.Silu,
                                                    bias=biases["bu"][:, h:h + 1],
                                                    scale=1.0), "ac")
                        B.s(su)
                        prev["silu_end"] = B.tok("ac")
                        def vof(v, tt=tt, tsl=tsl, acU0=acU0, tok_rstd=tok_rstd,
                                pv=prev["wmm_end"]):
                            B.wait(v, tok_rstd)
                            B.wait(v, pv)
                            t2 = None
                            for h in range(8):
                                B.wait(v, ("ac", acU0 + h + 1))
                                B.wait(v, t2)
                                t1 = B.inc(v.scalar_tensor_tensor(
                                    C.scr_a[:, 0:512], O_sb[:, h, tsl],
                                    biases["scale"][:, h:h + 1], rstd_t[tt % 2][:],
                                    ALU.mult, ALU.mult), "dv")
                                B.wait(v, t1)
                                t2 = B.inc(v.tensor_tensor(of_t[:, h, :], C.scr_a[:, 0:512],
                                                           u_t[:, h, :], ALU.mult), "dv")
                        B.v(vof)
                        prev["of_end"] = B.tok("dv")

                        o_tok = {}
                        xr_tok = {}
                        dvW0 = B.cnt["dv"]
                        def ldwo(g, tt=tt, dvW0=dvW0, o_tok=o_tok):
                            for e in range(8):
                                if e >= 2:
                                    B.wait(g, ("dv", dvW0 + e - 1))
                                elif tt > 0:
                                    B.wait(g, ("dv", dvW0))
                                o_tok[e] = B.inc(g.dma_start(wob[e % 2][:], d["wo"][e]),
                                                 f"o{e % 2}", 16)
                        B.g(ldwo)
                        def ldxr(y, tt=tt, b=b, dvW0=dvW0, xr_tok=xr_tok):
                            for e in range(8):
                                if e >= 2:
                                    B.wait(y, ("dv", dvW0 + e - 1))
                                elif tt > 0:
                                    B.wait(y, ("dv", dvW0))
                                xr_tok[e] = B.inc(y.dma_start(
                                    xres[e % 2][:],
                                    d["x"][b, e * 128:(e + 1) * 128,
                                           tt * 512:(tt + 1) * 512]), f"xr{e % 2}", 16)
                        B.y(ldxr)

                        mmW = {}
                        def tw(t, tt=tt, dvW0=dvW0, pv=prev["silu_end"]):
                            B.wait(t, prev["of_end"])
                            for e in range(8):
                                B.wait(t, o_tok[e])
                                if e >= 2:
                                    B.wait(t, ("dv", dvW0 + e - 1))
                                for hc in range(NC):
                                    mi = t.matmul(ps_p2[e % 2][:], wob[e % 2][:, hc, :],
                                                  of_t[:, hc, :],
                                                  start=(hc == 0), stop=(hc == NC - 1))
                                mmW[e] = B.inc(mi, "mm")
                        B.t(tw)
                        prev["wmm_end"] = B.tok("mm")
                        def vw(v, tt=tt, tsl=tsl, dvW0=dvW0):
                            if tt == 0:
                                B.wait(v, st_b)
                            for e in range(8):
                                B.wait(v, mmW[e])
                                B.wait(v, xr_tok[e])
                                w_wr[(e, tt)] = B.inc(
                                    v.scalar_tensor_tensor(W[:, e, tsl], ps_p2[e % 2][:],
                                                           biases["bo"][:, e:e + 1],
                                                           xres[e % 2][:],
                                                           ALU.add, ALU.add), "dv")
                        B.v(vw)
                        prev["wev_end"] = B.tok("dv")

                if DBG_L0A and b == 0:
                    def dbgW(y):
                        B.wait(y, B.tok("dv"))
                        B.inc(y.dma_start(d["dbg_W"][:], W[:]), "st", 16)
                    B.y(dbgW)
                    def dbgWw(v):
                        B.wait(v, B.tok("st"))
                    B.v(dbgWw)

                B.block()
                PHASE_MARKS.append((f"a{b}:ln", B.cnt["mm"]))
                emit_ln(B, C, nc, W,
                        lambda e, b=b: d["y"][b, e * 128:(e + 1) * 128, :], 4, f"l0a{b}",
                        w_wr=w_wr)


def l0_attn_inputs(shard_xT, I):
    return {
        "x": shard_xT,
        "wq": blk_T(I["L0_attn_Wq"], 8), "wk": blk_T(I["L0_attn_Wk"], 8),
        "wv": blk_T(I["L0_attn_Wv"], 8), "wu": blk_T(I["L0_attn_Wu"], 8),
        "wo": blk_T(I["L0_attn_Wo"], 8),
        "bq": col_bias(I["L0_attn_bq"]), "bk": col_bias(I["L0_attn_bk"]),
        "bv": col_bias(I["L0_attn_bv"]), "bu": col_bias(I["L0_attn_bu"]),
        "bo": col_bias(I["L0_attn_bo"]), "scale": col_bias(I["L0_attn_scale"]),
        "ident": np.eye(128, dtype=np.float32),
        "triu": np.triu(np.ones((128, 128), np.float32)),
    }



DBG_L0F = False


def build_l0_ffn(shared=None, d_over=None):
    if shared is not None:
        nc, es, B, C = shared
        return _body_l0_ffn(nc, es, B, C, d_over)
    nc = bass.Bass("TRN2", target_bir_lowering=False)
    d = {}
    d["x"] = nc.dram_tensor("x", [2, 1024, NTOK], F32, kind="ExternalInput").ap()
    d["gw1"] = nc.dram_tensor("gw1", [2, 128, 8, 128], F32, kind="ExternalInput").ap()
    d["gb1"] = nc.dram_tensor("gb1", [128, 2], F32, kind="ExternalInput").ap()
    d["gw2t"] = nc.dram_tensor("gw2t", [2, 128, 8], F32, kind="ExternalInput").ap()
    d["gb2"] = nc.dram_tensor("gb2", [1, 8], F32, kind="ExternalInput").ap()
    for p in "qkvo":
        d["wm" + p] = nc.dram_tensor("wm" + p, [8, 128, 8, 128], F32, kind="ExternalInput").ap()
        d["vt" + p] = nc.dram_tensor("vt" + p, [128, 8, 128], F32, kind="ExternalInput").ap()
        d["ut" + p] = nc.dram_tensor("ut" + p, [128, 8, 128], F32, kind="ExternalInput").ap()
        d["b" + p] = nc.dram_tensor("b" + p, [128, 8], F32, kind="ExternalInput").ap()
    d["svec"] = nc.dram_tensor("svec", [128, 4], F32, kind="ExternalInput").ap()
    d["ind"] = nc.dram_tensor("ind", [8, 128], F32, kind="ExternalInput").ap()
    d["ident"] = nc.dram_tensor("ident", [128, 128], F32, kind="ExternalInput").ap()
    d["y"] = nc.dram_tensor("y", [2, 1024, NTOK], F32, kind="ExternalOutput").ap()
    if DBG_L0F:
        d["dbg_w"] = nc.dram_tensor("dbg_w", [1, 8], F32, kind="ExternalOutput").ap()
        d["dbg_q"] = nc.dram_tensor("dbg_q", [128, NC, NTOK], BF16, kind="ExternalOutput").ap()
        d["dbg_ctx"] = nc.dram_tensor("dbg_ctx", [128, NC, NTOK], BF16, kind="ExternalOutput").ap()
        d["dbg_W"] = nc.dram_tensor("dbg_W", [128, NC, NTOK], F32, kind="ExternalOutput").ap()

    es = ExitStack()
    B = Bld(nc, es)
    C = alloc_common(B, es, nc, d)
    _body_l0_ffn(nc, es, B, C, d)
    B.block()
    def fin(y):
        B.wait_all_dma(y)
    B.y(fin)
    B.end()
    es.close()
    return nc


def _body_l0_ffn(nc, es, B, C, d):
    biases = {}
    for p in "qkvo":
        biases[p] = _sb(es, nc, f"b{p}_s", [128, 8], F32)
    gb1_s = _sb(es, nc, "gb1_s", [128, 2], F32)
    gb2_s = _sb(es, nc, "gb2_s", [1, 8], F32)
    svec_s = _sb(es, nc, "svec_s", [128, 4], F32)
    ind_s = _sb(es, nc, "ind_s", [8, 128], F32)
    ssc = _sb(es, nc, "ssc", [128, 4], F32)
    B.block()
    def ldb(y):
        for p in "qkvo":
            B.inc(y.dma_start(biases[p][:], d["b" + p][:]), "ini_y", 16)
        B.inc(y.dma_start(gb1_s[:], d["gb1"][:]), "ini_y", 16)
        B.inc(y.dma_start(gb2_s[:], d["gb2"][:]), "ini_y", 16)
        B.inc(y.dma_start(svec_s[:], d["svec"][:]), "ini_y", 16)
        B.inc(y.dma_start(ind_s[:], d["ind"][:]), "ini_y", 16)
    B.y(ldb)
    tok_ini = B.tok("ini_y")
    st_stage = B.tok("st")

    def proj_svdmoe(b, p, pcol, src_sb, dst_write, wmb2, vtb, utb, xvs, ps_xv, ps_p,
                    inline_v=None):
        tok_pdv = B.tok("dv")
        tok_pac = B.tok("ac")
        def ldvu(g):
            B.wait(g, B.tok("mm"))
            B.inc(g.dma_start(vtb[:], d["vt" + p]), "vt", 16)
            B.inc(g.dma_start(utb[:], d["ut" + p]), "ut", 16)
        B.g(ldvu)
        tok_vu = (B.tok("vt"), B.tok("ut"))

        toks_xv = {}
        for tt in range(4):
            tsl = slice(tt * 512, (tt + 1) * 512)
            def xmm(t, tt=tt, tsl=tsl):
                B.wait(t, tok_vu[0])
                if tt == 0:
                    B.wait(t, tok_pdv)
                    B.wait(t, tok_pac)
                B.wait(t, toks_xv.get(tt - 1))
                for ci in range(NC):
                    mi = t.matmul(ps_xv[tt % 2][:], vtb[:, ci, :], src_sb[:, ci, tsl],
                                  start=(ci == 0), stop=(ci == NC - 1))
                B.inc(mi, "mm")
            B.t(xmm)
            tok_xmm = B.tok("mm")
            def xev(v, tt=tt, tok_xmm=tok_xmm):
                B.wait(v, tok_xmm)
                B.inc(v.tensor_copy(xvs[:, tt, :], ps_xv[tt % 2][:]), "dv")
            B.v(xev)
            toks_xv[tt] = B.tok("dv")
        def vus(v):
            B.wait(v, tok_vu[1])
            B.wait(v, tok_gate)
            B.inc(v.tensor_scalar(utb[:], utb[:], ssc[:, pcol:pcol + 1],
                                  None, ALU.mult), "dv")
        B.v(vus)
        tok_uts = B.tok("dv")

        ev_toks = {}
        wm_tok = {}
        for oc in range(8):
            def ldwm(g, oc=oc):
                if oc < 2:
                    B.wait(g, B.tok("mm"))
                else:
                    B.wait(g, ev_toks[(oc - 2, 3)])
                wm_tok[oc] = B.inc(g.dma_start(wmb2[oc % 2][:], d["wm" + p][oc]),
                                   f"wm{oc % 2}", 16)
            B.g(ldwm)
            for tt in range(4):
                tsl = slice(tt * 512, (tt + 1) * 512)
                idx = oc * 4 + tt
                pp = ps_p[idx % 4]
                def mms(t, oc=oc, tt=tt, tsl=tsl, pp=pp, idx=idx):
                    B.wait(t, wm_tok[oc])
                    if idx < 4:
                        B.wait(t, tok_pdv)
                        B.wait(t, tok_pac)
                    B.wait(t, ev_toks.get(divmod(idx - 4, 4)))
                    for ci in range(NC):
                        t.matmul(pp[:], wmb2[oc % 2][:, ci, :], src_sb[:, ci, tsl],
                                 start=(ci == 0), stop=False)
                    B.wait(t, tok_uts)
                    B.wait(t, toks_xv[tt])
                    mi = t.matmul(pp[:], utb[:, oc, :], xvs[:, tt, :],
                                  start=False, stop=True)
                    B.inc(mi, "mm")
                B.t(mms)
                tok_mm = B.tok("mm")
                ev_toks[(oc, tt)] = dst_write(oc, tt, tsl, pp, tok_mm)
                if inline_v is not None:
                    inline_v(oc, tt, tsl, ev_toks[(oc, tt)])
        return ev_toks

    def proj_svdmoe_coarse(b, p, pcol, src_sb, wmb2, vtb, utb, xvs, ps_xv, ps_p,
                           evac_body):
        tok_pdv = B.tok("dv")
        tok_pac = B.tok("ac")
        def ldvu(g):
            B.wait(g, B.tok("mm"))
            B.inc(g.dma_start(vtb[:], d["vt" + p]), "vt", 16)
            B.inc(g.dma_start(utb[:], d["ut" + p]), "ut", 16)
        B.g(ldvu)
        tok_vu = (B.tok("vt"), B.tok("ut"))

        dvX0 = B.cnt["dv"]
        m_xv = {}
        def txv(t):
            for tt in range(4):
                B.wait(t, tok_vu[0])
                if tt == 0:
                    B.wait(t, tok_pdv)
                    B.wait(t, tok_pac)
                if tt >= 2:
                    B.wait(t, ("dv", dvX0 + tt - 1))
                for ci in range(NC):
                    mi = t.matmul(ps_xv[tt % 2][:], vtb[:, ci, :],
                                  src_sb[:, ci, slice(tt * 512, (tt + 1) * 512)],
                                  start=(ci == 0), stop=(ci == NC - 1))
                m_xv[tt] = B.inc(mi, "mm")
        B.t(txv)
        def vxv(v):
            for tt in range(4):
                B.wait(v, m_xv[tt])
                B.inc(v.tensor_copy(xvs[:, tt, :], ps_xv[tt % 2][:]), "dv")
            B.wait(v, tok_vu[1])
            B.wait(v, tok_gate)
            B.inc(v.tensor_scalar(utb[:], utb[:], ssc[:, pcol:pcol + 1],
                                  None, ALU.mult), "dv")
        B.v(vxv)
        tok_uts = B.tok("dv")

        dvE0 = B.cnt["dv"]
        w_tok = {}
        def ldwm(g):
            for oc in range(8):
                if oc >= 2:
                    B.wait(g, ("dv", dvE0 + 4 * (oc - 1)))
                w_tok[oc] = B.inc(g.dma_start(wmb2[oc % 2][:], d["wm" + p][oc]),
                                  f"wm{oc % 2}", 16)
        B.g(ldwm)
        m = {}
        def tmain(t):
            B.wait(t, tok_uts)
            for oc in range(8):
                B.wait(t, w_tok[oc])
                for tt in range(4):
                    g = oc * 4 + tt
                    tsl = slice(tt * 512, (tt + 1) * 512)
                    B.wait(t, ("dv", dvX0 + tt + 1))
                    if g >= 4:
                        B.wait(t, ("dv", dvE0 + g - 3))
                    for ci in range(NC):
                        t.matmul(ps_p[g % 4][:], wmb2[oc % 2][:, ci, :], src_sb[:, ci, tsl],
                                 start=(ci == 0), stop=False)
                    mi = t.matmul(ps_p[g % 4][:], utb[:, oc, :], xvs[:, tt, :],
                                  start=False, stop=True)
                    m[g] = B.inc(mi, "mm")
        B.t(tmain)
        evac_body(m, dvE0)
        return ("dv", dvE0 + 32)

    for b in (0, 1):
        st_b = B.tok("st")
        with ExitStack() as Sb:
            ctx = _sb(Sb, nc, f"ctx{b}", [128, NC, NTOK], BF16)
            with ExitStack() as S1:
                vt65 = _sb(S1, nc, f"vt65_{b}", [128, 16, 16, 65], BF16)
                q_sb = _sb(S1, nc, f"q{b}", [128, NC, NTOK], BF16)
                k_sb = _sb(S1, nc, f"k{b}", [128, NC, NTOK], BF16)

                with ExitStack() as S2:
                    Hb = _sb(S2, nc, f"Hb{b}", [128, NC, NTOK], BF16)
                    wmb = [_sb(S2, nc, f"wmb{b}{i}", [128, 8, 128], BF16) for i in range(2)]
                    vtb = _sb(S2, nc, f"vtb{b}", [128, 8, 128], BF16)
                    utb = _sb(S2, nc, f"utb{b}", [128, 8, 128], BF16)
                    xvs = _sb(S2, nc, f"xvs{b}", [128, 4, 512], BF16)
                    vtile = _sb(S2, nc, f"vtile{b}", [128, 512], BF16)

                    B.block()
                    mm_now = B.tok("mm")
                    def ldx(g, b=b, mm_now=mm_now, st_now=st_stage):
                        B.wait(g, mm_now)
                        B.wait(g, st_now)
                        B.wait(g, B.tok("ini_y"))
                        B.wait(g, B.tok("ini_g"))
                        B.inc(g.dma_start(Hb[:], d["x"][b].rearrange("(c p) t -> p c t", p=128)),
                              "dx", 16)
                    B.g(ldx)
                    tok_x = B.tok("dx")
                    def _fence_v(v, tok=st_b):
                        B.wait(v, tok)
                    B.v(_fence_v)
                    def _fence_s(sc, tok=st_b):
                        B.wait(sc, tok)
                    B.s(_fence_s)

                    PHASE_MARKS.append((f"f{b}:gate", B.cnt["mm"]))
                    with ExitStack() as Sg:
                        pooled = _sb(Sg, nc, f"pool{b}", [128, 8], F32)
                        pooled_b = _sb(Sg, nc, f"poolb{b}", [128, 8], BF16)
                        gw2_s = _sb(Sg, nc, f"gw2{b}", [128, 2, 8], BF16)
                        h1_b = _sb(Sg, nc, f"h1{b}", [128, 2], BF16)
                        lg = _sb(Sg, nc, f"lg{b}", [1, 8], F32)
                        e8 = _sb(Sg, nc, f"e8{b}", [1, 8], F32)
                        ge8 = _sb(Sg, nc, f"ge8{b}", [1, 8], F32)
                        w_row = _sb(Sg, nc, f"wrow{b}", [1, 8], F32)
                        w8 = _sb(Sg, nc, f"w8{b}", [8, 1], F32)
                        tp = _sb(Sg, nc, f"tp{b}", [1, 16], F32)
                        ps_g = _ps(Sg, nc, f"psg{b}", [128, 2])
                        ps_lg = _ps(Sg, nc, f"pslg{b}", [1, 8])
                        ps_w = _ps(Sg, nc, f"psw{b}", [8, 1])
                        ps_wer = _ps(Sg, nc, f"pswer{b}", [128, 1])

                        def pool_v(v, b=b):
                            B.wait(v, tok_x)
                            for c in range(NC):
                                mi = v.reduce_sum(pooled[:, c:c + 1], Hb[:, c, :], axis=AX.X)
                            t1 = B.inc(mi, "dv")
                            B.wait(v, t1)
                            B.inc(v.tensor_copy(pooled_b[:], pooled[:]), "dv")
                        B.v(pool_v)
                        tok_pool = B.tok("dv")
                        g1toks = {}
                        for oc in range(2):
                            def ldg1(g, oc=oc):
                                if b == 0 and oc == 0:
                                    B.wait(g, B.tok("gp"))
                                else:
                                    B.wait(g, B.tok("mm"))
                                B.inc(g.dma_start(wmb[0][:], d["gw1"][oc]), "wm", 16)
                            B.g(ldg1)
                            tok_g1 = B.tok("wm")
                            def g1mm(t, oc=oc, tok_g1=tok_g1):
                                B.wait(t, tok_g1)
                                B.wait(t, tok_pool)
                                for ci in range(NC):
                                    mi = t.matmul(ps_g[:, oc:oc + 1], wmb[0][:, ci, :],
                                                  pooled_b[:, ci:ci + 1],
                                                  start=(ci == 0), stop=(ci == NC - 1))
                                B.inc(mi, "mm")
                            B.t(g1mm)
                            g1toks[oc] = B.tok("mm")
                        def g1ev2(sc):
                            B.wait(sc, g1toks[1])
                            B.wait(sc, tok_ini)
                            sc.activation(h1_b[:, 0:1], ps_g[:, 0:1], AF.Relu,
                                          bias=gb1_s[:, 0:1], scale=1.0 / 2048.0)
                            B.inc(sc.activation(h1_b[:, 1:2], ps_g[:, 1:2], AF.Relu,
                                                bias=gb1_s[:, 1:2], scale=1.0 / 2048.0), "ac")
                        B.s(g1ev2)
                        tok_h1 = B.tok("ac")
                        def ldg2(g):
                            B.inc(g.dma_start(gw2_s[:], d["gw2t"].rearrange("o p k -> p o k")),
                                  "g2", 16)
                        B.g(ldg2)
                        tok_g2 = B.tok("g2")
                        def lgmm(t):
                            B.wait(t, tok_h1)
                            B.wait(t, tok_g2)
                            for oc in range(2):
                                mi = t.matmul(ps_lg[:], h1_b[:, oc:oc + 1], gw2_s[:, oc, :],
                                              start=(oc == 0), stop=(oc == 1))
                            B.inc(mi, "mm")
                        B.t(lgmm)
                        tok_lg = B.tok("mm")
                        def top2(v):
                            B.wait(v, tok_lg)
                            t = B.inc(v.tensor_add(lg[:], ps_lg[0:1, :], gb2_s[0:1, :]), "dv")
                            B.wait(v, t)
                            t = B.inc(v.reduce_max(tp[0:1, 0:1], lg[0:1, :], axis=AX.X), "dv")
                            B.wait(v, t)
                            t = B.inc(v.tensor_scalar(ge8[0:1, :], lg[0:1, :], tp[0:1, 0:1],
                                                      None, ALU.is_equal), "dv")
                            B.wait(v, t)
                            t = B.inc(v.scalar_tensor_tensor(e8[0:1, :], ge8[0:1, :], -1e30,
                                                             lg[0:1, :], ALU.mult, ALU.add), "dv")
                            B.wait(v, t)
                            t = B.inc(v.reduce_max(tp[0:1, 1:2], e8[0:1, :], axis=AX.X), "dv")
                            B.wait(v, t)
                            t = B.inc(v.tensor_scalar(ge8[0:1, :], lg[0:1, :], tp[0:1, 1:2],
                                                      None, ALU.is_ge), "dv")
                            B.wait(v, t)
                            B.inc(v.tensor_scalar(tp[0:1, 2:3], tp[0:1, 0:1], -1.0,
                                                  None, ALU.mult), "dv")
                        B.v(top2)
                        tok_t2 = B.tok("dv")
                        def expx(sc):
                            B.wait(sc, tok_t2)
                            B.inc(sc.activation(e8[0:1, :], lg[0:1, :], AF.Exp,
                                                bias=tp[0:1, 2:3], scale=1.0), "ac")
                        B.s(expx)
                        tok_e8 = B.tok("ac")
                        def gates(v):
                            B.wait(v, tok_e8)
                            t = B.inc(v.tensor_tensor(w_row[0:1, :], e8[0:1, :], ge8[0:1, :],
                                                      ALU.mult), "dv")
                            B.wait(v, t)
                            t = B.inc(v.reduce_sum(tp[0:1, 4:5], w_row[0:1, :], axis=AX.X), "dv")
                            B.wait(v, t)
                            t = B.inc(v.reciprocal(tp[0:1, 6:7], tp[0:1, 4:5]), "dv")
                            B.wait(v, t)
                            B.inc(v.tensor_scalar(w_row[0:1, :], w_row[0:1, :], tp[0:1, 6:7],
                                                  None, ALU.mult), "dv")
                        B.v(gates)
                        tok_w = B.tok("dv")
                        if DBG_L0F and b == 0:
                            def dbgw(y):
                                B.wait(y, tok_w)
                                B.inc(y.dma_start(d["dbg_w"][:], w_row[0:1, :]), "st", 16)
                            B.y(dbgw)
                            def dbgww(v):
                                B.wait(v, B.tok("st"))
                            B.v(dbgww)
                        def wermm(t):
                            B.wait(t, tok_w)
                            B.inc(t.matmul(ps_w[:], w_row[0:1, :], C.ones_row[0:1, 0:1],
                                           start=True, stop=True), "mm")
                        B.t(wermm)
                        tok_pw = B.tok("mm")
                        def w8ev(v):
                            B.wait(v, tok_pw)
                            B.inc(v.tensor_copy(w8[:], ps_w[:]), "dv")
                        B.v(w8ev)
                        tok_w8 = B.tok("dv")
                        def wermm2(t):
                            B.wait(t, tok_w8)
                            B.inc(t.matmul(ps_wer[:], ind_s[:], w8[:], start=True, stop=True), "mm")
                        B.t(wermm2)
                        tok_wer = B.tok("mm")
                        def sscv(v):
                            B.wait(v, tok_wer)
                            for pc in range(4):
                                mi = v.scalar_tensor_tensor(ssc[:, pc:pc + 1], svec_s[:, pc:pc + 1],
                                                            1.0, ps_wer[:], ALU.mult, ALU.mult)
                            B.inc(mi, "dv")
                        B.v(sscv)
                    tok_gate = B.tok("dv")

                    B.block()
                    with ExitStack() as Sp:
                        ps_xv = [_ps(Sp, nc, f"psxv{b}{i}", [128, 512]) for i in range(2)]
                        ps_p = [_ps(Sp, nc, f"pspj{b}{i}", [128, 512]) for i in range(4)]
                        ps_T = [_ps(Sp, nc, f"psTf{b}{i}", [128, 64], BF16) for i in range(2)]

                        def vset(v):
                            B.inc(v.memset(vt65[:], 1.0), "dv")
                        B.v(vset)
                        tok_vset = B.tok("dv")

                        tcnt = [0]
                        cp_toks = {}
                        def v_write(oc, tt, tsl, pp, tok_mm):
                            def ev(v, oc=oc, tt=tt, pp=pp, tok_mm=tok_mm):
                                B.wait(v, tok_mm)
                                B.wait(v, B.tok("mm"))
                                B.inc(v.tensor_scalar(vtile[:], pp[:],
                                                      biases["v"][:, oc:oc + 1], None,
                                                      ALU.add), "dv")
                            B.v(ev)
                            return B.tok("dv")

                        def v_inline(oc, tt, tsl, tok_ev):
                            for j in range(4):
                                for hh in range(2):
                                    ti = tcnt[0]; tcnt[0] += 1
                                    par = ti % 2
                                    def tr(t, j=j, hh=hh, oc=oc, tt=tt, par=par, ti=ti,
                                           tok_ev=tok_ev):
                                        B.wait(t, tok_ev)
                                        if ti == 0:
                                            B.wait(t, tok_vset)
                                        B.wait(t, cp_toks.get(ti - 2))
                                        B.inc(t.transpose(
                                            ps_T[par][:, 0:64],
                                            vtile[hh * 64:(hh + 1) * 64,
                                                  j * 128:(j + 1) * 128],
                                            C.ident[hh * 64:(hh + 1) * 64,
                                                    hh * 64:(hh + 1) * 64]), "mm")
                                    B.t(tr)
                                    tok_tr = B.tok("mm")
                                    def cp(v, j=j, hh=hh, oc=oc, tt=tt, par=par, tok_tr=tok_tr):
                                        B.wait(v, tok_tr)
                                        B.inc(v.tensor_copy(vt65[:, tt * 4 + j, 2 * oc + hh, 0:64],
                                                            ps_T[par][:, 0:64]), "dv")
                                    B.v(cp)
                                    cp_toks[ti] = B.tok("dv")

                        PHASE_MARKS.append((f"f{b}:vproj", B.cnt["mm"]))
                        proj_svdmoe(b, "v", 2, Hb, v_write, wmb, vtb, utb, xvs, ps_xv, ps_p,
                                    inline_v=v_inline)

                        PHASE_MARKS.append((f"f{b}:qk", B.cnt["mm"]))
                        for p, pcol, dst in (("q", 0, q_sb), ("k", 1, k_sb)):
                            def mk_eb(dst, p):
                                def eb(m, dvE0):
                                    def ve(v):
                                        for g in range(32):
                                            oc, tt = divmod(g, 4)
                                            tsl = slice(tt * 512, (tt + 1) * 512)
                                            B.wait(v, m[g])
                                            B.inc(v.tensor_scalar(
                                                dst[:, oc, tsl], ps_p[g % 4][:],
                                                biases[p][:, oc:oc + 1], None, ALU.add), "dv")
                                    B.v(ve)
                                return eb
                            proj_svdmoe_coarse(b, p, pcol, Hb, wmb, vtb, utb, xvs,
                                               ps_xv, ps_p, mk_eb(dst, p))

                if DBG_L0F and b == 0:
                    B.block()
                    def dbgq(y):
                        B.wait_all_dma(y)
                        B.wait(y, B.tok("ac"))
                        B.inc(y.dma_start(d["dbg_q"][:], q_sb[:]), "st", 16)
                    B.y(dbgq)
                    def dbgqw(v):
                        B.wait(v, B.tok("st"))
                    B.v(dbgqw)

                PHASE_MARKS.append((f"f{b}:attn", B.cnt["mm"]))
                B.block()
                with ExitStack() as S3:
                    E = [_sb(S3, nc, f"E{b}{i}", [128, 1024], BF16) for i in range(8)]
                    qpad = [_sb(S3, nc, f"qpad{b}{i}", [128, NTOK], BF16) for i in range(2)]
                    rb = [_sb(S3, nc, f"rb{b}{i}", [64, 512], F32) for i in range(2)]
                    ps_sc = [_ps(S3, nc, f"pssc{b}{i}", [128, 1024]) for i in range(2)]
                    ps_ctx = [_ps(S3, nc, f"psctx{b}{i}", [65, 512]) for i in range(2)]
                    ps_bc = [_ps(S3, nc, f"psbc{b}{i}", [64, 512]) for i in range(2)]

                    gpF0 = B.cnt["gp"]
                    exB = B.tok("ex")[1]
                    geo = []
                    for hh in range(16):
                        for tqt in range(4):
                            geo.append((hh, (hh % 2) * 64, hh // 2,
                                        slice(tqt * 512, (tqt + 1) * 512)))
                    NIT = len(geo)

                    def exv(ii, p):
                        return exB + ii * 8 + p + 1

                    sc_tok = {}
                    ctx_tok = {}
                    mm_ctx15 = {}
                    tok_bc = {}
                    den_tok = {}
                    exp_tok = {}
                    recip_tok = {}
                    evac_tok = {}

                    for sl in range(NIT + 2):
                        def tslot(t, sl=sl):
                            jj = sl - 2
                            if 0 <= jj < NIT:
                                B.wait(t, den_tok[jj])
                                B.wait(t, recip_tok.get(jj - 2))
                                tok_bc[jj] = B.inc(t.matmul(
                                    ps_bc[jj % 2][:], C.ones_row[0:1, 0:64],
                                    C.row_s[0:1, 0:512], start=True, stop=True), "mm")
                            ii = sl
                            io = sl - 1
                            hh = ch = qsl = None
                            if ii < NIT:
                                hh, r0, ch, qsl = geo[ii]
                                if ii % 4 == 0:
                                    B.wait(t, ("gp", gpF0 + 2 + hh + 1))
                            ho = cho = qslo = None
                            if 0 <= io < NIT:
                                ho, r0o, cho, qslo = geo[io]
                            for p in range(8):
                                if 0 <= io < NIT:
                                    B.wait(t, exp_tok[(io, p)])
                                    if p == 0:
                                        B.wait(t, evac_tok.get(io - 2))
                                    for j in (2 * p, 2 * p + 1):
                                        mi = t.matmul(ps_ctx[io % 2][:],
                                                      vt65[:, j, ho, :],
                                                      E[p][:, (j % 2) * 512:
                                                            (j % 2) * 512 + 512],
                                                      start=(j == 0), stop=(j == 15))
                                    ctx_tok[(io, p)] = B.inc(mi, "mm")
                                if ii < NIT:
                                    if p >= 2:
                                        B.wait(t, ("ex", exv(ii, p - 2)))
                                    elif ii >= 1:
                                        B.wait(t, ("ex", exv(ii - 1, p + 6)))
                                    for k in (2 * p, 2 * p + 1):
                                        mi = t.matmul(ps_sc[p % 2][:, (k % 2) * 512:
                                                                  (k % 2) * 512 + 512],
                                                      k_sb[:, ch, k * 128:(k + 1) * 128],
                                                      qpad[hh % 2][:, qsl],
                                                      start=True, stop=True)
                                    sc_tok[(ii, p)] = B.inc(mi, "mm")
                            if 0 <= io < NIT:
                                mm_ctx15[io] = ctx_tok[(io, 7)]
                        B.t(tslot)

                        def sslot(sc, sl=sl):
                            ii = sl
                            if ii < NIT:
                                for p in range(8):
                                    B.wait(sc, sc_tok[(ii, p)])
                                    B.wait(sc, ctx_tok.get((ii - 1, p)))
                                    exp_tok[(ii, p)] = B.inc(sc.activation(
                                        E[p][:], ps_sc[p % 2][:], AF.Exp,
                                        bias=0.0, scale=0.125), "ex")
                        B.s(sslot)

                        def vslot(v, sl=sl):
                            jj = sl - 2
                            if 0 <= jj < NIT:
                                hh, r0, ch, qsl = geo[jj]
                                B.wait(v, tok_bc[jj])
                                recip_tok[jj] = B.inc(
                                    v.reciprocal(rb[jj % 2][:], ps_bc[jj % 2][:]), "dv")
                                B.wait(v, mm_ctx15[jj])
                                evac_tok[jj] = B.inc(v.tensor_tensor(
                                    ctx[r0:r0 + 64, ch, qsl], ps_ctx[jj % 2][0:64, :],
                                    rb[jj % 2][:], ALU.mult), "dv")
                            jj2 = sl - 1
                            if 0 <= jj2 < NIT:
                                B.wait(v, tok_bc.get(jj2 - 1))
                                B.wait(v, mm_ctx15[jj2])
                                den_tok[jj2] = B.inc(v.tensor_copy(
                                    C.row_s[0:1, 0:512],
                                    ps_ctx[jj2 % 2][64:65, :]), "dv")
                        B.v(vslot)

                    def gstage(g):
                        B.inc(g.memset(qpad[0][64:128, :], 0.0), "gp")
                        B.inc(g.memset(qpad[1][0:64, :], 0.0), "gp")
                        for hh in range(16):
                            r0 = (hh % 2) * 64
                            ch = hh // 2
                            if hh >= 2:
                                B.wait(g, sc_tok[((hh - 2) * 4 + 3, 7)])
                            B.inc(g.tensor_copy(qpad[hh % 2][r0:r0 + 64, :],
                                                q_sb[r0:r0 + 64, ch, :]), "gp")
                    B.g(gstage)

            if DBG_L0F and b == 0:
                B.block()
                def dbgc(y):
                    B.wait(y, B.tok("dv"))
                    B.inc(y.dma_start(d["dbg_ctx"][:], ctx[:]), "st", 16)
                B.y(dbgc)
                def dbgcw(v):
                    B.wait(v, B.tok("st"))
                B.v(dbgcw)

            PHASE_MARKS.append((f"f{b}:oproj", B.cnt["mm"]))
            B.block()
            with ExitStack() as S4:
                W = _sb(S4, nc, f"W{b}", [128, NC, NTOK], F32)
                wmb2 = [_sb(S4, nc, f"wmb2{b}{i}", [128, 8, 128], BF16) for i in range(2)]
                vtb2 = _sb(S4, nc, f"vtb2{b}", [128, 8, 128], BF16)
                utb2 = _sb(S4, nc, f"utb2{b}", [128, 8, 128], BF16)
                xvs2 = _sb(S4, nc, f"xvs2{b}", [128, 4, 512], BF16)
                xres = [_sb(S4, nc, f"xrs{b}{i}", [128, 512], F32) for i in range(2)]
                with ExitStack() as Sp2:
                    ps_xv = [_ps(Sp2, nc, f"psxv2{b}{i}", [128, 512]) for i in range(2)]
                    ps_p = [_ps(Sp2, nc, f"pspo{b}{i}", [128, 512]) for i in range(4)]

                    w_wr = {}
                    def o_eb(m, dvE0, b=b):
                        xr_tok = {}
                        def ldxr(y):
                            for g in range(32):
                                oc, tt = divmod(g, 4)
                                if g >= 2:
                                    B.wait(y, ("dv", dvE0 + g - 1))
                                xr_tok[g] = B.inc(y.dma_start(
                                    xres[g % 2][:],
                                    d["x"][b, oc * 128:(oc + 1) * 128,
                                           tt * 512:(tt + 1) * 512]), f"xr{g % 2}", 16)
                        B.y(ldxr)
                        def ve(v):
                            B.wait(v, st_b)
                            for g in range(32):
                                oc, tt = divmod(g, 4)
                                tsl = slice(tt * 512, (tt + 1) * 512)
                                B.wait(v, m[g])
                                B.wait(v, xr_tok[g])
                                w_wr[(oc, tt)] = B.inc(
                                    v.scalar_tensor_tensor(W[:, oc, tsl], ps_p[g % 4][:],
                                                           biases["o"][:, oc:oc + 1],
                                                           xres[g % 2][:],
                                                           ALU.add, ALU.add), "dv")
                        B.v(ve)

                    proj_svdmoe_coarse(b, "o", 3, ctx, wmb2, vtb2, utb2, xvs2,
                                       ps_xv, ps_p, o_eb)

                if DBG_L0F and b == 0:
                    def dbgW(y):
                        B.wait(y, B.tok("dv"))
                        B.inc(y.dma_start(d["dbg_W"][:], W[:]), "st", 16)
                    B.y(dbgW)
                    def dbgWw(v):
                        B.wait(v, B.tok("st"))
                    B.v(dbgWw)

                B.block()
                PHASE_MARKS.append((f"f{b}:ln", B.cnt["mm"]))
                emit_ln(B, C, nc, W,
                        lambda e, b=b: d["y"][b, e * 128:(e + 1) * 128, :], 4, f"l0f{b}",
                        w_wr=w_wr)


def l0_ffn_inputs(shard_x1T, I):
    ins = {
        "x": shard_x1T,
        "gw1": blk_T(I["L0_ffn_gw1"], 2),
        "gb1": col_bias(I["L0_ffn_gb1"]),
        "gw2t": np.ascontiguousarray(I["L0_ffn_gw2"].T.reshape(2, 128, 8)),
        "gb2": I["L0_ffn_gb2"].reshape(1, 8),
        "svec": np.ascontiguousarray(
            np.stack([I[f"L0_ffn_{p}S"].reshape(128) for p in "qkvo"], axis=1)),
        "ind": np.repeat(np.eye(8, dtype=np.float32), 16, axis=1),
        "ident": np.eye(128, dtype=np.float32),
    }
    for p in "qkvo":
        ins["wm" + p] = blk_T(I[f"L0_ffn_{p}Wm"], 8)
        V = I[f"L0_ffn_{p}V"]
        ins["vt" + p] = np.ascontiguousarray(
            V.transpose(2, 0, 1).reshape(8, 128, 128).transpose(1, 0, 2))
        U = I[f"L0_ffn_{p}U"]
        ins["ut" + p] = np.ascontiguousarray(
            U.transpose(0, 2, 1).reshape(128, 8, 128))
        ins["b" + p] = col_bias(I[f"L0_ffn_{p}b"])
    return ins



def build_l1_attn(shared=None, d_over=None):
    if shared is not None:
        nc, es, B, C = shared
        return _body_l1_attn(nc, es, B, C, d_over)
    nc = bass.Bass("TRN2", target_bir_lowering=False)
    d = {}
    d["x"] = nc.dram_tensor("x", [2, 1024, NTOK], F32, kind="ExternalInput").ap()
    d["wqkv"] = nc.dram_tensor("wqkv", [24, 128, 8, 128], F32, kind="ExternalInput").ap()
    d["bqkv"] = nc.dram_tensor("bqkv", [128, 24], F32, kind="ExternalInput").ap()
    d["wout"] = nc.dram_tensor("wout", [8, 128, 8, 128], F32, kind="ExternalInput").ap()
    d["bout"] = nc.dram_tensor("bout", [128, 8], F32, kind="ExternalInput").ap()
    d["ident"] = nc.dram_tensor("ident", [128, 128], F32, kind="ExternalInput").ap()
    d["y"] = nc.dram_tensor("y", [2, 1024, NTOK], F32, kind="ExternalOutput").ap()

    es = ExitStack()
    B = Bld(nc, es)
    C = alloc_common(B, es, nc, d)
    _body_l1_attn(nc, es, B, C, d)
    B.block()
    def fin(y):
        B.wait_all_dma(y)
    B.y(fin)
    B.end()
    es.close()
    return nc


def _body_l1_attn(nc, es, B, C, d):
    PHASE_MARKS.append(("qa:start", B.cnt["mm"]))
    st_stage = B.tok("st")
    bqkv_s = _sb(es, nc, "bqkv_s", [128, 24], F32)
    bout_s = _sb(es, nc, "bout_s", [128, 8], F32)
    B.block()
    def ldb(y):
        B.inc(y.dma_start(bqkv_s[:], d["bqkv"][:]), "ini_y", 16)
        B.inc(y.dma_start(bout_s[:], d["bout"][:]), "ini_y", 16)
    B.y(ldb)
    tok_ini = B.tok("ini_y")

    for b in (0, 1):
        st_b = B.tok("st")
        with ExitStack() as Sb:
            ctx = _sb(Sb, nc, f"ctx{b}", [128, NC, NTOK], BF16)
            with ExitStack() as S1:
                vt = _sb(S1, nc, f"vt{b}", [128, 16, 8, 128], BF16)
                q_sb = _sb(S1, nc, f"q{b}", [128, NC, NTOK], BF16)
                k_sb = _sb(S1, nc, f"k{b}", [128, NC, NTOK], BF16)

                with ExitStack() as S2:
                    Hb = _sb(S2, nc, f"Hb{b}", [128, NC, NTOK], BF16)
                    wb_ = [_sb(S2, nc, f"wb{b}{i}", [128, 8, 128], BF16) for i in range(2)]
                    vtile = _sb(S2, nc, f"vtile{b}", [128, 512], BF16)
                    B.block()
                    mm_now = B.tok("mm")
                    def ldx(g, b=b, mm_now=mm_now, st_now=st_stage):
                        B.wait(g, mm_now)
                        B.wait(g, st_now)
                        B.wait(g, B.tok("ini_y"))
                        B.wait(g, B.tok("ini_g"))
                        B.inc(g.dma_start(Hb[:], d["x"][b].rearrange("(c p) t -> p c t", p=128)),
                              "dx", 16)
                    B.g(ldx)
                    tok_x = B.tok("dx")
                    def _fence_v(v, tok=st_b):
                        B.wait(v, tok)
                    B.v(_fence_v)
                    def _fence_s(sc, tok=st_b):
                        B.wait(sc, tok)
                    B.s(_fence_s)

                    with ExitStack() as Sp:
                        ps_p = [_ps(Sp, nc, f"pspq{b}{i}", [128, 512]) for i in range(2)]
                        ps_T = [_ps(Sp, nc, f"psTq{b}{i}", [128, 128], BF16) for i in range(2)]
                        w_toks = {}
                        ev_toks = {}
                        cp_toks = {}
                        tcnt = [0]
                        def ldw(g, oc=0):
                            B.inc(g.dma_start(wb_[0][:], d["wqkv"][0]), "w0", 16)
                        B.g(ldw)
                        w_toks[0] = B.tok("w0")
                        for oc in range(24):
                            par = oc % 2
                            def ldw2(g, oc=oc, par=par):
                                if oc + 1 < 24:
                                    B.wait(g, ev_toks.get((oc - 1, 3)))
                                    B.inc(g.dma_start(wb_[1 - par][:], d["wqkv"][oc + 1]),
                                          f"w{(oc + 1) % 2}", 16)
                                    w_toks[oc + 1] = B.tok(f"w{(oc + 1) % 2}")
                            B.g(ldw2)
                            for tt in range(4):
                                tsl = slice(tt * 512, (tt + 1) * 512)
                                idx = oc * 4 + tt
                                pp = ps_p[idx % 2]
                                def mms(t, oc=oc, par=par, tsl=tsl, pp=pp, idx=idx):
                                    if idx == 0:
                                        B.wait(t, tok_x)
                                    B.wait(t, w_toks[oc])
                                    B.wait(t, ev_toks.get(divmod(idx - 2, 4)))
                                    for ci in range(NC):
                                        mi = t.matmul(pp[:], wb_[par][:, ci, :], Hb[:, ci, tsl],
                                                      start=(ci == 0), stop=(ci == NC - 1))
                                    B.inc(mi, "mm")
                                B.t(mms)
                                tok_mm = B.tok("mm")
                                if oc < 16:
                                    dst = q_sb if oc < 8 else k_sb
                                    def ev(v, dst=dst, oc=oc, tsl=tsl, pp=pp, tok_mm=tok_mm):
                                        B.wait(v, tok_mm)
                                        if oc == 0:
                                            B.wait(v, tok_ini)
                                        B.inc(v.tensor_scalar(dst[:, oc % 8, tsl], pp[:],
                                                              bqkv_s[:, oc:oc + 1],
                                                              None, ALU.add), "dv")
                                    B.v(ev)
                                    ev_toks[(oc, tt)] = B.tok("dv")
                                else:
                                    dvc = oc - 16
                                    def evv(v, oc=oc, tt=tt, pp=pp, tok_mm=tok_mm):
                                        B.wait(v, tok_mm)
                                        B.wait(v, B.tok("mm"))
                                        B.inc(v.tensor_scalar(vtile[:], pp[:],
                                                              bqkv_s[:, oc:oc + 1], None,
                                                              ALU.add), "dv")
                                    B.v(evv)
                                    tok_ev = B.tok("dv")
                                    ev_toks[(oc, tt)] = tok_ev
                                    for j in range(4):
                                        ti = tcnt[0]; tcnt[0] += 1
                                        tpar = ti % 2
                                        def tr(t, j=j, tt=tt, tpar=tpar, ti=ti, tok_ev=tok_ev):
                                            B.wait(t, tok_ev)
                                            B.wait(t, cp_toks.get(ti - 2))
                                            B.inc(t.transpose(ps_T[tpar][:],
                                                              vtile[:, j * 128:(j + 1) * 128],
                                                              C.ident[:]), "mm")
                                        B.t(tr)
                                        tok_tr = B.tok("mm")
                                        def cp(v, j=j, tt=tt, dvc=dvc, tpar=tpar, tok_tr=tok_tr):
                                            B.wait(v, tok_tr)
                                            B.inc(v.tensor_copy(vt[:, tt * 4 + j, dvc, :],
                                                                ps_T[tpar][:]), "dv")
                                        B.v(cp)
                                        cp_toks[ti] = B.tok("dv")

                PHASE_MARKS.append((f"qa{b}:attn", B.cnt["mm"]))
                B.block()
                with ExitStack() as S3:
                    E_full = _sb(S3, nc, f"Ef{b}", [128, 16, 512], BF16)
                    rb = _sb(S3, nc, f"rb{b}", [128, 512], F32)
                    ps_sc = [_ps(S3, nc, f"pssc{b}{i}", [128, 512]) for i in range(2)]
                    ps_dn = _ps(S3, nc, f"psdn{b}", [1, 512])
                    ps_bc = _ps(S3, nc, f"psbc{b}", [128, 512])
                    ps_ctx = [_ps(S3, nc, f"psctx{b}{i}", [128, 512]) for i in range(2)]

                    for tqt in range(4):
                        qsl = slice(tqt * 512, (tqt + 1) * 512)
                        ac0 = B.cnt["ac"]
                        dv0 = B.cnt["dv"]
                        mm_sc = {}
                        mm_ctx = {}
                        def tbody(t, tqt=tqt, qsl=qsl, ac0=ac0, dv0=dv0):
                            for tkc in range(16):
                                if tkc < 2:
                                    B.wait(t, ("dv", dv0))
                                    B.wait(t, ("ac", ac0))
                                else:
                                    B.wait(t, ("ac", ac0 + tkc - 1))
                                for ci in range(NC):
                                    mi = t.matmul(ps_sc[tkc % 2][:],
                                                  k_sb[:, ci, tkc * 128:(tkc + 1) * 128],
                                                  q_sb[:, ci, qsl],
                                                  start=(ci == 0), stop=(ci == NC - 1))
                                mm_sc[tkc] = B.inc(mi, "mm")
                                if tkc >= 1:
                                    B.wait(t, ("ac", ac0 + tkc))
                                    B.inc(t.matmul(ps_dn[:], C.ones_col[:],
                                                   E_full[:, tkc - 1, :],
                                                   start=(tkc - 1 == 0), stop=False), "mm")
                            B.wait(t, ("ac", ac0 + 16))
                            B.inc(t.matmul(ps_dn[:], C.ones_col[:], E_full[:, 15, :],
                                           start=False, stop=True), "mm")
                            B.wait(t, ("dv", dv0 + 1))
                            B.inc(t.matmul(ps_bc[:], C.ones_row[:], C.row_s[0:1, 0:512],
                                           start=True, stop=True), "mm")
                            for dvc in range(8):
                                if dvc >= 2:
                                    B.wait(t, ("dv", dv0 + 2 + dvc - 1))
                                for tkc in range(16):
                                    mi = t.matmul(ps_ctx[dvc % 2][:], vt[:, tkc, dvc, :],
                                                  E_full[:, tkc, :],
                                                  start=(tkc == 0), stop=(tkc == 15))
                                mm_ctx[dvc] = B.inc(mi, "mm")
                        B.t(tbody)
                        def sbody(sc, ac0=ac0):
                            for tkc in range(16):
                                B.wait(sc, mm_sc[tkc])
                                B.inc(sc.activation(E_full[:, tkc, :], ps_sc[tkc % 2][:],
                                                    AF.Exp, bias=0.0, scale=1.0 / 32.0), "ac")
                        B.s(sbody)
                        def vbody(v, tqt=tqt, qsl=qsl, dv0=dv0):
                            B.wait(v, ("mm", mm_sc[15][1] + 2))
                            t1 = B.inc(v.reciprocal(C.row_s[0:1, 0:512], ps_dn[0:1, :]), "dv")
                            B.wait(v, ("mm", mm_sc[15][1] + 3))
                            t2 = B.inc(v.tensor_copy(rb[:], ps_bc[:]), "dv")
                            B.wait(v, t2)
                            for dvc in range(8):
                                B.wait(v, mm_ctx[dvc])
                                B.inc(v.tensor_tensor(ctx[:, dvc, qsl], ps_ctx[dvc % 2][:],
                                                      rb[:], ALU.mult), "dv")
                        B.v(vbody)

            PHASE_MARKS.append((f"qa{b}:oproj", B.cnt["mm"]))
            B.block()
            with ExitStack() as S4:
                W = _sb(S4, nc, f"W{b}", [128, NC, NTOK], F32)
                wob = [_sb(S4, nc, f"wob{b}{i}", [128, 8, 128], BF16) for i in range(2)]
                xres = [_sb(S4, nc, f"xrs{b}{i}", [128, 512], F32) for i in range(2)]
                with ExitStack() as Sp2:
                    ps_p = [_ps(Sp2, nc, f"pspo{b}{i}", [128, 512]) for i in range(2)]
                    o_toks = {}
                    wo_toks = {}
                    def ldo(g):
                        B.wait(g, B.tok("mm"))
                        B.inc(g.dma_start(wob[0][:], d["wout"][0]), "o0", 16)
                    B.g(ldo)
                    wo_toks[0] = B.tok("o0")
                    for e in range(8):
                        par = e % 2
                        def ldo2(g, e=e, par=par):
                            if e + 1 < 8:
                                B.wait(g, o_toks.get((e - 1, 3)))
                                B.inc(g.dma_start(wob[1 - par][:], d["wout"][e + 1]),
                                      f"o{(e + 1) % 2}", 16)
                                wo_toks[e + 1] = B.tok(f"o{(e + 1) % 2}")
                        B.g(ldo2)
                        for tt in range(4):
                            tsl = slice(tt * 512, (tt + 1) * 512)
                            idx = e * 4 + tt
                            xpar = idx % 2
                            def ldxr(y, e=e, tt=tt, tsl=tsl, xpar=xpar, idx=idx, b=b):
                                B.wait(y, o_toks.get(divmod(idx - 2, 4)))
                                B.inc(y.dma_start(xres[xpar][:],
                                                  d["x"][b, e * 128:(e + 1) * 128, tsl]),
                                      f"xr{xpar}", 16)
                            B.y(ldxr)
                            tok_xr = B.tok(f"xr{xpar}")
                            pp = ps_p[idx % 2]
                            def omm(t, e=e, par=par, tsl=tsl, pp=pp, idx=idx):
                                B.wait(t, wo_toks[e])
                                B.wait(t, o_toks.get(divmod(idx - 2, 4)))
                                if idx <= 1:
                                    B.wait(t, B.tok("dv"))
                                for ci in range(NC):
                                    mi = t.matmul(pp[:], wob[par][:, ci, :], ctx[:, ci, tsl],
                                                  start=(ci == 0), stop=(ci == NC - 1))
                                B.inc(mi, "mm")
                            B.t(omm)
                            tok_om = B.tok("mm")
                            def oev(v, e=e, tt=tt, tsl=tsl, pp=pp, xpar=xpar,
                                    tok_om=tok_om, tok_xr=tok_xr):
                                if e == 0 and tt == 0:
                                    B.wait(v, st_b)
                                B.wait(v, tok_om)
                                B.wait(v, tok_xr)
                                B.inc(v.scalar_tensor_tensor(W[:, e, tsl], pp[:],
                                                             bout_s[:, e:e + 1],
                                                             xres[xpar][:], ALU.add, ALU.add), "dv")
                            B.v(oev)
                            o_toks[(e, tt)] = B.tok("dv")

                B.block()
                emit_ln(B, C, nc, W,
                        lambda e, b=b: d["y"][b, e * 128:(e + 1) * 128, :], 4, f"l1a{b}",
                        w_wr=o_toks)


def l1_attn_inputs(shard_x2T, I):
    return {
        "x": shard_x2T,
        "wqkv": blk_N(I["L1_attn_W_qkv"], 24),
        "bqkv": col_bias(I["L1_attn_b_qkv"]),
        "wout": blk_N(I["L1_attn_W_out"], 8),
        "bout": col_bias(I["L1_attn_b_out"]),
        "ident": np.eye(128, dtype=np.float32),
    }



def build_l1_ffn(shared=None, d_over=None):
    if shared is not None:
        nc, es, B, C = shared
        return _body_l1_ffn(nc, es, B, C, d_over)
    nc = bass.Bass("TRN2", target_bir_lowering=False)
    d = {}
    d["x"] = nc.dram_tensor("x", [2, 1024, NTOK], F32, kind="ExternalInput").ap()
    d["w1t"] = nc.dram_tensor("w1t", [32, 128, 8, 128], F32, kind="ExternalInput").ap()
    d["w2t"] = nc.dram_tensor("w2t", [32, 128, 8, 128], F32, kind="ExternalInput").ap()
    d["h1t"] = nc.dram_tensor("h1t", [32, 128, 32, 128], F32, kind="ExternalInput").ap()
    d["h2t"] = nc.dram_tensor("h2t", [32, 128, 32, 128], F32, kind="ExternalInput").ap()
    d["w3t"] = nc.dram_tensor("w3t", [8, 128, 32, 128], F32, kind="ExternalInput").ap()
    d["s1"] = nc.dram_tensor("s1", [128, 32], F32, kind="ExternalInput").ap()
    d["s2"] = nc.dram_tensor("s2", [128, 32], F32, kind="ExternalInput").ap()
    d["hs1"] = nc.dram_tensor("hs1", [128, 32], F32, kind="ExternalInput").ap()
    d["hs2"] = nc.dram_tensor("hs2", [128, 32], F32, kind="ExternalInput").ap()
    d["ws3"] = nc.dram_tensor("ws3", [128, 8], F32, kind="ExternalInput").ap()
    d["ident"] = nc.dram_tensor("ident", [128, 128], F32, kind="ExternalInput").ap()
    d["y"] = nc.dram_tensor("y", [2, 1024, NTOK], F32, kind="ExternalOutput").ap()

    es = ExitStack()
    B = Bld(nc, es)
    C = alloc_common(B, es, nc, d)
    _body_l1_ffn(nc, es, B, C, d)
    B.block()
    def fin(y):
        B.wait_all_dma(y)
    B.y(fin)
    B.end()
    es.close()
    return nc


def _body_l1_ffn(nc, es, B, C, d):
    PHASE_MARKS.append(("ff:start", B.cnt["mm"]))
    st_stage = B.tok("st")
    scales = {}
    for s in ("s1", "s2", "hs1", "hs2", "ws3"):
        scales[s] = _sb(es, nc, s + "_s", [128, 32 if s != "ws3" else 8], F32)
    B.block()
    def ldb(y):
        for s in ("s1", "s2", "hs1", "hs2", "ws3"):
            B.inc(y.dma_start(scales[s][:], d[s][:]), "ini_y", 16)
    B.y(ldb)
    tok_ini = B.tok("ini_y")

    slabdef = [(0, 0), (0, 1), (1, 0), (1, 1)]
    Hb2 = [_sb(es, nc, f"Hb2_{i}", [128, NC, 1024], BF16) for i in range(2)]
    tok_xs = {}
    tok_hlast = {}
    for si, (b, half) in enumerate(slabdef):
        st_slab = B.tok("st")
        hsl = slice(half * 1024, (half + 1) * 1024)
        with ExitStack() as Ss:
            h_sb = _sb(Ss, nc, f"h{si}", [128, 32, 1024], BF16)
            hwb = [_sb(Ss, nc, f"hwb{si}{i}", [128, 32, 128], BF16) for i in range(2)]
            asc = _sb(Ss, nc, f"asc{si}", [128, 2, 1024], F32)

            with ExitStack() as SH:
                Hb = Hb2[si % 2]
                B.block()
                mm_now = B.tok("mm")
                if si == 0:
                    def ldx0(g, b=b, hsl=hsl, mm_now=mm_now, st_now=st_slab):
                        B.wait(g, mm_now)
                        B.wait(g, st_now)
                        B.wait(g, B.tok("ini_y"))
                        B.wait(g, B.tok("ini_g"))
                        B.inc(g.dma_start(
                            Hb2[0][:],
                            d["x"][b, :, hsl].rearrange("(c p) t -> p c t", p=128)),
                            "dx", 16)
                    B.g(ldx0)
                    tok_xs[0] = B.tok("dx")
                tok_x = tok_xs[si]
                def _fence_v(v, tok=st_slab):
                    B.wait(v, tok)
                B.v(_fence_v)
                def _fence_s(sc, tok=st_slab):
                    B.wait(sc, tok)
                B.s(_fence_s)
                with ExitStack() as SP:
                    ps1 = [_ps(SP, nc, f"ps1{si}{i}", [128, 1024]) for i in range(2)]
                    ps2 = [_ps(SP, nc, f"ps2{si}{i}", [128, 1024]) for i in range(2)]
                    dvH0 = B.cnt["dv"]
                    acH0 = B.cnt["ac"]
                    w_toks = {}
                    def ldw(g, dvH0=dvH0):
                        for oc in range(32):
                            if oc >= 2:
                                B.wait(g, ("dv", dvH0 + oc - 1))
                            B.inc(g.dma_start(hwb[oc % 2][:, 0:8, :], d["w1t"][oc]),
                                  f"w{oc % 2}", 16)
                            w_toks[oc] = B.inc(g.dma_start(hwb[oc % 2][:, 8:16, :],
                                                           d["w2t"][oc]), f"w{oc % 2}", 16)
                    B.g(ldw)
                    nxt = si + 1
                    if nxt < 4:
                        bn, hn = slabdef[nxt]
                        hsln = slice(hn * 1024, (hn + 1) * 1024)
                        def ldxn(g, nxt=nxt, bn=bn, hsln=hsln, st_now=st_slab):
                            B.wait(g, st_now)
                            if nxt >= 2:
                                B.wait(g, tok_hlast[nxt - 2])
                            B.inc(g.dma_start(
                                Hb2[nxt % 2][:],
                                d["x"][bn, :, hsln].rearrange("(c p) t -> p c t", p=128)),
                                "dx", 16)
                        B.g(ldxn)
                        tok_xs[nxt] = B.tok("dx")
                    mmH = {}
                    def th(t, dvH0=dvH0):
                        B.wait(t, tok_x)
                        for oc in range(32):
                            par = oc % 2
                            B.wait(t, w_toks[oc])
                            if oc >= 2:
                                B.wait(t, ("dv", dvH0 + oc - 1))
                            for j in range(2):
                                jsl = slice(j * 512, (j + 1) * 512)
                                for ci in range(NC):
                                    t.matmul(ps1[par][:, jsl], hwb[par][:, ci, :], Hb[:, ci, jsl],
                                             start=(ci == 0), stop=(ci == NC - 1))
                            for j in range(2):
                                jsl = slice(j * 512, (j + 1) * 512)
                                for ci in range(NC):
                                    mi = t.matmul(ps2[par][:, jsl], hwb[par][:, 8 + ci, :],
                                                  Hb[:, ci, jsl],
                                                  start=(ci == 0), stop=(ci == NC - 1))
                            mmH[oc] = B.inc(mi, "mm")
                    B.t(th)
                    tok_hlast[si] = B.tok("mm")
                    def sh(sc, dvH0=dvH0):
                        B.wait(sc, tok_ini)
                        for oc in range(32):
                            B.wait(sc, mmH[oc])
                            if oc >= 2:
                                B.wait(sc, ("dv", dvH0 + oc - 1))
                            B.inc(sc.activation(asc[:, oc % 2, :], ps1[oc % 2][:], AF.Silu,
                                                bias=0.0,
                                                scale=scales["s1"][:, oc:oc + 1]), "ac")
                    B.s(sh)
                    def vh(v, acH0=acH0):
                        for oc in range(32):
                            B.wait(v, mmH[oc])
                            B.wait(v, ("ac", acH0 + oc + 1))
                            B.inc(v.scalar_tensor_tensor(h_sb[:, oc, :], ps2[oc % 2][:],
                                                         scales["s2"][:, oc:oc + 1],
                                                         asc[:, oc % 2, :],
                                                         ALU.mult, ALU.mult), "dv")
                    B.v(vh)
                    tok_h_done = B.tok("dv")

            PHASE_MARKS.append((f"ff{si}:dense", B.cnt["mm"]))
            def dense_pass(src, dst, wname, sname, tok_src_done, dst_sem):
                B.block()
                with ExitStack() as SP:
                    pg = [_ps(SP, nc, f"pg{wname}{si}{i}", [128, 1024]) for i in range(2)]
                    acG0 = B.cnt[dst_sem]
                    w_toks = {}
                    def ldw(g, acG0=acG0):
                        for oc in range(32):
                            if oc < 2:
                                B.wait(g, B.tok("mm"))
                            else:
                                B.wait(g, (dst_sem, acG0 + oc - 1))
                            w_toks[oc] = B.inc(g.dma_start(hwb[oc % 2][:], d[wname][oc]),
                                               f"w{oc % 2}", 16)
                    B.g(ldw)
                    mmG = {}
                    def tg(t, acG0=acG0):
                        B.wait(t, tok_src_done)
                        for oc in range(32):
                            par = oc % 2
                            B.wait(t, w_toks[oc])
                            if oc >= 2:
                                B.wait(t, (dst_sem, acG0 + oc - 1))
                            for j in range(2):
                                jsl = slice(j * 512, (j + 1) * 512)
                                for ci in range(32):
                                    mi = t.matmul(pg[par][:, jsl], hwb[par][:, ci, :],
                                                  src[:, ci, jsl],
                                                  start=(ci == 0), stop=(ci == 31))
                            mmG[oc] = B.inc(mi, "mm")
                    B.t(tg)
                    def sg(sc):
                        for oc in range(32):
                            B.wait(sc, mmG[oc])
                            B.inc(sc.activation(dst[:, oc, :], pg[oc % 2][:], AF.Gelu,
                                                bias=0.0,
                                                scale=scales[sname][:, oc:oc + 1]), dst_sem)
                    B.s(sg)
                    return B.tok(dst_sem)

            with ExitStack() as SG:
                g1_sb = _sb(SG, nc, f"g1_{si}", [128, 32, 1024], BF16)
                tok_g1 = dense_pass(h_sb, g1_sb, "h1t", "hs1", tok_h_done, "ac")
                tok_g2 = dense_pass(g1_sb, h_sb, "h2t", "hs2", tok_g1, "ac")

            PHASE_MARKS.append((f"ff{si}:out", B.cnt["mm"]))
            B.block()
            with ExitStack() as SO:
                W = _sb(SO, nc, f"W{si}", [128, NC, 1024], F32)
                xres = [_sb(SO, nc, f"xrs{si}{i}", [128, 512], F32) for i in range(2)]
                with ExitStack() as SP:
                    pg = [_ps(SP, nc, f"pgo{si}{i}", [128, 1024]) for i in range(2)]
                    dvO0 = B.cnt["dv"]
                    w_toks = {}
                    def ldw(g, dvO0=dvO0):
                        for e in range(8):
                            if e < 2:
                                B.wait(g, B.tok("mm"))
                            else:
                                B.wait(g, ("dv", dvO0 + 2 * (e - 1)))
                            w_toks[e] = B.inc(g.dma_start(hwb[e % 2][:], d["w3t"][e]),
                                              f"w{e % 2}", 16)
                    B.g(ldw)
                    xr_tok = {}
                    def ldxr(y, b=b, hsl=hsl, dvO0=dvO0):
                        for g in range(16):
                            e, j = divmod(g, 2)
                            if g >= 2:
                                B.wait(y, ("dv", dvO0 + g - 1))
                            xr_tok[g] = B.inc(y.dma_start(
                                xres[g % 2][:],
                                d["x"][b, e * 128:(e + 1) * 128,
                                       hsl.start + j * 512:hsl.start + (j + 1) * 512]),
                                f"xr{g % 2}", 16)
                    B.y(ldxr)
                    mmO = {}
                    def to(t, dvO0=dvO0):
                        B.wait(t, tok_g2)
                        for e in range(8):
                            par = e % 2
                            B.wait(t, w_toks[e])
                            if e >= 2:
                                B.wait(t, ("dv", dvO0 + 2 * (e - 1)))
                            for j in range(2):
                                jsl = slice(j * 512, (j + 1) * 512)
                                for ci in range(32):
                                    mi = t.matmul(pg[par][:, jsl], hwb[par][:, ci, :],
                                                  h_sb[:, ci, jsl],
                                                  start=(ci == 0), stop=(ci == 31))
                            mmO[e] = B.inc(mi, "mm")
                    B.t(to)
                    w_wr = {}
                    def vo(v):
                        B.wait(v, st_slab)
                        for e in range(8):
                            B.wait(v, mmO[e])
                            for j in range(2):
                                jsl = slice(j * 512, (j + 1) * 512)
                                B.wait(v, xr_tok[e * 2 + j])
                                w_wr[(e, j)] = B.inc(
                                    v.scalar_tensor_tensor(W[:, e, jsl], pg[e % 2][:, jsl],
                                                           scales["ws3"][:, e:e + 1],
                                                           xres[(e * 2 + j) % 2][:],
                                                           ALU.mult, ALU.add), "dv")
                    B.v(vo)

                B.block()
                emit_ln(B, C, nc, W,
                        lambda e, b=b, hsl=hsl: d["y"][b, e * 128:(e + 1) * 128, hsl],
                        2, f"l1f{si}", w_wr=w_wr)


def l1_ffn_inputs(shard_x3T, I):
    return {
        "x": shard_x3T,
        "w1t": blk_T(I["L1_ffn_w1_w"], 32),
        "w2t": blk_T(I["L1_ffn_w2_w"], 32),
        "h1t": blk_T(I["L1_ffn_h1_w"], 32),
        "h2t": blk_T(I["L1_ffn_h2_w"], 32),
        "w3t": blk_T(I["L1_ffn_w3_w"], 8),
        "s1": col_bias(I["L1_ffn_w1_s"]),
        "s2": col_bias(I["L1_ffn_w2_s"]),
        "hs1": col_bias(I["L1_ffn_h1_s"]),
        "hs2": col_bias(I["L1_ffn_h2_s"]),
        "ws3": col_bias(I["L1_ffn_w3_s"]),
        "ident": np.eye(128, dtype=np.float32),
    }



_STAGE_KEYS = {
    "a": ["wq", "wk", "wv", "wu", "wo", "bq", "bk", "bv", "bu", "bo", "scale"],
    "f": ["gw1", "gb1", "gw2t", "gb2", "svec", "ind"] +
         [p + q for q in "qkvo" for p in ("wm", "vt", "ut", "b")],
    "qa": ["wqkv", "bqkv", "wout", "bout"],
    "ff": ["w1t", "w2t", "h1t", "h2t", "w3t", "s1", "s2", "hs1", "hs2", "ws3"],
}


def build_mega(stage_ins):
    nc = bass.Bass("TRN2", target_bir_lowering=False)

    def decl(pre, keys, ref_ins):
        dd = {}
        for k in keys:
            arr = ref_ins[k]
            dd[k] = nc.dram_tensor(pre + "_" + k, list(arr.shape), F32,
                                   kind="ExternalInput").ap()
        return dd

    a, f, qa, ff = stage_ins
    d1 = decl("a", _STAGE_KEYS["a"], a)
    d2 = decl("f", _STAGE_KEYS["f"], f)
    d3 = decl("qa", _STAGE_KEYS["qa"], qa)
    d4 = decl("ff", _STAGE_KEYS["ff"], ff)
    d1["ident"] = nc.dram_tensor("ident", [128, 128], F32, kind="ExternalInput").ap()
    d1["triu"] = nc.dram_tensor("triu", [128, 128], F32, kind="ExternalInput").ap()
    d1["x"] = nc.dram_tensor("x0", [2, 1024, NTOK], F32, kind="ExternalInput").ap()
    x1 = nc.dram_tensor("x1", [2, 1024, NTOK], F32, kind="Internal").ap()
    x2 = nc.dram_tensor("x2", [2, 1024, NTOK], F32, kind="Internal").ap()
    x3 = nc.dram_tensor("x3", [2, 1024, NTOK], F32, kind="Internal").ap()
    y = nc.dram_tensor("y", [2, 1024, NTOK], F32, kind="ExternalOutput").ap()
    d1["y"] = x1
    d2["x"], d2["y"] = x1, x2
    d3["x"], d3["y"] = x2, x3
    d4["x"], d4["y"] = x3, y

    es = ExitStack()
    B = Bld(nc, es)
    C = alloc_common(B, es, nc, d1, triu=True, scr_b=True)
    for name, body, dd in (("l0_attn", _body_l0_attn, d1), ("l0_ffn", _body_l0_ffn, d2),
                           ("l1_attn", _body_l1_attn, d3), ("l1_ffn", _body_l1_ffn, d4)):
        with ExitStack() as Se, nc.named_scope(name):
            body(nc, Se, B, C, dd)
    B.block()
    def fin(yq):
        B.wait_all_dma(yq)
    B.y(fin)
    B.end()
    es.close()
    return nc


def mega_stage_ins(I):
    a = l0_attn_inputs(None, I)
    f = l0_ffn_inputs(None, I)
    qa = l1_attn_inputs(None, I)
    ff = l1_ffn_inputs(None, I)
    return a, f, qa, ff


def mega_weight_map(stage_ins):
    a, f, qa, ff = stage_ins
    ins = {"ident": a["ident"], "triu": a["triu"]}
    for pre, srcd in (("a", a), ("f", f), ("qa", qa), ("ff", ff)):
        for k in _STAGE_KEYS[pre]:
            ins[pre + "_" + k] = srcd[k]
    return ins



def kernel(**inputs):
    I = inputs
    cores = list(range(N_CORES))
    x = I["L0_attn_x"]
    xT = np.ascontiguousarray(x.transpose(0, 2, 1))

    stage_ins = mega_stage_ins(I)
    nc = build_mega(stage_ins)
    wmap = mega_weight_map(stage_ins)
    in_maps = [{**wmap, "x0": xT[c * 2:(c + 1) * 2]} for c in cores]
    res = run_bass_kernel_spmd(nc, in_maps, core_ids=cores)
    x4T = np.concatenate([r["y"] for r in res.results], axis=0)
    return np.ascontiguousarray(x4T.transpose(0, 2, 1))


# version 24
# speedup vs baseline: 1.0929x; 1.0929x over previous
import numpy as np
import ml_dtypes
from contextlib import ExitStack

import concourse.bass as bass
import concourse.mybir as mybir
from concourse.bass_utils import run_bass_kernel_spmd

F32 = mybir.dt.float32
F16 = mybir.dt.float16
BF16 = mybir.dt.bfloat16
AF = mybir.ActivationFunctionType
ALU = mybir.AluOpType
AX = mybir.AxisListType

N_CORES = 8
NTOK = 2048
NC = 8
PHASE_MARKS = []


class Bld:

    def __init__(self, nc, es):
        self.nc = nc
        self.es = es
        self.blk = None
        self.sems = {}
        self.cnt = {}
        self.dma_names = set()
        for name in ("mm", "dv", "ac", "gp"):
            self._mk(name)

    def _mk(self, name):
        self.sems[name] = self.es.enter_context(self.nc.semaphore(name))
        self.cnt[name] = 0

    def block(self):
        self.end()
        self.blk = self.nc.Block().__enter__()

    def end(self):
        if self.blk is not None:
            self.blk.__exit__(None, None, None)
            self.blk = None

    def t(self, fn): self.blk.tensor(fn)
    def v(self, fn): self.blk.vector(fn)
    def s(self, fn): self.blk.scalar(fn)
    def g(self, fn): self.blk.gpsimd(fn)
    def y(self, fn): self.blk.sync(fn)

    def inc(self, inst, sem, n=1):
        if sem not in self.sems:
            self._mk(sem)
        if n >= 16:
            self.dma_names.add(sem)
        inst.then_inc(self.sems[sem], n)
        self.cnt[sem] += n
        return (sem, self.cnt[sem])

    def tok(self, sem):
        if sem not in self.sems:
            self._mk(sem)
        return (sem, self.cnt[sem])

    def wait(self, eng, tok):
        if tok is not None:
            eng.wait_ge(self.sems[tok[0]], tok[1])

    def wait_all_dma(self, eng):
        for s in sorted(self.dma_names):
            if self.cnt[s]:
                eng.wait_ge(self.sems[s], self.cnt[s])


_NAME_CTR = [0]


def _sb(es, nc, name, shape, dt=BF16):
    _NAME_CTR[0] += 1
    return es.enter_context(nc.sbuf_tensor(f"{name}__{_NAME_CTR[0]}", shape, dt))


def _ps(es, nc, name, shape, dt=F32):
    _NAME_CTR[0] += 1
    return es.enter_context(nc.psum_tensor(f"{name}__{_NAME_CTR[0]}", shape, dt))



def blk_T(W, no):
    o, i = W.shape
    assert o == no * 128
    return np.ascontiguousarray(W.reshape(no, 128, i // 128, 128).transpose(0, 3, 2, 1))


def blk_N(W, no):
    i, o = W.shape
    assert o == no * 128
    return np.ascontiguousarray(W.reshape(i // 128, 128, no, 128).transpose(2, 1, 0, 3))


def col_bias(b):
    return np.ascontiguousarray(b.reshape(-1, 128).T)



class _C:
    pass


def alloc_common(B, es, nc, dram, triu=False, scr_b=False):
    C = _C()
    C.ident = _sb(es, nc, "ident_c", [128, 128], BF16)
    C.ones_col = _sb(es, nc, "ones_col", [128, 1], BF16)
    C.ones_row = _sb(es, nc, "ones_row", [1, 128], F32)
    C.ones128 = _sb(es, nc, "ones128", [128, 128], BF16)
    C.wb = _sb(es, nc, "wb", [128, 1024], BF16)
    C.sq = _sb(es, nc, "sq", [128, 1024], BF16)
    C.scr_a = _sb(es, nc, "scr_a", [128, 1024], F32)
    C.row_s = _sb(es, nc, "row_s", [1, 512], F32)

    B.block()
    def initc(g):
        B.inc(g.dma_start(C.ident[:], dram["ident"][:]), "ini_g", 16)
        g.memset(C.ones_col[:], 1.0)
        B.inc(g.memset(C.ones128[:], 1.0), "gp")
        B.inc(g.memset(C.ones_row[:], 1.0), "gp")
    B.g(initc)
    C.tok_ones128 = B.tok("gp")
    return C


def emit_ln(B, C, nc, W, y_dst_fn, ntt, tag):
    with ExitStack() as S:
        ps_s1 = [_ps(S, nc, f"lns1{tag}{i}", [128, 512]) for i in range(2)]
        ps_s2 = [_ps(S, nc, f"lns2{tag}{i}", [128, 512]) for i in range(2)]
        mt = [_sb(S, nc, f"lnm{tag}{i}", [128, 512], F32) for i in range(2)]
        rt_s = _sb(S, nc, f"lnr{tag}", [128, 512], F32)
        rt = [rt_s, rt_s]
        sv = [_sb(S, nc, f"lnv{tag}{i}", [128, 512], F32) for i in range(2)]

        mm_entry = B.tok("mm")
        cast_tok = {}
        stat_tok = {}
        mm_seen = {}
        var_tok = {}
        sd_tok = {}
        norm_tok = {}

        def emit_cast(tt):
            tsl = slice(tt * 512, (tt + 1) * 512)
            mm_now = B.cnt["mm"]
            def vcast(v, tt=tt, tsl=tsl, mm_now=mm_now):
                for e in range(NC):
                    esl = slice((e % 2) * 512, (e % 2) * 512 + 512)
                    prev = (tt * NC + e) - 2
                    if prev < 0:
                        B.wait(v, mm_entry)
                    elif prev < tt * NC:
                        B.wait(v, mm_seen[divmod(prev, NC)])
                    else:
                        B.wait(v, ("mm", mm_now + (prev - tt * NC) + 1))
                    t1 = B.inc(v.tensor_copy(C.wb[:, esl], W[:, e, tsl]), "dv")
                    B.wait(v, t1)
                    cast_tok[(tt, e)] = B.inc(
                        v.tensor_tensor(C.sq[:, esl], C.wb[:, esl], C.wb[:, esl],
                                        ALU.mult), "dv")
            B.v(vcast)

        def emit_stats(tt):
            par = tt % 2
            def tstat(t, tt=tt, par=par):
                if tt == 0:
                    B.wait(t, C.tok_ones128)
                if tt >= 2:
                    B.wait(t, var_tok[tt - 2])
                for e in range(NC):
                    esl = slice((e % 2) * 512, (e % 2) * 512 + 512)
                    B.wait(t, cast_tok[(tt, e)])
                    t.matmul(ps_s1[par][:], C.ones128[:], C.wb[:, esl],
                             start=(e == 0), stop=(e == NC - 1))
                    mm_seen[(tt, e)] = B.inc(
                        t.matmul(ps_s2[par][:], C.ones128[:], C.sq[:, esl],
                                 start=(e == 0), stop=(e == NC - 1)), "mm")
            B.t(tstat)
            stat_tok[tt] = B.tok("mm")

        def emit_var(tt):
            par = tt % 2
            def vvar(v, tt=tt, par=par):
                B.wait(v, stat_tok[tt])
                t1 = B.inc(v.tensor_scalar(mt[par][:], ps_s1[par][:],
                                           1.0 / 1024.0, None, ALU.mult), "dv")
                B.wait(v, t1)
                t2 = B.inc(v.tensor_tensor(rt[par][:], mt[par][:], mt[par][:],
                                           ALU.mult), "dv")
                B.wait(v, t2)
                t3 = B.inc(v.tensor_scalar(sv[par][:], ps_s2[par][:],
                                           1.0 / 1024.0, 1e-5,
                                           ALU.mult, ALU.add), "dv")
                B.wait(v, t3)
                B.inc(v.tensor_tensor(sv[par][:], sv[par][:], rt[par][:],
                                      ALU.subtract), "dv")
            B.v(vvar)
            var_tok[tt] = B.tok("dv")
            def ssd(sc, tt=tt, par=par):
                B.wait(sc, var_tok[tt])
                B.inc(sc.activation(sv[par][:], sv[par][:], AF.Sqrt,
                                    bias=0.0, scale=1.0), "ac")
            B.s(ssd)
            sd_tok[tt] = B.tok("ac")

        def emit_norm(tt):
            par = tt % 2
            tsl = slice(tt * 512, (tt + 1) * 512)
            def vnorm(v, tt=tt, par=par, tsl=tsl):
                B.wait(v, sd_tok[tt])
                B.inc(v.reciprocal(rt[par][:], sv[par][:]), "dv")
                for e in range(NC):
                    v.tensor_tensor(sv[par][:], W[:, e, tsl], mt[par][:],
                                    ALU.subtract)
                    B.inc(v.tensor_tensor(W[:, e, tsl], sv[par][:], rt[par][:],
                                          ALU.mult), "dv")
            B.v(vnorm)
            norm_tok[tt] = B.tok("dv")
            def store(y, tt=tt, tsl=tsl):
                B.wait(y, norm_tok[tt])
                for e in range(NC):
                    B.inc(y.dma_start(y_dst_fn(e)[:, tsl], W[:, e, tsl]), "st", 16)
            B.y(store)

        for tt in range(ntt):
            emit_cast(tt)
            emit_stats(tt)
            emit_var(tt)
            if tt >= 1:
                emit_norm(tt - 1)
        emit_norm(ntt - 1)
        return B.tok("st")



DBG_L0A = False


def build_l0_attn(shared=None, d_over=None):
    if shared is not None:
        nc, es, B, C = shared
        d = d_over
        return _body_l0_attn(nc, es, B, C, d)
    nc = bass.Bass("TRN2", target_bir_lowering=False)
    d = {}
    d["x"] = nc.dram_tensor("x", [2, 1024, NTOK], F32, kind="ExternalInput").ap()
    for w in ("wq", "wk", "wv", "wu", "wo"):
        d[w] = nc.dram_tensor(w, [8, 128, 8, 128], F32, kind="ExternalInput").ap()
    for b in ("bq", "bk", "bv", "bu", "bo", "scale"):
        d[b] = nc.dram_tensor(b, [128, 8], F32, kind="ExternalInput").ap()
    d["ident"] = nc.dram_tensor("ident", [128, 128], F32, kind="ExternalInput").ap()
    d["triu"] = nc.dram_tensor("triu", [128, 128], F32, kind="ExternalInput").ap()
    d["y"] = nc.dram_tensor("y", [2, 1024, NTOK], F32, kind="ExternalOutput").ap()
    if DBG_L0A:
        d["dbg_q"] = nc.dram_tensor("dbg_q", [128, NTOK], BF16, kind="ExternalOutput").ap()
        d["dbg_k"] = nc.dram_tensor("dbg_k", [128, NTOK], BF16, kind="ExternalOutput").ap()
        d["dbg_v"] = nc.dram_tensor("dbg_v", [128, NTOK], BF16, kind="ExternalOutput").ap()
        d["dbg_O"] = nc.dram_tensor("dbg_O", [128, NC, NTOK], BF16, kind="ExternalOutput").ap()
        d["dbg_W"] = nc.dram_tensor("dbg_W", [128, NC, NTOK], F32, kind="ExternalOutput").ap()
        d["dbg_rr"] = nc.dram_tensor("dbg_rr", [1, NTOK], F32, kind="ExternalOutput").ap()

    es = ExitStack()
    B = Bld(nc, es)
    C = alloc_common(B, es, nc, d, triu=True, scr_b=True)
    _body_l0_attn(nc, es, B, C, d)
    B.block()
    def fin(y):
        B.wait_all_dma(y)
    B.y(fin)
    B.end()
    es.close()
    return nc


def _body_l0_attn(nc, es, B, C, d):
    C.scr_b = _sb(es, nc, "scr_b", [128, 2048], F32)
    C.triu = _sb(es, nc, "triu_s1", [128, 128], F32)
    B.block()
    def ldtriu(y):
        B.inc(y.dma_start(C.triu[:], d["triu"][:]), "ini_y", 16)
    B.y(ldtriu)
    biases = {}
    for b in ("bq", "bk", "bv", "bu", "bo", "scale"):
        biases[b] = _sb(es, nc, b + "_s", [128, 8], F32)
    B.block()
    def ldb(y):
        for b in ("bq", "bk", "bv", "bu", "bo", "scale"):
            B.inc(y.dma_start(biases[b][:], d[b][:]), "ini_y", 16)
    B.y(ldb)

    st_stage = B.tok("st")
    for b in (0, 1):
        PHASE_MARKS.append((f"a{b}:start", B.cnt["mm"]))
        st_b = B.tok("st")
        with ExitStack() as Sb:
            Xb = _sb(Sb, nc, f"Xb{b}", [128, NC, NTOK], BF16)
            O_sb = _sb(Sb, nc, f"O{b}", [128, NC, NTOK], BF16)
            B.block()
            mm_now = B.tok("mm")
            def _fence_v(v, tok=st_b):
                B.wait(v, tok)
            B.v(_fence_v)
            def _fence_s(sc, tok=st_b):
                B.wait(sc, tok)
            B.s(_fence_s)

            with ExitStack() as Sa:
                qh = [_sb(Sa, nc, f"qh{b}{i}", [128, NTOK], BF16) for i in range(2)]
                kh = [_sb(Sa, nc, f"kh{b}{i}", [128, NTOK], BF16) for i in range(2)]
                vh = [_sb(Sa, nc, f"vh{b}{i}", [128, NTOK], BF16) for i in range(2)]
                wqb = [_sb(Sa, nc, f"wqb{b}{i}", [128, 8, 128], BF16) for i in range(2)]
                wkb = [_sb(Sa, nc, f"wkb{b}{i}", [128, 8, 128], BF16) for i in range(2)]
                wvb = [_sb(Sa, nc, f"wvb{b}{i}", [128, 8, 128], BF16) for i in range(2)]
                S_sb = _sb(Sa, nc, f"S{b}", [128, 128], BF16)
                kT_all = [_sb(Sa, nc, f"kTa{b}{i}", [128, 16, 128], BF16) for i in range(2)]
                vT_all = [_sb(Sa, nc, f"vTa{b}{i}", [128, 16, 128], BF16) for i in range(2)]
                AT_all = [_sb(Sa, nc, f"ATa{b}{i}", [128, 16, 128], BF16) for i in range(2)]
                ps_p = [_ps(Sa, nc, f"psp{b}{i}", [128, 512]) for i in range(4)]
                ps_S = _ps(Sa, nc, f"psS{b}", [128, 128])
                ps_O = _ps(Sa, nc, f"psO{b}", [128, 128])
                ps_T = _ps(Sa, nc, f"psT{b}", [128, 128], BF16)
                ps_A = _ps(Sa, nc, f"psA{b}", [128, 128])

                B.block()
                def ldw0(g, b=b, mm_now=mm_now, st_now=st_stage):
                    B.wait(g, mm_now)
                    B.wait(g, st_now)
                    B.wait(g, B.tok("ini_y"))
                    B.wait(g, B.tok("ini_g"))
                    for wbuf, wd in ((wqb, "wq"), (wkb, "wk"), (wvb, "wv")):
                        B.inc(g.dma_start(wbuf[0][:], d[wd][0]), "w0", 16)
                    for tt in range(4):
                        tsl = slice(tt * 512, (tt + 1) * 512)
                        B.inc(g.dma_start(
                            Xb[:, :, tsl],
                            d["x"][b, :, tsl].rearrange("(c p) t -> p c t", p=128)),
                            f"dx{tt}", 16)
                B.g(ldw0)
                wtoks = {0: B.tok("w0")}
                tok_xt = [B.tok(f"dx{tt}") for tt in range(4)]

                prev_tail = {}
                pipe = {}
                for h in range(9):
                    hp = h % 2
                    if h < 8:
                        def ldw(g, h=h, hp=hp):
                            if h + 1 < 8:
                                if h >= 1:
                                    B.wait(g, prev_tail["mmw"])
                                for wbuf, wd in ((wqb, "wq"), (wkb, "wk"), (wvb, "wv")):
                                    B.inc(g.dma_start(wbuf[1 - hp][:], d[wd][h + 1]),
                                          f"w{(h + 1) % 2}", 16)
                                wtoks[h + 1] = B.tok(f"w{(h + 1) % 2}")
                        B.g(ldw)

                        dv0 = B.cnt["dv"]
                        ac0 = B.cnt["ac"]
                        evac_tok = {}
                        for g in range(12):
                            if g < 4:
                                evac_tok[g] = ("dv", dv0 + 6 + 2 * g)
                            elif g < 8:
                                evac_tok[g] = ("dv", dv0 + 18 + 2 * (g - 4))
                            else:
                                evac_tok[g] = ("ac", ac0 + 9 + (g - 8))
                        m = {}
                        def tproj(t, h=h, hp=hp, dv0=dv0, ac0=ac0):
                            B.wait(t, wtoks[h])
                            for g in range(12):
                                pi, tt = g // 4, g % 4
                                tsl = slice(tt * 512, (tt + 1) * 512)
                                if h == 0:
                                    B.wait(t, tok_xt[tt])
                                if g < 4:
                                    if h == 0:
                                        B.wait(t, ("dv", dv0))
                                        B.wait(t, ("ac", ac0))
                                    else:
                                        B.wait(t, pipe[("v_done", h - 1)])
                                else:
                                    B.wait(t, evac_tok[g - 4])
                                wbuf = (wqb, wkb, wvb)[pi]
                                for ci in range(NC):
                                    mi = t.matmul(ps_p[g % 4][:], wbuf[hp][:, ci, :],
                                                  Xb[:, ci, tsl],
                                                  start=(ci == 0), stop=(ci == NC - 1))
                                m[g] = B.inc(mi, "mm")
                        B.t(tproj)
                        prev_tail["mmw"] = B.tok("mm")

                        for pi, (dstl, bcol, gbase) in enumerate(
                                ((qh, "bq", 0), (kh, "bk", 4))):
                            dst = dstl[hp]
                            def vb1(v, dst=dst, bcol=bcol, gbase=gbase, pi=pi,
                                    dv0=dv0, ac0=ac0, h=h):
                                B.wait(v, ("dv", dv0 + 12 * pi))
                                for tt in range(4):
                                    B.wait(v, m[gbase + tt])
                                    if tt >= 2:
                                        B.wait(v, ("ac", ac0 + 4 * pi + tt - 1))
                                    B.inc(v.tensor_scalar(
                                        C.scr_a[:, (tt % 2) * 512:(tt % 2) * 512 + 512],
                                        ps_p[(gbase + tt) % 4][:],
                                        biases[bcol][:, h:h + 1], 0.0, ALU.add, ALU.min), "dv")
                            B.v(vb1)
                            def sb1(sc, gbase=gbase, pi=pi, dv0=dv0, ac0=ac0):
                                mb = dv0 + 12 * pi
                                B.wait(sc, ("dv", dv0 + 12 * pi))
                                for tt in range(4):
                                    B.wait(sc, ("dv", mb + 1 + tt))
                                    B.inc(sc.activation(
                                        C.scr_b[:, tt * 512:tt * 512 + 512],
                                        C.scr_a[:, (tt % 2) * 512:(tt % 2) * 512 + 512],
                                        AF.Exp, bias=0.0, scale=1.0), "ac")
                            B.s(sb1)
                            def vb2(v, dst=dst, bcol=bcol, gbase=gbase, pi=pi,
                                    dv0=dv0, ac0=ac0, h=h):
                                B.wait(v, ("dv", dv0 + 12 * pi + 4))
                                B.wait(v, ("ac", ac0 + 4 * pi + 4))
                                for tt in range(4):
                                    tsl = slice(tt * 512, (tt + 1) * 512)
                                    ssl = slice((tt % 2) * 512, (tt % 2) * 512 + 512)
                                    t1 = B.inc(v.tensor_scalar(C.scr_a[:, ssl],
                                                               ps_p[(gbase + tt) % 4][:],
                                                               biases[bcol][:, h:h + 1],
                                                               0.0, ALU.add, ALU.max), "dv")
                                    B.wait(v, t1)
                                    B.inc(v.scalar_tensor_tensor(
                                        dst[:, tsl],
                                        C.scr_b[:, tt * 512:tt * 512 + 512],
                                        -1.0, C.scr_a[:, ssl], ALU.add, ALU.add), "dv")
                            B.v(vb2)
                        def sbv(sc, h=h, hp=hp, ac0=ac0):
                            for tt in range(4):
                                tsl = slice(tt * 512, (tt + 1) * 512)
                                B.wait(sc, m[8 + tt])
                                B.inc(sc.activation(vh[hp][:, tsl], ps_p[(8 + tt) % 4][:],
                                                    AF.Silu,
                                                    bias=biases["bv"][:, h:h + 1],
                                                    scale=1.0), "ac")
                        B.s(sbv)
                        pipe[("qk_done", h)] = ("dv", dv0 + 24)
                        pipe[("v_done", h)] = ("ac", ac0 + 12)

                    mA, mK, mV, mB = {}, {}, {}, {}
                    hb = h - 1
                    hbp = hb % 2
                    sA = (2 if hb >= 0 else 0) + (2 if h < 8 else 0)
                    O_off = 2
                    kT_off = (2 if hb >= 0 else 0) + 1
                    vT_off = kT_off + 1
                    prev_dv = prev_tail.get("dv_all")
                    prev_ac = prev_tail.get("ac_all")
                    acB0 = B.cnt["ac"]
                    dvA0 = B.cnt["dv"]
                    def tcomb(t, h=h, hp=hp, hb=hb, hbp=hbp, acB0=acB0, dvA0=dvA0,
                              sA=sA, O_off=O_off, kT_off=kT_off, vT_off=vT_off,
                              prev_dv=prev_dv, prev_ac=prev_ac):
                        if h < 8:
                            B.wait(t, pipe[("qk_done", h)])
                            B.wait(t, pipe[("v_done", h)])
                        if hb >= 0:
                            B.wait(t, prev_dv)
                            B.wait(t, prev_ac)
                        for c in range(16):
                            cs = slice(c * 128, (c + 1) * 128)
                            if hb >= 0:
                                if c > 0:
                                    B.wait(t, ("ac", acB0 + sA * (c - 1) + O_off))
                                    t.matmul(ps_O[:], S_sb[:], qh[hbp][:, cs],
                                             start=True, stop=False)
                                t.matmul(ps_O[:], vT_all[hbp][:, c, :], AT_all[hbp][:, c, :],
                                         start=(c == 0), stop=True)
                                mB[c] = B.inc(t.matmul(ps_S[:], kT_all[hbp][:, c, :],
                                                       vT_all[hbp][:, c, :],
                                                       start=(c == 0), stop=(c == 15),
                                                       skip_group_check=True), "mm")
                            if h < 8:
                                if c >= 1:
                                    B.wait(t, ("dv", dvA0 + c))
                                mA[c] = B.inc(t.matmul(ps_A[:], kh[hp][:, cs], qh[hp][:, cs],
                                                       start=True, stop=True), "mm")
                                if c >= 1:
                                    B.wait(t, ("ac", acB0 + sA * (c - 1) + vT_off))
                                mK[c] = B.inc(t.transpose(ps_T[:], kh[hp][:, cs],
                                                          C.ident[:]), "mm")
                                B.wait(t, ("ac", acB0 + sA * c + kT_off))
                                mV[c] = B.inc(t.transpose(ps_T[:], vh[hp][:, cs],
                                                          C.ident[:]), "mm")
                    B.t(tcomb)
                    def vcomb(v, h=h, hb=hb, hp=hp):
                        if h < 8:
                            for c in range(16):
                                B.wait(v, mA[c])
                                B.inc(v.tensor_tensor(AT_all[hp][:, c, :], ps_A[:],
                                                      C.triu[:], ALU.mult), "dv")
                    B.v(vcomb)
                    def scomb(sc, h=h, hb=hb, hp=hp, hbp=hbp):
                        for c in range(16):
                            cs = slice(c * 128, (c + 1) * 128)
                            if hb >= 0:
                                B.wait(sc, mB[c])
                                B.inc(sc.activation(S_sb[:], ps_S[:], AF.Identity,
                                                    bias=0.0, scale=1.0), "ac")
                                B.inc(sc.activation(O_sb[:, hb, cs], ps_O[:], AF.Identity,
                                                    bias=0.0, scale=1.0), "ac")
                            if h < 8:
                                B.wait(sc, mK[c])
                                B.inc(sc.activation(kT_all[hp][:, c, :], ps_T[:],
                                                    AF.Identity, bias=0.0, scale=1.0), "ac")
                                B.wait(sc, mV[c])
                                B.inc(sc.activation(vT_all[hp][:, c, :], ps_T[:],
                                                    AF.Identity, bias=0.0, scale=1.0), "ac")
                    B.s(scomb)
                    prev_tail["dv_all"] = B.tok("dv")
                    prev_tail["ac_all"] = B.tok("ac")

            with ExitStack() as Sf:
                W = _sb(Sf, nc, f"W{b}", [128, NC, NTOK], F32)
                u_t = _sb(Sf, nc, f"ut{b}", [128, NC, 512], BF16)
                of_t = _sb(Sf, nc, f"oft{b}", [128, NC, 512], BF16)
                wub = [_sb(Sf, nc, f"wub{b}{i}", [128, 8, 128], BF16) for i in range(2)]
                wob = [_sb(Sf, nc, f"wob{b}{i}", [128, 8, 128], BF16) for i in range(2)]
                rstd_t = [_sb(Sf, nc, f"rstd{b}{i}", [128, 512], F32) for i in range(2)]
                rstd_s = _sb(Sf, nc, f"rstds{b}", [128, 512], F32)
                xres = [_sb(Sf, nc, f"xres{b}{i}", [128, 512], F32) for i in range(2)]

                PHASE_MARKS.append((f"a{b}:fin", B.cnt["mm"]))
                B.block()
                with ExitStack() as Sp:
                    ps_p2 = [_ps(Sp, nc, f"psp2{b}{i}", [128, 512]) for i in range(2)]
                    ps_bcr = _ps(Sp, nc, f"psbcr{b}", [128, 512])
                    def ldwuo(g):
                        B.inc(g.dma_start(wub[0][:], d["wu"][0]), "u0", 16)
                    B.g(ldwuo)
                    wu_tok0 = B.tok("u0")
                    prev = {"of_end": None, "wmm_end": None, "wev_end": None,
                            "silu_end": None}
                    sq_tok = {}
                    rc_tok = {}
                    for tt in range(4):
                        tsl = slice(tt * 512, (tt + 1) * 512)
                        mm0 = B.cnt["mm"]
                        dv0s = B.cnt["dv"]
                        def vsq(v, tt=tt, tsl=tsl, mm0=mm0):
                            for hh in range(8):
                                esl = slice((hh % 2) * 512, (hh % 2) * 512 + 512)
                                if hh >= 2:
                                    B.wait(v, ("mm", mm0 + hh - 1))
                                B.inc(v.tensor_tensor(C.sq[:, esl], O_sb[:, hh, tsl],
                                                      O_sb[:, hh, tsl], ALU.mult), "dv")
                        B.v(vsq)
                        def tsq(t, tt=tt, mm0=mm0, dv0s=dv0s):
                            if tt == 0:
                                B.wait(t, C.tok_ones128)
                            B.wait(t, sq_tok.get(tt - 1))
                            for hh in range(8):
                                esl = slice((hh % 2) * 512, (hh % 2) * 512 + 512)
                                B.wait(t, ("dv", dv0s + hh + 1))
                                B.inc(t.matmul(ps_bcr[:, 0:512], C.ones128[:], C.sq[:, esl],
                                               start=(hh == 0), stop=(hh == 7)), "mm")
                        B.t(tsq)
                        tok_ss = B.tok("mm")
                        def ssq(sc, tt=tt, tok_ss=tok_ss):
                            B.wait(sc, tok_ss)
                            B.wait(sc, rc_tok.get(tt - 1))
                            B.inc(sc.activation(rstd_s[:], ps_bcr[:, 0:512], AF.Sqrt,
                                                bias=0.0, scale=1.0 / 1024.0), "ac")
                        B.s(ssq)
                        sq_tok[tt] = B.tok("ac")
                        def vrc(v, tt=tt):
                            B.wait(v, sq_tok[tt])
                            t1 = B.inc(v.tensor_scalar(rstd_s[:], rstd_s[:], 1e-8,
                                                       None, ALU.add), "dv")
                            B.wait(v, t1)
                            B.inc(v.reciprocal(rstd_t[tt % 2][:], rstd_s[:]), "dv")
                        B.v(vrc)
                        rc_tok[tt] = B.tok("dv")
                        tok_rstd = rc_tok[tt]

                        acU0 = B.cnt["ac"]
                        u_toks = {}
                        def ldu(g, tt=tt, acU0=acU0, u_toks=u_toks):
                            for h in range(8):
                                if (tt, h) == (0, 0):
                                    u_toks[h] = wu_tok0
                                    continue
                                if h >= 2 or tt > 0:
                                    B.wait(g, ("ac", acU0 + h - 1))
                                u_toks[h] = B.inc(g.dma_start(wub[h % 2][:], d["wu"][h]),
                                                  f"u{h % 2}", 16)
                        B.g(ldu)

                        mmU = {}
                        def tu(t, tt=tt, tsl=tsl, acU0=acU0, u_toks=u_toks,
                               pv=prev["wev_end"]):
                            B.wait(t, pv)
                            for h in range(8):
                                B.wait(t, u_toks[h])
                                if h >= 2:
                                    B.wait(t, ("ac", acU0 + h - 1))
                                for ci in range(NC):
                                    mi = t.matmul(ps_p2[h % 2][:], wub[h % 2][:, ci, :],
                                                  Xb[:, ci, tsl],
                                                  start=(ci == 0), stop=(ci == NC - 1))
                                mmU[h] = B.inc(mi, "mm")
                        B.t(tu)
                        def su(sc, tt=tt, acU0=acU0, pv=prev["of_end"]):
                            B.wait(sc, pv)
                            for h in range(8):
                                B.wait(sc, mmU[h])
                                B.inc(sc.activation(u_t[:, h, :], ps_p2[h % 2][:], AF.Silu,
                                                    bias=biases["bu"][:, h:h + 1],
                                                    scale=1.0), "ac")
                        B.s(su)
                        prev["silu_end"] = B.tok("ac")
                        def vof(v, tt=tt, tsl=tsl, acU0=acU0, tok_rstd=tok_rstd,
                                pv=prev["wmm_end"]):
                            B.wait(v, tok_rstd)
                            B.wait(v, pv)
                            t2 = None
                            for h in range(8):
                                B.wait(v, ("ac", acU0 + h + 1))
                                B.wait(v, t2)
                                t1 = B.inc(v.scalar_tensor_tensor(
                                    C.scr_a[:, 0:512], O_sb[:, h, tsl],
                                    biases["scale"][:, h:h + 1], rstd_t[tt % 2][:],
                                    ALU.mult, ALU.mult), "dv")
                                B.wait(v, t1)
                                t2 = B.inc(v.tensor_tensor(of_t[:, h, :], C.scr_a[:, 0:512],
                                                           u_t[:, h, :], ALU.mult), "dv")
                        B.v(vof)
                        prev["of_end"] = B.tok("dv")

                        o_tok = {}
                        xr_tok = {}
                        dvW0 = B.cnt["dv"]
                        def ldwo(g, tt=tt, dvW0=dvW0, o_tok=o_tok):
                            for e in range(8):
                                if e >= 2:
                                    B.wait(g, ("dv", dvW0 + e - 1))
                                elif tt > 0:
                                    B.wait(g, ("dv", dvW0))
                                o_tok[e] = B.inc(g.dma_start(wob[e % 2][:], d["wo"][e]),
                                                 f"o{e % 2}", 16)
                        B.g(ldwo)
                        def ldxr(y, tt=tt, b=b, dvW0=dvW0, xr_tok=xr_tok):
                            for e in range(8):
                                if e >= 2:
                                    B.wait(y, ("dv", dvW0 + e - 1))
                                elif tt > 0:
                                    B.wait(y, ("dv", dvW0))
                                xr_tok[e] = B.inc(y.dma_start(
                                    xres[e % 2][:],
                                    d["x"][b, e * 128:(e + 1) * 128,
                                           tt * 512:(tt + 1) * 512]), f"xr{e % 2}", 16)
                        B.y(ldxr)

                        mmW = {}
                        def tw(t, tt=tt, dvW0=dvW0, pv=prev["silu_end"]):
                            B.wait(t, prev["of_end"])
                            for e in range(8):
                                B.wait(t, o_tok[e])
                                if e >= 2:
                                    B.wait(t, ("dv", dvW0 + e - 1))
                                for hc in range(NC):
                                    mi = t.matmul(ps_p2[e % 2][:], wob[e % 2][:, hc, :],
                                                  of_t[:, hc, :],
                                                  start=(hc == 0), stop=(hc == NC - 1))
                                mmW[e] = B.inc(mi, "mm")
                        B.t(tw)
                        prev["wmm_end"] = B.tok("mm")
                        def vw(v, tt=tt, tsl=tsl, dvW0=dvW0):
                            if tt == 0:
                                B.wait(v, st_b)
                            for e in range(8):
                                B.wait(v, mmW[e])
                                B.wait(v, xr_tok[e])
                                B.inc(v.scalar_tensor_tensor(W[:, e, tsl], ps_p2[e % 2][:],
                                                             biases["bo"][:, e:e + 1],
                                                             xres[e % 2][:],
                                                             ALU.add, ALU.add), "dv")
                        B.v(vw)
                        prev["wev_end"] = B.tok("dv")

                if DBG_L0A and b == 0:
                    def dbgW(y):
                        B.wait(y, B.tok("dv"))
                        B.inc(y.dma_start(d["dbg_W"][:], W[:]), "st", 16)
                    B.y(dbgW)
                    def dbgWw(v):
                        B.wait(v, B.tok("st"))
                    B.v(dbgWw)

                B.block()
                PHASE_MARKS.append((f"a{b}:ln", B.cnt["mm"]))
                emit_ln(B, C, nc, W,
                        lambda e, b=b: d["y"][b, e * 128:(e + 1) * 128, :], 4, f"l0a{b}")


def l0_attn_inputs(shard_xT, I):
    return {
        "x": shard_xT,
        "wq": blk_T(I["L0_attn_Wq"], 8), "wk": blk_T(I["L0_attn_Wk"], 8),
        "wv": blk_T(I["L0_attn_Wv"], 8), "wu": blk_T(I["L0_attn_Wu"], 8),
        "wo": blk_T(I["L0_attn_Wo"], 8),
        "bq": col_bias(I["L0_attn_bq"]), "bk": col_bias(I["L0_attn_bk"]),
        "bv": col_bias(I["L0_attn_bv"]), "bu": col_bias(I["L0_attn_bu"]),
        "bo": col_bias(I["L0_attn_bo"]), "scale": col_bias(I["L0_attn_scale"]),
        "ident": np.eye(128, dtype=np.float32),
        "triu": np.triu(np.ones((128, 128), np.float32)),
    }



DBG_L0F = False


def build_l0_ffn(shared=None, d_over=None):
    if shared is not None:
        nc, es, B, C = shared
        return _body_l0_ffn(nc, es, B, C, d_over)
    nc = bass.Bass("TRN2", target_bir_lowering=False)
    d = {}
    d["x"] = nc.dram_tensor("x", [2, 1024, NTOK], F32, kind="ExternalInput").ap()
    d["gw1"] = nc.dram_tensor("gw1", [2, 128, 8, 128], F32, kind="ExternalInput").ap()
    d["gb1"] = nc.dram_tensor("gb1", [128, 2], F32, kind="ExternalInput").ap()
    d["gw2t"] = nc.dram_tensor("gw2t", [2, 128, 8], F32, kind="ExternalInput").ap()
    d["gb2"] = nc.dram_tensor("gb2", [1, 8], F32, kind="ExternalInput").ap()
    for p in "qkvo":
        d["wm" + p] = nc.dram_tensor("wm" + p, [8, 128, 8, 128], F32, kind="ExternalInput").ap()
        d["vt" + p] = nc.dram_tensor("vt" + p, [128, 8, 128], F32, kind="ExternalInput").ap()
        d["ut" + p] = nc.dram_tensor("ut" + p, [128, 8, 128], F32, kind="ExternalInput").ap()
        d["b" + p] = nc.dram_tensor("b" + p, [128, 8], F32, kind="ExternalInput").ap()
    d["svec"] = nc.dram_tensor("svec", [128, 4], F32, kind="ExternalInput").ap()
    d["ind"] = nc.dram_tensor("ind", [8, 128], F32, kind="ExternalInput").ap()
    d["ident"] = nc.dram_tensor("ident", [128, 128], F32, kind="ExternalInput").ap()
    d["y"] = nc.dram_tensor("y", [2, 1024, NTOK], F32, kind="ExternalOutput").ap()
    if DBG_L0F:
        d["dbg_w"] = nc.dram_tensor("dbg_w", [1, 8], F32, kind="ExternalOutput").ap()
        d["dbg_q"] = nc.dram_tensor("dbg_q", [128, NC, NTOK], BF16, kind="ExternalOutput").ap()
        d["dbg_ctx"] = nc.dram_tensor("dbg_ctx", [128, NC, NTOK], BF16, kind="ExternalOutput").ap()
        d["dbg_W"] = nc.dram_tensor("dbg_W", [128, NC, NTOK], F32, kind="ExternalOutput").ap()

    es = ExitStack()
    B = Bld(nc, es)
    C = alloc_common(B, es, nc, d)
    _body_l0_ffn(nc, es, B, C, d)
    B.block()
    def fin(y):
        B.wait_all_dma(y)
    B.y(fin)
    B.end()
    es.close()
    return nc


def _body_l0_ffn(nc, es, B, C, d):
    biases = {}
    for p in "qkvo":
        biases[p] = _sb(es, nc, f"b{p}_s", [128, 8], F32)
    gb1_s = _sb(es, nc, "gb1_s", [128, 2], F32)
    gb2_s = _sb(es, nc, "gb2_s", [1, 8], F32)
    svec_s = _sb(es, nc, "svec_s", [128, 4], F32)
    ind_s = _sb(es, nc, "ind_s", [8, 128], F32)
    ssc = _sb(es, nc, "ssc", [128, 4], F32)
    B.block()
    def ldb(y):
        for p in "qkvo":
            B.inc(y.dma_start(biases[p][:], d["b" + p][:]), "ini_y", 16)
        B.inc(y.dma_start(gb1_s[:], d["gb1"][:]), "ini_y", 16)
        B.inc(y.dma_start(gb2_s[:], d["gb2"][:]), "ini_y", 16)
        B.inc(y.dma_start(svec_s[:], d["svec"][:]), "ini_y", 16)
        B.inc(y.dma_start(ind_s[:], d["ind"][:]), "ini_y", 16)
    B.y(ldb)
    tok_ini = B.tok("ini_y")
    st_stage = B.tok("st")

    def proj_svdmoe(b, p, pcol, src_sb, dst_write, wmb2, vtb, utb, xvs, ps_xv, ps_p,
                    inline_v=None, preloaded=None, x_toks=None):
        tok_pdv = B.tok("dv")
        tok_pac = B.tok("ac")
        if preloaded is not None:
            tok_vu = preloaded
        else:
            def ldvu(g):
                B.wait(g, B.tok("mm"))
                B.inc(g.dma_start(vtb[:], d["vt" + p]), "vt", 16)
                B.inc(g.dma_start(utb[:], d["ut" + p]), "ut", 16)
            B.g(ldvu)
            tok_vu = (B.tok("vt"), B.tok("ut"))

        toks_xv = {}
        for tt in range(4):
            tsl = slice(tt * 512, (tt + 1) * 512)
            def xmm(t, tt=tt, tsl=tsl):
                B.wait(t, tok_vu[0])
                if x_toks is not None:
                    B.wait(t, x_toks[tt])
                if tt == 0:
                    B.wait(t, tok_pdv)
                    B.wait(t, tok_pac)
                B.wait(t, toks_xv.get(tt - 1))
                for ci in range(NC):
                    mi = t.matmul(ps_xv[tt % 2][:], vtb[:, ci, :], src_sb[:, ci, tsl],
                                  start=(ci == 0), stop=(ci == NC - 1))
                B.inc(mi, "mm")
            B.t(xmm)
            tok_xmm = B.tok("mm")
            def xev(v, tt=tt, tok_xmm=tok_xmm):
                B.wait(v, tok_xmm)
                B.inc(v.tensor_copy(xvs[:, tt, :], ps_xv[tt % 2][:]), "dv")
            B.v(xev)
            toks_xv[tt] = B.tok("dv")
        def vus(v):
            B.wait(v, tok_vu[1])
            B.wait(v, tok_gate)
            B.inc(v.tensor_scalar(utb[:], utb[:], ssc[:, pcol:pcol + 1],
                                  None, ALU.mult), "dv")
        B.v(vus)
        tok_uts = B.tok("dv")

        ev_toks = {}
        wm_tok = {}
        for oc in range(8):
            def ldwm(g, oc=oc):
                if oc < 2:
                    B.wait(g, B.tok("mm"))
                else:
                    B.wait(g, ev_toks[(oc - 2, 3)])
                wm_tok[oc] = B.inc(g.dma_start(wmb2[oc % 2][:], d["wm" + p][oc]),
                                   f"wm{oc % 2}", 16)
            B.g(ldwm)
            for tt in range(4):
                tsl = slice(tt * 512, (tt + 1) * 512)
                idx = oc * 4 + tt
                pp = ps_p[idx % 4]
                def mms(t, oc=oc, tt=tt, tsl=tsl, pp=pp, idx=idx):
                    B.wait(t, wm_tok[oc])
                    if idx < 4:
                        B.wait(t, tok_pdv)
                        B.wait(t, tok_pac)
                    B.wait(t, ev_toks.get(divmod(idx - 4, 4)))
                    for ci in range(NC):
                        t.matmul(pp[:], wmb2[oc % 2][:, ci, :], src_sb[:, ci, tsl],
                                 start=(ci == 0), stop=False)
                    B.wait(t, tok_uts)
                    B.wait(t, toks_xv[tt])
                    mi = t.matmul(pp[:], utb[:, oc, :], xvs[:, tt, :],
                                  start=False, stop=True)
                    B.inc(mi, "mm")
                B.t(mms)
                tok_mm = B.tok("mm")
                ev_toks[(oc, tt)] = dst_write(oc, tt, tsl, pp, tok_mm)
                if inline_v is not None:
                    inline_v(oc, tt, tsl, ev_toks[(oc, tt)])
        return ev_toks

    def proj_svdmoe_coarse(b, p, pcol, src_sb, wmb2, vtb, utb, xvs, ps_xv, ps_p,
                           evac_body):
        tok_pdv = B.tok("dv")
        tok_pac = B.tok("ac")
        def ldvu(g):
            B.wait(g, B.tok("mm"))
            B.inc(g.dma_start(vtb[:], d["vt" + p]), "vt", 16)
            B.inc(g.dma_start(utb[:], d["ut" + p]), "ut", 16)
        B.g(ldvu)
        tok_vu = (B.tok("vt"), B.tok("ut"))

        dvX0 = B.cnt["dv"]
        m_xv = {}
        def txv(t):
            for tt in range(4):
                B.wait(t, tok_vu[0])
                if tt == 0:
                    B.wait(t, tok_pdv)
                    B.wait(t, tok_pac)
                if tt >= 2:
                    B.wait(t, ("dv", dvX0 + tt - 1))
                for ci in range(NC):
                    mi = t.matmul(ps_xv[tt % 2][:], vtb[:, ci, :],
                                  src_sb[:, ci, slice(tt * 512, (tt + 1) * 512)],
                                  start=(ci == 0), stop=(ci == NC - 1))
                m_xv[tt] = B.inc(mi, "mm")
        B.t(txv)
        def vxv(v):
            for tt in range(4):
                B.wait(v, m_xv[tt])
                B.inc(v.tensor_copy(xvs[:, tt, :], ps_xv[tt % 2][:]), "dv")
            B.wait(v, tok_vu[1])
            B.wait(v, tok_gate)
            B.inc(v.tensor_scalar(utb[:], utb[:], ssc[:, pcol:pcol + 1],
                                  None, ALU.mult), "dv")
        B.v(vxv)
        tok_uts = B.tok("dv")

        dvE0 = B.cnt["dv"]
        w_tok = {}
        def ldwm(g):
            for oc in range(8):
                if oc >= 2:
                    B.wait(g, ("dv", dvE0 + 4 * (oc - 1)))
                w_tok[oc] = B.inc(g.dma_start(wmb2[oc % 2][:], d["wm" + p][oc]),
                                  f"wm{oc % 2}", 16)
        B.g(ldwm)
        m = {}
        def tmain(t):
            B.wait(t, tok_uts)
            for oc in range(8):
                B.wait(t, w_tok[oc])
                for tt in range(4):
                    g = oc * 4 + tt
                    tsl = slice(tt * 512, (tt + 1) * 512)
                    B.wait(t, ("dv", dvX0 + tt + 1))
                    if g >= 4:
                        B.wait(t, ("dv", dvE0 + g - 3))
                    for ci in range(NC):
                        t.matmul(ps_p[g % 4][:], wmb2[oc % 2][:, ci, :], src_sb[:, ci, tsl],
                                 start=(ci == 0), stop=False)
                    mi = t.matmul(ps_p[g % 4][:], utb[:, oc, :], xvs[:, tt, :],
                                  start=False, stop=True)
                    m[g] = B.inc(mi, "mm")
        B.t(tmain)
        evac_body(m, dvE0)
        return ("dv", dvE0 + 32)

    for b in (0, 1):
        st_b = B.tok("st")
        with ExitStack() as Sb:
            ctx = _sb(Sb, nc, f"ctx{b}", [128, NC, NTOK], BF16)
            with ExitStack() as S1:
                vt65 = _sb(S1, nc, f"vt65_{b}", [128, 16, 16, 65], BF16)
                q_sb = _sb(S1, nc, f"q{b}", [128, NC, NTOK], BF16)
                k_sb = _sb(S1, nc, f"k{b}", [128, NC, NTOK], BF16)

                with ExitStack() as S2:
                    Hb = _sb(S2, nc, f"Hb{b}", [128, NC, NTOK], BF16)
                    wmb = [_sb(S2, nc, f"wmb{b}{i}", [128, 8, 128], BF16) for i in range(2)]
                    vtb = _sb(S2, nc, f"vtb{b}", [128, 8, 128], BF16)
                    utb = _sb(S2, nc, f"utb{b}", [128, 8, 128], BF16)
                    xvs = _sb(S2, nc, f"xvs{b}", [128, 4, 512], BF16)
                    vtile = _sb(S2, nc, f"vtile{b}", [128, 512], BF16)

                    B.block()
                    mm_now = B.tok("mm")
                    def ldx(g, b=b, mm_now=mm_now, st_now=st_stage):
                        B.wait(g, mm_now)
                        B.wait(g, st_now)
                        B.wait(g, B.tok("ini_y"))
                        B.wait(g, B.tok("ini_g"))
                        B.inc(g.dma_start(vtb[:], d["vtv"]), "vt", 16)
                        B.inc(g.dma_start(utb[:], d["utv"]), "ut", 16)
                        for tt in range(4):
                            tsl = slice(tt * 512, (tt + 1) * 512)
                            B.inc(g.dma_start(
                                Hb[:, :, tsl],
                                d["x"][b, :, tsl].rearrange("(c p) t -> p c t", p=128)),
                                f"dx{tt}", 16)
                    B.g(ldx)
                    tok_xt = [B.tok(f"dx{tt}") for tt in range(4)]
                    tok_x = tok_xt
                    tok_vu_pre = (B.tok("vt"), B.tok("ut"))
                    def _fence_v(v, tok=st_b):
                        B.wait(v, tok)
                    B.v(_fence_v)
                    def _fence_s(sc, tok=st_b):
                        B.wait(sc, tok)
                    B.s(_fence_s)

                    PHASE_MARKS.append((f"f{b}:gate", B.cnt["mm"]))
                    with ExitStack() as Sg:
                        pooled = _sb(Sg, nc, f"pool{b}", [128, 8], F32)
                        pooled_b = _sb(Sg, nc, f"poolb{b}", [128, 8], BF16)
                        gw2_s = _sb(Sg, nc, f"gw2{b}", [128, 2, 8], BF16)
                        h1_b = _sb(Sg, nc, f"h1{b}", [128, 2], BF16)
                        lg = _sb(Sg, nc, f"lg{b}", [1, 8], F32)
                        e8 = _sb(Sg, nc, f"e8{b}", [1, 8], F32)
                        ge8 = _sb(Sg, nc, f"ge8{b}", [1, 8], F32)
                        w_row = _sb(Sg, nc, f"wrow{b}", [1, 8], F32)
                        w8 = _sb(Sg, nc, f"w8{b}", [8, 1], F32)
                        tp = _sb(Sg, nc, f"tp{b}", [1, 16], F32)
                        ps_g = _ps(Sg, nc, f"psg{b}", [128, 2])
                        ps_lg = _ps(Sg, nc, f"pslg{b}", [1, 8])
                        ps_w = _ps(Sg, nc, f"psw{b}", [8, 1])
                        ps_wer = _ps(Sg, nc, f"pswer{b}", [128, 1])

                        def pool_v(v, b=b):
                            for _tx in tok_xt:
                                B.wait(v, _tx)
                            for c in range(NC):
                                mi = v.reduce_sum(pooled[:, c:c + 1], Hb[:, c, :], axis=AX.X)
                            t1 = B.inc(mi, "dv")
                            B.wait(v, t1)
                            B.inc(v.tensor_copy(pooled_b[:], pooled[:]), "dv")
                        B.v(pool_v)
                        tok_pool = B.tok("dv")
                        g1toks = {}
                        for oc in range(2):
                            def ldg1(g, oc=oc):
                                if b == 0 and oc == 0:
                                    B.wait(g, B.tok("gp"))
                                else:
                                    B.wait(g, B.tok("mm"))
                                B.inc(g.dma_start(wmb[0][:], d["gw1"][oc]), "wm", 16)
                            B.g(ldg1)
                            tok_g1 = B.tok("wm")
                            def g1mm(t, oc=oc, tok_g1=tok_g1):
                                B.wait(t, tok_g1)
                                B.wait(t, tok_pool)
                                for ci in range(NC):
                                    mi = t.matmul(ps_g[:, oc:oc + 1], wmb[0][:, ci, :],
                                                  pooled_b[:, ci:ci + 1],
                                                  start=(ci == 0), stop=(ci == NC - 1))
                                B.inc(mi, "mm")
                            B.t(g1mm)
                            g1toks[oc] = B.tok("mm")
                        def g1ev2(sc):
                            B.wait(sc, g1toks[1])
                            B.wait(sc, tok_ini)
                            sc.activation(h1_b[:, 0:1], ps_g[:, 0:1], AF.Relu,
                                          bias=gb1_s[:, 0:1], scale=1.0 / 2048.0)
                            B.inc(sc.activation(h1_b[:, 1:2], ps_g[:, 1:2], AF.Relu,
                                                bias=gb1_s[:, 1:2], scale=1.0 / 2048.0), "ac")
                        B.s(g1ev2)
                        tok_h1 = B.tok("ac")
                        def ldg2(g):
                            B.inc(g.dma_start(gw2_s[:], d["gw2t"].rearrange("o p k -> p o k")),
                                  "g2", 16)
                        B.g(ldg2)
                        tok_g2 = B.tok("g2")
                        def lgmm(t):
                            B.wait(t, tok_h1)
                            B.wait(t, tok_g2)
                            for oc in range(2):
                                mi = t.matmul(ps_lg[:], h1_b[:, oc:oc + 1], gw2_s[:, oc, :],
                                              start=(oc == 0), stop=(oc == 1))
                            B.inc(mi, "mm")
                        B.t(lgmm)
                        tok_lg = B.tok("mm")
                        def top2(v):
                            B.wait(v, tok_lg)
                            t = B.inc(v.tensor_add(lg[:], ps_lg[0:1, :], gb2_s[0:1, :]), "dv")
                            B.wait(v, t)
                            t = B.inc(v.reduce_max(tp[0:1, 0:1], lg[0:1, :], axis=AX.X), "dv")
                            B.wait(v, t)
                            t = B.inc(v.tensor_scalar(ge8[0:1, :], lg[0:1, :], tp[0:1, 0:1],
                                                      None, ALU.is_equal), "dv")
                            B.wait(v, t)
                            t = B.inc(v.scalar_tensor_tensor(e8[0:1, :], ge8[0:1, :], -1e30,
                                                             lg[0:1, :], ALU.mult, ALU.add), "dv")
                            B.wait(v, t)
                            t = B.inc(v.reduce_max(tp[0:1, 1:2], e8[0:1, :], axis=AX.X), "dv")
                            B.wait(v, t)
                            t = B.inc(v.tensor_scalar(ge8[0:1, :], lg[0:1, :], tp[0:1, 1:2],
                                                      None, ALU.is_ge), "dv")
                            B.wait(v, t)
                            B.inc(v.tensor_scalar(tp[0:1, 2:3], tp[0:1, 0:1], -1.0,
                                                  None, ALU.mult), "dv")
                        B.v(top2)
                        tok_t2 = B.tok("dv")
                        def expx(sc):
                            B.wait(sc, tok_t2)
                            B.inc(sc.activation(e8[0:1, :], lg[0:1, :], AF.Exp,
                                                bias=tp[0:1, 2:3], scale=1.0), "ac")
                        B.s(expx)
                        tok_e8 = B.tok("ac")
                        def gates(v):
                            B.wait(v, tok_e8)
                            t = B.inc(v.tensor_tensor(w_row[0:1, :], e8[0:1, :], ge8[0:1, :],
                                                      ALU.mult), "dv")
                            B.wait(v, t)
                            t = B.inc(v.reduce_sum(tp[0:1, 4:5], w_row[0:1, :], axis=AX.X), "dv")
                            B.wait(v, t)
                            t = B.inc(v.reciprocal(tp[0:1, 6:7], tp[0:1, 4:5]), "dv")
                            B.wait(v, t)
                            B.inc(v.tensor_scalar(w_row[0:1, :], w_row[0:1, :], tp[0:1, 6:7],
                                                  None, ALU.mult), "dv")
                        B.v(gates)
                        tok_w = B.tok("dv")
                        if DBG_L0F and b == 0:
                            def dbgw(y):
                                B.wait(y, tok_w)
                                B.inc(y.dma_start(d["dbg_w"][:], w_row[0:1, :]), "st", 16)
                            B.y(dbgw)
                            def dbgww(v):
                                B.wait(v, B.tok("st"))
                            B.v(dbgww)
                        def wermm(t):
                            B.wait(t, tok_w)
                            B.inc(t.matmul(ps_w[:], w_row[0:1, :], C.ones_row[0:1, 0:1],
                                           start=True, stop=True), "mm")
                        B.t(wermm)
                        tok_pw = B.tok("mm")
                        def w8ev(v):
                            B.wait(v, tok_pw)
                            B.inc(v.tensor_copy(w8[:], ps_w[:]), "dv")
                        B.v(w8ev)
                        tok_w8 = B.tok("dv")
                        def wermm2(t):
                            B.wait(t, tok_w8)
                            B.inc(t.matmul(ps_wer[:], ind_s[:], w8[:], start=True, stop=True), "mm")
                        B.t(wermm2)
                        tok_wer = B.tok("mm")
                        def sscv(v):
                            B.wait(v, tok_wer)
                            for pc in range(4):
                                mi = v.scalar_tensor_tensor(ssc[:, pc:pc + 1], svec_s[:, pc:pc + 1],
                                                            1.0, ps_wer[:], ALU.mult, ALU.mult)
                            B.inc(mi, "dv")
                        B.v(sscv)
                    tok_gate = B.tok("dv")

                    B.block()
                    with ExitStack() as Sp:
                        ps_xv = [_ps(Sp, nc, f"psxv{b}{i}", [128, 512]) for i in range(2)]
                        ps_p = [_ps(Sp, nc, f"pspj{b}{i}", [128, 512]) for i in range(4)]
                        ps_T = [_ps(Sp, nc, f"psTf{b}{i}", [128, 64], BF16) for i in range(2)]

                        def vset(v):
                            B.inc(v.memset(vt65[:], 1.0), "dv")
                        B.v(vset)
                        tok_vset = B.tok("dv")

                        tcnt = [0]
                        cp_toks = {}
                        def v_write(oc, tt, tsl, pp, tok_mm):
                            def ev(v, oc=oc, tt=tt, pp=pp, tok_mm=tok_mm):
                                B.wait(v, tok_mm)
                                B.wait(v, B.tok("mm"))
                                B.inc(v.tensor_scalar(vtile[:], pp[:],
                                                      biases["v"][:, oc:oc + 1], None,
                                                      ALU.add), "dv")
                            B.v(ev)
                            return B.tok("dv")

                        def v_inline(oc, tt, tsl, tok_ev):
                            for j in range(4):
                                for hh in range(2):
                                    ti = tcnt[0]; tcnt[0] += 1
                                    par = ti % 2
                                    def tr(t, j=j, hh=hh, oc=oc, tt=tt, par=par, ti=ti,
                                           tok_ev=tok_ev):
                                        B.wait(t, tok_ev)
                                        if ti == 0:
                                            B.wait(t, tok_vset)
                                        B.wait(t, cp_toks.get(ti - 2))
                                        B.inc(t.transpose(
                                            ps_T[par][:, 0:64],
                                            vtile[hh * 64:(hh + 1) * 64,
                                                  j * 128:(j + 1) * 128],
                                            C.ident[hh * 64:(hh + 1) * 64,
                                                    hh * 64:(hh + 1) * 64]), "mm")
                                    B.t(tr)
                                    tok_tr = B.tok("mm")
                                    def cp(v, j=j, hh=hh, oc=oc, tt=tt, par=par, tok_tr=tok_tr):
                                        B.wait(v, tok_tr)
                                        B.inc(v.tensor_copy(vt65[:, tt * 4 + j, 2 * oc + hh, 0:64],
                                                            ps_T[par][:, 0:64]), "dv")
                                    B.v(cp)
                                    cp_toks[ti] = B.tok("dv")

                        PHASE_MARKS.append((f"f{b}:vproj", B.cnt["mm"]))
                        proj_svdmoe(b, "v", 2, Hb, v_write, wmb, vtb, utb, xvs, ps_xv, ps_p,
                                    inline_v=v_inline, preloaded=tok_vu_pre,
                                    x_toks=tok_xt)

                        PHASE_MARKS.append((f"f{b}:qk", B.cnt["mm"]))
                        for p, pcol, dst in (("q", 0, q_sb), ("k", 1, k_sb)):
                            def mk_eb(dst, p):
                                def eb(m, dvE0):
                                    def ve(v):
                                        for g in range(32):
                                            oc, tt = divmod(g, 4)
                                            tsl = slice(tt * 512, (tt + 1) * 512)
                                            B.wait(v, m[g])
                                            B.inc(v.tensor_scalar(
                                                dst[:, oc, tsl], ps_p[g % 4][:],
                                                biases[p][:, oc:oc + 1], None, ALU.add), "dv")
                                    B.v(ve)
                                return eb
                            proj_svdmoe_coarse(b, p, pcol, Hb, wmb, vtb, utb, xvs,
                                               ps_xv, ps_p, mk_eb(dst, p))

                if DBG_L0F and b == 0:
                    B.block()
                    def dbgq(y):
                        B.wait_all_dma(y)
                        B.wait(y, B.tok("ac"))
                        B.inc(y.dma_start(d["dbg_q"][:], q_sb[:]), "st", 16)
                    B.y(dbgq)
                    def dbgqw(v):
                        B.wait(v, B.tok("st"))
                    B.v(dbgqw)

                PHASE_MARKS.append((f"f{b}:attn", B.cnt["mm"]))
                B.block()
                with ExitStack() as S3:
                    E = [_sb(S3, nc, f"E{b}{i}", [128, 1024], BF16) for i in range(8)]
                    qpad = [_sb(S3, nc, f"qpad{b}{i}", [128, NTOK], BF16) for i in range(2)]
                    rb = [_sb(S3, nc, f"rb{b}{i}", [64, 512], F32) for i in range(2)]
                    ps_sc = [_ps(S3, nc, f"pssc{b}{i}", [128, 1024]) for i in range(2)]
                    ps_ctx = [_ps(S3, nc, f"psctx{b}{i}", [65, 512]) for i in range(2)]
                    ps_bc = [_ps(S3, nc, f"psbc{b}{i}", [64, 512]) for i in range(2)]

                    gpF0 = B.cnt["gp"]
                    exB = B.tok("ex")[1]
                    geo = []
                    for hh in range(16):
                        for tqt in range(4):
                            geo.append((hh, (hh % 2) * 64, hh // 2,
                                        slice(tqt * 512, (tqt + 1) * 512)))
                    NIT = len(geo)

                    def exv(ii, p):
                        return exB + ii * 8 + p + 1

                    sc_tok = {}
                    ctx_tok = {}
                    mm_ctx15 = {}
                    tok_bc = {}
                    den_tok = {}
                    exp_tok = {}
                    recip_tok = {}
                    evac_tok = {}

                    for sl in range(NIT + 2):
                        def tslot(t, sl=sl):
                            jj = sl - 2
                            if 0 <= jj < NIT:
                                B.wait(t, den_tok[jj])
                                B.wait(t, recip_tok.get(jj - 2))
                                tok_bc[jj] = B.inc(t.matmul(
                                    ps_bc[jj % 2][:], C.ones_row[0:1, 0:64],
                                    C.row_s[0:1, 0:512], start=True, stop=True), "mm")
                            ii = sl
                            io = sl - 1
                            hh = ch = qsl = None
                            if ii < NIT:
                                hh, r0, ch, qsl = geo[ii]
                                if ii % 4 == 0:
                                    B.wait(t, ("gp", gpF0 + 2 + hh + 1))
                            ho = cho = qslo = None
                            if 0 <= io < NIT:
                                ho, r0o, cho, qslo = geo[io]
                            for p in range(8):
                                if 0 <= io < NIT:
                                    B.wait(t, exp_tok[(io, p)])
                                    if p == 0:
                                        B.wait(t, evac_tok.get(io - 2))
                                    for j in (2 * p, 2 * p + 1):
                                        mi = t.matmul(ps_ctx[io % 2][:],
                                                      vt65[:, j, ho, :],
                                                      E[p][:, (j % 2) * 512:
                                                            (j % 2) * 512 + 512],
                                                      start=(j == 0), stop=(j == 15))
                                    ctx_tok[(io, p)] = B.inc(mi, "mm")
                                if ii < NIT:
                                    if p >= 2:
                                        B.wait(t, ("ex", exv(ii, p - 2)))
                                    elif ii >= 1:
                                        B.wait(t, ("ex", exv(ii - 1, p + 6)))
                                    for k in (2 * p, 2 * p + 1):
                                        mi = t.matmul(ps_sc[p % 2][:, (k % 2) * 512:
                                                                  (k % 2) * 512 + 512],
                                                      k_sb[:, ch, k * 128:(k + 1) * 128],
                                                      qpad[hh % 2][:, qsl],
                                                      start=True, stop=True)
                                    sc_tok[(ii, p)] = B.inc(mi, "mm")
                            if 0 <= io < NIT:
                                mm_ctx15[io] = ctx_tok[(io, 7)]
                        B.t(tslot)

                        def sslot(sc, sl=sl):
                            ii = sl
                            if ii < NIT:
                                for p in range(8):
                                    B.wait(sc, sc_tok[(ii, p)])
                                    B.wait(sc, ctx_tok.get((ii - 1, p)))
                                    exp_tok[(ii, p)] = B.inc(sc.activation(
                                        E[p][:], ps_sc[p % 2][:], AF.Exp,
                                        bias=0.0, scale=0.125), "ex")
                        B.s(sslot)

                        def vslot(v, sl=sl):
                            jj = sl - 2
                            if 0 <= jj < NIT:
                                hh, r0, ch, qsl = geo[jj]
                                B.wait(v, tok_bc[jj])
                                recip_tok[jj] = B.inc(
                                    v.reciprocal(rb[jj % 2][:], ps_bc[jj % 2][:]), "dv")
                                B.wait(v, mm_ctx15[jj])
                                evac_tok[jj] = B.inc(v.tensor_tensor(
                                    ctx[r0:r0 + 64, ch, qsl], ps_ctx[jj % 2][0:64, :],
                                    rb[jj % 2][:], ALU.mult), "dv")
                            jj2 = sl - 1
                            if 0 <= jj2 < NIT:
                                B.wait(v, tok_bc.get(jj2 - 1))
                                B.wait(v, mm_ctx15[jj2])
                                den_tok[jj2] = B.inc(v.tensor_copy(
                                    C.row_s[0:1, 0:512],
                                    ps_ctx[jj2 % 2][64:65, :]), "dv")
                        B.v(vslot)

                    def gstage(g):
                        B.inc(g.memset(qpad[0][64:128, :], 0.0), "gp")
                        B.inc(g.memset(qpad[1][0:64, :], 0.0), "gp")
                        for hh in range(16):
                            r0 = (hh % 2) * 64
                            ch = hh // 2
                            if hh >= 2:
                                B.wait(g, sc_tok[((hh - 2) * 4 + 3, 7)])
                            B.inc(g.tensor_copy(qpad[hh % 2][r0:r0 + 64, :],
                                                q_sb[r0:r0 + 64, ch, :]), "gp")
                    B.g(gstage)

            if DBG_L0F and b == 0:
                B.block()
                def dbgc(y):
                    B.wait(y, B.tok("dv"))
                    B.inc(y.dma_start(d["dbg_ctx"][:], ctx[:]), "st", 16)
                B.y(dbgc)
                def dbgcw(v):
                    B.wait(v, B.tok("st"))
                B.v(dbgcw)

            PHASE_MARKS.append((f"f{b}:oproj", B.cnt["mm"]))
            B.block()
            with ExitStack() as S4:
                W = _sb(S4, nc, f"W{b}", [128, NC, NTOK], F32)
                wmb2 = [_sb(S4, nc, f"wmb2{b}{i}", [128, 8, 128], BF16) for i in range(2)]
                vtb2 = _sb(S4, nc, f"vtb2{b}", [128, 8, 128], BF16)
                utb2 = _sb(S4, nc, f"utb2{b}", [128, 8, 128], BF16)
                xvs2 = _sb(S4, nc, f"xvs2{b}", [128, 4, 512], BF16)
                xres = [_sb(S4, nc, f"xrs{b}{i}", [128, 512], F32) for i in range(2)]
                with ExitStack() as Sp2:
                    ps_xv = [_ps(Sp2, nc, f"psxv2{b}{i}", [128, 512]) for i in range(2)]
                    ps_p = [_ps(Sp2, nc, f"pspo{b}{i}", [128, 512]) for i in range(4)]

                    def o_eb(m, dvE0, b=b):
                        xr_tok = {}
                        def ldxr(y):
                            for g in range(32):
                                oc, tt = divmod(g, 4)
                                if g >= 2:
                                    B.wait(y, ("dv", dvE0 + g - 1))
                                xr_tok[g] = B.inc(y.dma_start(
                                    xres[g % 2][:],
                                    d["x"][b, oc * 128:(oc + 1) * 128,
                                           tt * 512:(tt + 1) * 512]), f"xr{g % 2}", 16)
                        B.y(ldxr)
                        def ve(v):
                            B.wait(v, st_b)
                            for g in range(32):
                                oc, tt = divmod(g, 4)
                                tsl = slice(tt * 512, (tt + 1) * 512)
                                B.wait(v, m[g])
                                B.wait(v, xr_tok[g])
                                B.inc(v.scalar_tensor_tensor(W[:, oc, tsl], ps_p[g % 4][:],
                                                             biases["o"][:, oc:oc + 1],
                                                             xres[g % 2][:],
                                                             ALU.add, ALU.add), "dv")
                        B.v(ve)

                    proj_svdmoe_coarse(b, "o", 3, ctx, wmb2, vtb2, utb2, xvs2,
                                       ps_xv, ps_p, o_eb)

                if DBG_L0F and b == 0:
                    def dbgW(y):
                        B.wait(y, B.tok("dv"))
                        B.inc(y.dma_start(d["dbg_W"][:], W[:]), "st", 16)
                    B.y(dbgW)
                    def dbgWw(v):
                        B.wait(v, B.tok("st"))
                    B.v(dbgWw)

                B.block()
                PHASE_MARKS.append((f"f{b}:ln", B.cnt["mm"]))
                emit_ln(B, C, nc, W,
                        lambda e, b=b: d["y"][b, e * 128:(e + 1) * 128, :], 4, f"l0f{b}")


def l0_ffn_inputs(shard_x1T, I):
    ins = {
        "x": shard_x1T,
        "gw1": blk_T(I["L0_ffn_gw1"], 2),
        "gb1": col_bias(I["L0_ffn_gb1"]),
        "gw2t": np.ascontiguousarray(I["L0_ffn_gw2"].T.reshape(2, 128, 8)),
        "gb2": I["L0_ffn_gb2"].reshape(1, 8),
        "svec": np.ascontiguousarray(
            np.stack([I[f"L0_ffn_{p}S"].reshape(128) for p in "qkvo"], axis=1)),
        "ind": np.repeat(np.eye(8, dtype=np.float32), 16, axis=1),
        "ident": np.eye(128, dtype=np.float32),
    }
    for p in "qkvo":
        ins["wm" + p] = blk_T(I[f"L0_ffn_{p}Wm"], 8)
        V = I[f"L0_ffn_{p}V"]
        ins["vt" + p] = np.ascontiguousarray(
            V.transpose(2, 0, 1).reshape(8, 128, 128).transpose(1, 0, 2))
        U = I[f"L0_ffn_{p}U"]
        ins["ut" + p] = np.ascontiguousarray(
            U.transpose(0, 2, 1).reshape(128, 8, 128))
        ins["b" + p] = col_bias(I[f"L0_ffn_{p}b"])
    return ins



def build_l1_attn(shared=None, d_over=None):
    if shared is not None:
        nc, es, B, C = shared
        return _body_l1_attn(nc, es, B, C, d_over)
    nc = bass.Bass("TRN2", target_bir_lowering=False)
    d = {}
    d["x"] = nc.dram_tensor("x", [2, 1024, NTOK], F32, kind="ExternalInput").ap()
    d["wqkv"] = nc.dram_tensor("wqkv", [24, 128, 8, 128], F32, kind="ExternalInput").ap()
    d["bqkv"] = nc.dram_tensor("bqkv", [128, 24], F32, kind="ExternalInput").ap()
    d["wout"] = nc.dram_tensor("wout", [8, 128, 8, 128], F32, kind="ExternalInput").ap()
    d["bout"] = nc.dram_tensor("bout", [128, 8], F32, kind="ExternalInput").ap()
    d["ident"] = nc.dram_tensor("ident", [128, 128], F32, kind="ExternalInput").ap()
    d["y"] = nc.dram_tensor("y", [2, 1024, NTOK], F32, kind="ExternalOutput").ap()

    es = ExitStack()
    B = Bld(nc, es)
    C = alloc_common(B, es, nc, d)
    _body_l1_attn(nc, es, B, C, d)
    B.block()
    def fin(y):
        B.wait_all_dma(y)
    B.y(fin)
    B.end()
    es.close()
    return nc


def _body_l1_attn(nc, es, B, C, d):
    PHASE_MARKS.append(("qa:start", B.cnt["mm"]))
    st_stage = B.tok("st")
    bqkv_s = _sb(es, nc, "bqkv_s", [128, 24], F32)
    bout_s = _sb(es, nc, "bout_s", [128, 8], F32)
    B.block()
    def ldb(y):
        B.inc(y.dma_start(bqkv_s[:], d["bqkv"][:]), "ini_y", 16)
        B.inc(y.dma_start(bout_s[:], d["bout"][:]), "ini_y", 16)
    B.y(ldb)
    tok_ini = B.tok("ini_y")

    for b in (0, 1):
        st_b = B.tok("st")
        with ExitStack() as Sb:
            ctx = _sb(Sb, nc, f"ctx{b}", [128, NC, NTOK], BF16)
            with ExitStack() as S1:
                vt = _sb(S1, nc, f"vt{b}", [128, 16, 8, 128], BF16)
                q_sb = _sb(S1, nc, f"q{b}", [128, NC, NTOK], BF16)
                k_sb = _sb(S1, nc, f"k{b}", [128, NC, NTOK], BF16)

                with ExitStack() as S2:
                    Hb = _sb(S2, nc, f"Hb{b}", [128, NC, NTOK], BF16)
                    wb_ = [_sb(S2, nc, f"wb{b}{i}", [128, 8, 128], BF16) for i in range(2)]
                    vtile = _sb(S2, nc, f"vtile{b}", [128, 512], BF16)
                    B.block()
                    mm_now = B.tok("mm")
                    def ldx(g, b=b, mm_now=mm_now, st_now=st_stage):
                        B.wait(g, mm_now)
                        B.wait(g, st_now)
                        B.wait(g, B.tok("ini_y"))
                        B.wait(g, B.tok("ini_g"))
                        B.inc(g.dma_start(wb_[0][:], d["wqkv"][0]), "w0", 16)
                        for tt in range(4):
                            tsl = slice(tt * 512, (tt + 1) * 512)
                            B.inc(g.dma_start(
                                Hb[:, :, tsl],
                                d["x"][b, :, tsl].rearrange("(c p) t -> p c t", p=128)),
                                f"dx{tt}", 16)
                    B.g(ldx)
                    tok_xt = [B.tok(f"dx{tt}") for tt in range(4)]
                    def _fence_v(v, tok=st_b):
                        B.wait(v, tok)
                    B.v(_fence_v)
                    def _fence_s(sc, tok=st_b):
                        B.wait(sc, tok)
                    B.s(_fence_s)

                    with ExitStack() as Sp:
                        ps_p = [_ps(Sp, nc, f"pspq{b}{i}", [128, 512]) for i in range(2)]
                        ps_T = [_ps(Sp, nc, f"psTq{b}{i}", [128, 128], BF16) for i in range(2)]
                        w_toks = {}
                        ev_toks = {}
                        cp_toks = {}
                        tcnt = [0]
                        w_toks[0] = B.tok("w0")
                        for oc in range(24):
                            par = oc % 2
                            def ldw2(g, oc=oc, par=par):
                                if oc + 1 < 24:
                                    B.wait(g, ev_toks.get((oc - 1, 3)))
                                    B.inc(g.dma_start(wb_[1 - par][:], d["wqkv"][oc + 1]),
                                          f"w{(oc + 1) % 2}", 16)
                                    w_toks[oc + 1] = B.tok(f"w{(oc + 1) % 2}")
                            B.g(ldw2)
                            for tt in range(4):
                                tsl = slice(tt * 512, (tt + 1) * 512)
                                idx = oc * 4 + tt
                                pp = ps_p[idx % 2]
                                def mms(t, oc=oc, par=par, tsl=tsl, pp=pp, idx=idx, tt=tt):
                                    if oc == 0:
                                        B.wait(t, tok_xt[tt])
                                    B.wait(t, w_toks[oc])
                                    B.wait(t, ev_toks.get(divmod(idx - 2, 4)))
                                    for ci in range(NC):
                                        mi = t.matmul(pp[:], wb_[par][:, ci, :], Hb[:, ci, tsl],
                                                      start=(ci == 0), stop=(ci == NC - 1))
                                    B.inc(mi, "mm")
                                B.t(mms)
                                tok_mm = B.tok("mm")
                                if oc < 16:
                                    dst = q_sb if oc < 8 else k_sb
                                    def ev(v, dst=dst, oc=oc, tsl=tsl, pp=pp, tok_mm=tok_mm):
                                        B.wait(v, tok_mm)
                                        if oc == 0:
                                            B.wait(v, tok_ini)
                                        B.inc(v.tensor_scalar(dst[:, oc % 8, tsl], pp[:],
                                                              bqkv_s[:, oc:oc + 1],
                                                              None, ALU.add), "dv")
                                    B.v(ev)
                                    ev_toks[(oc, tt)] = B.tok("dv")
                                else:
                                    dvc = oc - 16
                                    def evv(v, oc=oc, tt=tt, pp=pp, tok_mm=tok_mm):
                                        B.wait(v, tok_mm)
                                        B.wait(v, B.tok("mm"))
                                        B.inc(v.tensor_scalar(vtile[:], pp[:],
                                                              bqkv_s[:, oc:oc + 1], None,
                                                              ALU.add), "dv")
                                    B.v(evv)
                                    tok_ev = B.tok("dv")
                                    ev_toks[(oc, tt)] = tok_ev
                                    for j in range(4):
                                        ti = tcnt[0]; tcnt[0] += 1
                                        tpar = ti % 2
                                        def tr(t, j=j, tt=tt, tpar=tpar, ti=ti, tok_ev=tok_ev):
                                            B.wait(t, tok_ev)
                                            B.wait(t, cp_toks.get(ti - 2))
                                            B.inc(t.transpose(ps_T[tpar][:],
                                                              vtile[:, j * 128:(j + 1) * 128],
                                                              C.ident[:]), "mm")
                                        B.t(tr)
                                        tok_tr = B.tok("mm")
                                        def cp(v, j=j, tt=tt, dvc=dvc, tpar=tpar, tok_tr=tok_tr):
                                            B.wait(v, tok_tr)
                                            B.inc(v.tensor_copy(vt[:, tt * 4 + j, dvc, :],
                                                                ps_T[tpar][:]), "dv")
                                        B.v(cp)
                                        cp_toks[ti] = B.tok("dv")

                PHASE_MARKS.append((f"qa{b}:attn", B.cnt["mm"]))
                B.block()
                with ExitStack() as S3:
                    E_full = _sb(S3, nc, f"Ef{b}", [128, 16, 512], BF16)
                    rb = _sb(S3, nc, f"rb{b}", [128, 512], F32)
                    ps_sc = [_ps(S3, nc, f"pssc{b}{i}", [128, 512]) for i in range(2)]
                    ps_dn = _ps(S3, nc, f"psdn{b}", [1, 512])
                    ps_bc = _ps(S3, nc, f"psbc{b}", [128, 512])
                    ps_ctx = [_ps(S3, nc, f"psctx{b}{i}", [128, 512]) for i in range(2)]

                    for tqt in range(4):
                        qsl = slice(tqt * 512, (tqt + 1) * 512)
                        ac0 = B.cnt["ac"]
                        dv0 = B.cnt["dv"]
                        mm_sc = {}
                        mm_ctx = {}
                        def tbody(t, tqt=tqt, qsl=qsl, ac0=ac0, dv0=dv0):
                            for tkc in range(16):
                                if tkc < 2:
                                    B.wait(t, ("dv", dv0))
                                    B.wait(t, ("ac", ac0))
                                else:
                                    B.wait(t, ("ac", ac0 + tkc - 1))
                                for ci in range(NC):
                                    mi = t.matmul(ps_sc[tkc % 2][:],
                                                  k_sb[:, ci, tkc * 128:(tkc + 1) * 128],
                                                  q_sb[:, ci, qsl],
                                                  start=(ci == 0), stop=(ci == NC - 1))
                                mm_sc[tkc] = B.inc(mi, "mm")
                                if tkc >= 1:
                                    B.wait(t, ("ac", ac0 + tkc))
                                    B.inc(t.matmul(ps_dn[:], C.ones_col[:],
                                                   E_full[:, tkc - 1, :],
                                                   start=(tkc - 1 == 0), stop=False), "mm")
                            B.wait(t, ("ac", ac0 + 16))
                            B.inc(t.matmul(ps_dn[:], C.ones_col[:], E_full[:, 15, :],
                                           start=False, stop=True), "mm")
                            B.wait(t, ("dv", dv0 + 1))
                            B.inc(t.matmul(ps_bc[:], C.ones_row[:], C.row_s[0:1, 0:512],
                                           start=True, stop=True), "mm")
                            for dvc in range(8):
                                if dvc >= 2:
                                    B.wait(t, ("dv", dv0 + 2 + dvc - 1))
                                for tkc in range(16):
                                    mi = t.matmul(ps_ctx[dvc % 2][:], vt[:, tkc, dvc, :],
                                                  E_full[:, tkc, :],
                                                  start=(tkc == 0), stop=(tkc == 15))
                                mm_ctx[dvc] = B.inc(mi, "mm")
                        B.t(tbody)
                        def sbody(sc, ac0=ac0):
                            for tkc in range(16):
                                B.wait(sc, mm_sc[tkc])
                                B.inc(sc.activation(E_full[:, tkc, :], ps_sc[tkc % 2][:],
                                                    AF.Exp, bias=0.0, scale=1.0 / 32.0), "ac")
                        B.s(sbody)
                        def vbody(v, tqt=tqt, qsl=qsl, dv0=dv0):
                            B.wait(v, ("mm", mm_sc[15][1] + 2))
                            t1 = B.inc(v.reciprocal(C.row_s[0:1, 0:512], ps_dn[0:1, :]), "dv")
                            B.wait(v, ("mm", mm_sc[15][1] + 3))
                            t2 = B.inc(v.tensor_copy(rb[:], ps_bc[:]), "dv")
                            B.wait(v, t2)
                            for dvc in range(8):
                                B.wait(v, mm_ctx[dvc])
                                B.inc(v.tensor_tensor(ctx[:, dvc, qsl], ps_ctx[dvc % 2][:],
                                                      rb[:], ALU.mult), "dv")
                        B.v(vbody)

            PHASE_MARKS.append((f"qa{b}:oproj", B.cnt["mm"]))
            B.block()
            with ExitStack() as S4:
                W = _sb(S4, nc, f"W{b}", [128, NC, NTOK], F32)
                wob = [_sb(S4, nc, f"wob{b}{i}", [128, 8, 128], BF16) for i in range(2)]
                xres = [_sb(S4, nc, f"xrs{b}{i}", [128, 512], F32) for i in range(2)]
                with ExitStack() as Sp2:
                    ps_p = [_ps(Sp2, nc, f"pspo{b}{i}", [128, 512]) for i in range(2)]
                    o_toks = {}
                    wo_toks = {}
                    def ldo(g):
                        B.wait(g, B.tok("mm"))
                        B.inc(g.dma_start(wob[0][:], d["wout"][0]), "o0", 16)
                    B.g(ldo)
                    wo_toks[0] = B.tok("o0")
                    for e in range(8):
                        par = e % 2
                        def ldo2(g, e=e, par=par):
                            if e + 1 < 8:
                                B.wait(g, o_toks.get((e - 1, 3)))
                                B.inc(g.dma_start(wob[1 - par][:], d["wout"][e + 1]),
                                      f"o{(e + 1) % 2}", 16)
                                wo_toks[e + 1] = B.tok(f"o{(e + 1) % 2}")
                        B.g(ldo2)
                        for tt in range(4):
                            tsl = slice(tt * 512, (tt + 1) * 512)
                            idx = e * 4 + tt
                            xpar = idx % 2
                            def ldxr(y, e=e, tt=tt, tsl=tsl, xpar=xpar, idx=idx, b=b):
                                B.wait(y, o_toks.get(divmod(idx - 2, 4)))
                                B.inc(y.dma_start(xres[xpar][:],
                                                  d["x"][b, e * 128:(e + 1) * 128, tsl]),
                                      f"xr{xpar}", 16)
                            B.y(ldxr)
                            tok_xr = B.tok(f"xr{xpar}")
                            pp = ps_p[idx % 2]
                            def omm(t, e=e, par=par, tsl=tsl, pp=pp, idx=idx):
                                B.wait(t, wo_toks[e])
                                B.wait(t, o_toks.get(divmod(idx - 2, 4)))
                                if idx <= 1:
                                    B.wait(t, B.tok("dv"))
                                for ci in range(NC):
                                    mi = t.matmul(pp[:], wob[par][:, ci, :], ctx[:, ci, tsl],
                                                  start=(ci == 0), stop=(ci == NC - 1))
                                B.inc(mi, "mm")
                            B.t(omm)
                            tok_om = B.tok("mm")
                            def oev(v, e=e, tt=tt, tsl=tsl, pp=pp, xpar=xpar,
                                    tok_om=tok_om, tok_xr=tok_xr):
                                if e == 0 and tt == 0:
                                    B.wait(v, st_b)
                                B.wait(v, tok_om)
                                B.wait(v, tok_xr)
                                B.inc(v.scalar_tensor_tensor(W[:, e, tsl], pp[:],
                                                             bout_s[:, e:e + 1],
                                                             xres[xpar][:], ALU.add, ALU.add), "dv")
                            B.v(oev)
                            o_toks[(e, tt)] = B.tok("dv")

                B.block()
                emit_ln(B, C, nc, W,
                        lambda e, b=b: d["y"][b, e * 128:(e + 1) * 128, :], 4, f"l1a{b}")


def l1_attn_inputs(shard_x2T, I):
    return {
        "x": shard_x2T,
        "wqkv": blk_N(I["L1_attn_W_qkv"], 24),
        "bqkv": col_bias(I["L1_attn_b_qkv"]),
        "wout": blk_N(I["L1_attn_W_out"], 8),
        "bout": col_bias(I["L1_attn_b_out"]),
        "ident": np.eye(128, dtype=np.float32),
    }



def build_l1_ffn(shared=None, d_over=None):
    if shared is not None:
        nc, es, B, C = shared
        return _body_l1_ffn(nc, es, B, C, d_over)
    nc = bass.Bass("TRN2", target_bir_lowering=False)
    d = {}
    d["x"] = nc.dram_tensor("x", [2, 1024, NTOK], F32, kind="ExternalInput").ap()
    d["w1t"] = nc.dram_tensor("w1t", [32, 128, 8, 128], F32, kind="ExternalInput").ap()
    d["w2t"] = nc.dram_tensor("w2t", [32, 128, 8, 128], F32, kind="ExternalInput").ap()
    d["h1t"] = nc.dram_tensor("h1t", [32, 128, 32, 128], F32, kind="ExternalInput").ap()
    d["h2t"] = nc.dram_tensor("h2t", [32, 128, 32, 128], F32, kind="ExternalInput").ap()
    d["w3t"] = nc.dram_tensor("w3t", [8, 128, 32, 128], F32, kind="ExternalInput").ap()
    d["s1"] = nc.dram_tensor("s1", [128, 32], F32, kind="ExternalInput").ap()
    d["s2"] = nc.dram_tensor("s2", [128, 32], F32, kind="ExternalInput").ap()
    d["hs1"] = nc.dram_tensor("hs1", [128, 32], F32, kind="ExternalInput").ap()
    d["hs2"] = nc.dram_tensor("hs2", [128, 32], F32, kind="ExternalInput").ap()
    d["ws3"] = nc.dram_tensor("ws3", [128, 8], F32, kind="ExternalInput").ap()
    d["ident"] = nc.dram_tensor("ident", [128, 128], F32, kind="ExternalInput").ap()
    d["y"] = nc.dram_tensor("y", [2, 1024, NTOK], F32, kind="ExternalOutput").ap()

    es = ExitStack()
    B = Bld(nc, es)
    C = alloc_common(B, es, nc, d)
    _body_l1_ffn(nc, es, B, C, d)
    B.block()
    def fin(y):
        B.wait_all_dma(y)
    B.y(fin)
    B.end()
    es.close()
    return nc


def _body_l1_ffn(nc, es, B, C, d):
    PHASE_MARKS.append(("ff:start", B.cnt["mm"]))
    st_stage = B.tok("st")
    scales = {}
    for s in ("s1", "s2", "hs1", "hs2", "ws3"):
        scales[s] = _sb(es, nc, s + "_s", [128, 32 if s != "ws3" else 8], F32)
    B.block()
    def ldb(y):
        for s in ("s1", "s2", "hs1", "hs2", "ws3"):
            B.inc(y.dma_start(scales[s][:], d[s][:]), "ini_y", 16)
    B.y(ldb)
    tok_ini = B.tok("ini_y")

    slabdef = [(0, 0), (0, 1), (1, 0), (1, 1)]
    Hb2 = [_sb(es, nc, f"Hb2_{i}", [128, NC, 1024], BF16) for i in range(2)]
    tok_xs = {}
    tok_hlast = {}
    for si, (b, half) in enumerate(slabdef):
        st_slab = B.tok("st")
        hsl = slice(half * 1024, (half + 1) * 1024)
        with ExitStack() as Ss:
            h_sb = _sb(Ss, nc, f"h{si}", [128, 32, 1024], BF16)
            hwb = [_sb(Ss, nc, f"hwb{si}{i}", [128, 32, 128], BF16) for i in range(2)]
            asc = _sb(Ss, nc, f"asc{si}", [128, 2, 1024], F32)

            with ExitStack() as SH:
                Hb = Hb2[si % 2]
                B.block()
                mm_now = B.tok("mm")
                if si == 0:
                    def ldx0(g, b=b, hsl=hsl, mm_now=mm_now, st_now=st_slab):
                        B.wait(g, mm_now)
                        B.wait(g, st_now)
                        B.wait(g, B.tok("ini_y"))
                        B.wait(g, B.tok("ini_g"))
                        B.inc(g.dma_start(
                            Hb2[0][:],
                            d["x"][b, :, hsl].rearrange("(c p) t -> p c t", p=128)),
                            "dx", 16)
                    B.g(ldx0)
                    tok_xs[0] = B.tok("dx")
                tok_x = tok_xs[si]
                def _fence_v(v, tok=st_slab):
                    B.wait(v, tok)
                B.v(_fence_v)
                def _fence_s(sc, tok=st_slab):
                    B.wait(sc, tok)
                B.s(_fence_s)
                with ExitStack() as SP:
                    ps1 = [_ps(SP, nc, f"ps1{si}{i}", [128, 1024]) for i in range(2)]
                    ps2 = [_ps(SP, nc, f"ps2{si}{i}", [128, 1024]) for i in range(2)]
                    dvH0 = B.cnt["dv"]
                    acH0 = B.cnt["ac"]
                    w_toks = {}
                    def ldw(g, dvH0=dvH0):
                        for oc in range(32):
                            if oc >= 2:
                                B.wait(g, ("dv", dvH0 + oc - 1))
                            B.inc(g.dma_start(hwb[oc % 2][:, 0:8, :], d["w1t"][oc]),
                                  f"w{oc % 2}", 16)
                            w_toks[oc] = B.inc(g.dma_start(hwb[oc % 2][:, 8:16, :],
                                                           d["w2t"][oc]), f"w{oc % 2}", 16)
                    B.g(ldw)
                    nxt = si + 1
                    if nxt < 4:
                        bn, hn = slabdef[nxt]
                        hsln = slice(hn * 1024, (hn + 1) * 1024)
                        def ldxn(g, nxt=nxt, bn=bn, hsln=hsln, st_now=st_slab):
                            B.wait(g, st_now)
                            if nxt >= 2:
                                B.wait(g, tok_hlast[nxt - 2])
                            B.inc(g.dma_start(
                                Hb2[nxt % 2][:],
                                d["x"][bn, :, hsln].rearrange("(c p) t -> p c t", p=128)),
                                "dx", 16)
                        B.g(ldxn)
                        tok_xs[nxt] = B.tok("dx")
                    mmH = {}
                    def th(t, dvH0=dvH0):
                        B.wait(t, tok_x)
                        for oc in range(32):
                            par = oc % 2
                            B.wait(t, w_toks[oc])
                            if oc >= 2:
                                B.wait(t, ("dv", dvH0 + oc - 1))
                            for j in range(2):
                                jsl = slice(j * 512, (j + 1) * 512)
                                for ci in range(NC):
                                    t.matmul(ps1[par][:, jsl], hwb[par][:, ci, :], Hb[:, ci, jsl],
                                             start=(ci == 0), stop=(ci == NC - 1))
                            for j in range(2):
                                jsl = slice(j * 512, (j + 1) * 512)
                                for ci in range(NC):
                                    mi = t.matmul(ps2[par][:, jsl], hwb[par][:, 8 + ci, :],
                                                  Hb[:, ci, jsl],
                                                  start=(ci == 0), stop=(ci == NC - 1))
                            mmH[oc] = B.inc(mi, "mm")
                    B.t(th)
                    tok_hlast[si] = B.tok("mm")
                    def sh(sc, dvH0=dvH0):
                        B.wait(sc, tok_ini)
                        for oc in range(32):
                            B.wait(sc, mmH[oc])
                            if oc >= 2:
                                B.wait(sc, ("dv", dvH0 + oc - 1))
                            B.inc(sc.activation(asc[:, oc % 2, :], ps1[oc % 2][:], AF.Silu,
                                                bias=0.0,
                                                scale=scales["s1"][:, oc:oc + 1]), "ac")
                    B.s(sh)
                    def vh(v, acH0=acH0):
                        for oc in range(32):
                            B.wait(v, mmH[oc])
                            B.wait(v, ("ac", acH0 + oc + 1))
                            B.inc(v.scalar_tensor_tensor(h_sb[:, oc, :], ps2[oc % 2][:],
                                                         scales["s2"][:, oc:oc + 1],
                                                         asc[:, oc % 2, :],
                                                         ALU.mult, ALU.mult), "dv")
                    B.v(vh)
                    tok_h_done = B.tok("dv")

            PHASE_MARKS.append((f"ff{si}:dense", B.cnt["mm"]))
            def dense_pass(src, dst, wname, sname, tok_src_done, dst_sem):
                B.block()
                with ExitStack() as SP:
                    pg = [_ps(SP, nc, f"pg{wname}{si}{i}", [128, 1024]) for i in range(2)]
                    acG0 = B.cnt[dst_sem]
                    w_toks = {}
                    def ldw(g, acG0=acG0):
                        for oc in range(32):
                            if oc < 2:
                                B.wait(g, B.tok("mm"))
                            else:
                                B.wait(g, (dst_sem, acG0 + oc - 1))
                            w_toks[oc] = B.inc(g.dma_start(hwb[oc % 2][:], d[wname][oc]),
                                               f"w{oc % 2}", 16)
                    B.g(ldw)
                    mmG = {}
                    def tg(t, acG0=acG0):
                        B.wait(t, tok_src_done)
                        for oc in range(32):
                            par = oc % 2
                            B.wait(t, w_toks[oc])
                            if oc >= 2:
                                B.wait(t, (dst_sem, acG0 + oc - 1))
                            for j in range(2):
                                jsl = slice(j * 512, (j + 1) * 512)
                                for ci in range(32):
                                    mi = t.matmul(pg[par][:, jsl], hwb[par][:, ci, :],
                                                  src[:, ci, jsl],
                                                  start=(ci == 0), stop=(ci == 31))
                            mmG[oc] = B.inc(mi, "mm")
                    B.t(tg)
                    def sg(sc):
                        for oc in range(32):
                            B.wait(sc, mmG[oc])
                            B.inc(sc.activation(dst[:, oc, :], pg[oc % 2][:], AF.Gelu,
                                                bias=0.0,
                                                scale=scales[sname][:, oc:oc + 1]), dst_sem)
                    B.s(sg)
                    return B.tok(dst_sem)

            with ExitStack() as SG:
                g1_sb = _sb(SG, nc, f"g1_{si}", [128, 32, 1024], BF16)
                tok_g1 = dense_pass(h_sb, g1_sb, "h1t", "hs1", tok_h_done, "ac")
                tok_g2 = dense_pass(g1_sb, h_sb, "h2t", "hs2", tok_g1, "ac")

            PHASE_MARKS.append((f"ff{si}:out", B.cnt["mm"]))
            B.block()
            with ExitStack() as SO:
                W = _sb(SO, nc, f"W{si}", [128, NC, 1024], F32)
                xres = [_sb(SO, nc, f"xrs{si}{i}", [128, 512], F32) for i in range(2)]
                with ExitStack() as SP:
                    pg = [_ps(SP, nc, f"pgo{si}{i}", [128, 1024]) for i in range(2)]
                    dvO0 = B.cnt["dv"]
                    w_toks = {}
                    def ldw(g, dvO0=dvO0):
                        for e in range(8):
                            if e < 2:
                                B.wait(g, B.tok("mm"))
                            else:
                                B.wait(g, ("dv", dvO0 + 2 * (e - 1)))
                            w_toks[e] = B.inc(g.dma_start(hwb[e % 2][:], d["w3t"][e]),
                                              f"w{e % 2}", 16)
                    B.g(ldw)
                    xr_tok = {}
                    def ldxr(y, b=b, hsl=hsl, dvO0=dvO0):
                        for g in range(16):
                            e, j = divmod(g, 2)
                            if g >= 2:
                                B.wait(y, ("dv", dvO0 + g - 1))
                            xr_tok[g] = B.inc(y.dma_start(
                                xres[g % 2][:],
                                d["x"][b, e * 128:(e + 1) * 128,
                                       hsl.start + j * 512:hsl.start + (j + 1) * 512]),
                                f"xr{g % 2}", 16)
                    B.y(ldxr)
                    mmO = {}
                    def to(t, dvO0=dvO0):
                        B.wait(t, tok_g2)
                        for e in range(8):
                            par = e % 2
                            B.wait(t, w_toks[e])
                            if e >= 2:
                                B.wait(t, ("dv", dvO0 + 2 * (e - 1)))
                            for j in range(2):
                                jsl = slice(j * 512, (j + 1) * 512)
                                for ci in range(32):
                                    mi = t.matmul(pg[par][:, jsl], hwb[par][:, ci, :],
                                                  h_sb[:, ci, jsl],
                                                  start=(ci == 0), stop=(ci == 31))
                            mmO[e] = B.inc(mi, "mm")
                    B.t(to)
                    def vo(v):
                        B.wait(v, st_slab)
                        for e in range(8):
                            B.wait(v, mmO[e])
                            for j in range(2):
                                jsl = slice(j * 512, (j + 1) * 512)
                                B.wait(v, xr_tok[e * 2 + j])
                                B.inc(v.scalar_tensor_tensor(W[:, e, jsl], pg[e % 2][:, jsl],
                                                             scales["ws3"][:, e:e + 1],
                                                             xres[(e * 2 + j) % 2][:],
                                                             ALU.mult, ALU.add), "dv")
                    B.v(vo)

                B.block()
                emit_ln(B, C, nc, W,
                        lambda e, b=b, hsl=hsl: d["y"][b, e * 128:(e + 1) * 128, hsl],
                        2, f"l1f{si}")


def l1_ffn_inputs(shard_x3T, I):
    return {
        "x": shard_x3T,
        "w1t": blk_T(I["L1_ffn_w1_w"], 32),
        "w2t": blk_T(I["L1_ffn_w2_w"], 32),
        "h1t": blk_T(I["L1_ffn_h1_w"], 32),
        "h2t": blk_T(I["L1_ffn_h2_w"], 32),
        "w3t": blk_T(I["L1_ffn_w3_w"], 8),
        "s1": col_bias(I["L1_ffn_w1_s"]),
        "s2": col_bias(I["L1_ffn_w2_s"]),
        "hs1": col_bias(I["L1_ffn_h1_s"]),
        "hs2": col_bias(I["L1_ffn_h2_s"]),
        "ws3": col_bias(I["L1_ffn_w3_s"]),
        "ident": np.eye(128, dtype=np.float32),
    }



_STAGE_KEYS = {
    "a": ["wq", "wk", "wv", "wu", "wo", "bq", "bk", "bv", "bu", "bo", "scale"],
    "f": ["gw1", "gb1", "gw2t", "gb2", "svec", "ind"] +
         [p + q for q in "qkvo" for p in ("wm", "vt", "ut", "b")],
    "qa": ["wqkv", "bqkv", "wout", "bout"],
    "ff": ["w1t", "w2t", "h1t", "h2t", "w3t", "s1", "s2", "hs1", "hs2", "ws3"],
}


def build_mega(stage_ins):
    nc = bass.Bass("TRN2", target_bir_lowering=False)

    def decl(pre, keys, ref_ins):
        dd = {}
        for k in keys:
            arr = ref_ins[k]
            dd[k] = nc.dram_tensor(pre + "_" + k, list(arr.shape), F32,
                                   kind="ExternalInput").ap()
        return dd

    a, f, qa, ff = stage_ins
    d1 = decl("a", _STAGE_KEYS["a"], a)
    d2 = decl("f", _STAGE_KEYS["f"], f)
    d3 = decl("qa", _STAGE_KEYS["qa"], qa)
    d4 = decl("ff", _STAGE_KEYS["ff"], ff)
    d1["ident"] = nc.dram_tensor("ident", [128, 128], F32, kind="ExternalInput").ap()
    d1["triu"] = nc.dram_tensor("triu", [128, 128], F32, kind="ExternalInput").ap()
    d1["x"] = nc.dram_tensor("x0", [2, 1024, NTOK], F32, kind="ExternalInput").ap()
    x1 = nc.dram_tensor("x1", [2, 1024, NTOK], F32, kind="Internal").ap()
    x2 = nc.dram_tensor("x2", [2, 1024, NTOK], F32, kind="Internal").ap()
    x3 = nc.dram_tensor("x3", [2, 1024, NTOK], F32, kind="Internal").ap()
    y = nc.dram_tensor("y", [2, 1024, NTOK], F32, kind="ExternalOutput").ap()
    d1["y"] = x1
    d2["x"], d2["y"] = x1, x2
    d3["x"], d3["y"] = x2, x3
    d4["x"], d4["y"] = x3, y

    es = ExitStack()
    B = Bld(nc, es)
    C = alloc_common(B, es, nc, d1, triu=True, scr_b=True)
    for name, body, dd in (("l0_attn", _body_l0_attn, d1), ("l0_ffn", _body_l0_ffn, d2),
                           ("l1_attn", _body_l1_attn, d3), ("l1_ffn", _body_l1_ffn, d4)):
        with ExitStack() as Se, nc.named_scope(name):
            body(nc, Se, B, C, dd)
    B.block()
    def fin(yq):
        B.wait_all_dma(yq)
    B.y(fin)
    B.end()
    es.close()
    return nc


def mega_stage_ins(I):
    a = l0_attn_inputs(None, I)
    f = l0_ffn_inputs(None, I)
    qa = l1_attn_inputs(None, I)
    ff = l1_ffn_inputs(None, I)
    return a, f, qa, ff


def mega_weight_map(stage_ins):
    a, f, qa, ff = stage_ins
    ins = {"ident": a["ident"], "triu": a["triu"]}
    for pre, srcd in (("a", a), ("f", f), ("qa", qa), ("ff", ff)):
        for k in _STAGE_KEYS[pre]:
            ins[pre + "_" + k] = srcd[k]
    return ins



def kernel(**inputs):
    I = inputs
    cores = list(range(N_CORES))
    x = I["L0_attn_x"]
    xT = np.ascontiguousarray(x.transpose(0, 2, 1))

    stage_ins = mega_stage_ins(I)
    nc = build_mega(stage_ins)
    wmap = mega_weight_map(stage_ins)
    in_maps = [{**wmap, "x0": xT[c * 2:(c + 1) * 2]} for c in cores]
    res = run_bass_kernel_spmd(nc, in_maps, core_ids=cores)
    x4T = np.concatenate([r["y"] for r in res.results], axis=0)
    return np.ascontiguousarray(x4T.transpose(0, 2, 1))
